# version 19
# baseline (speedup 1.0000x reference)
import os
import numpy as np
import ml_dtypes
from contextlib import ExitStack
import concourse.bass as bass
import concourse.mybir as mybir
from concourse.bass_utils import run_bass_kernel_spmd
from concourse.bass_core_rust import IndirectOffsetOnAxis
from concourse.aot_env import allow_ici_all_to_all

F32 = mybir.dt.float32
BF16 = mybir.dt.bfloat16
I32 = mybir.dt.int32
U32 = mybir.dt.uint32
AF = mybir.ActivationFunctionType
ALU = mybir.AluOpType

NCORES = 8
B = 2
L = 16384
D = 1024
DD = 2048
TPC = 4096
LPC = 2048
NCB = 8
E = 64
TK = 8
FF = 768
CAP = 512
SF = 768
EPC = 8
TCH = 512
NTCH = TPC // TCH
LCH = 1024
NLCH = L // LCH
NTILE = TPC // 128
LN_EPS = 1e-5
BIGIDX = 1.0e9
RG = [list(range(NCORES))]
INV_D = 1.0 / D
SWEEPS = ("P", "N", "N", "N", "N")


def build_nc(taps=(), nphases=6):
    nc = bass.Bass("TRN2", target_bir_lowering=False)
    dt_ = nc.dram_tensor

    xcm = dt_("xcm", [D, TPC], F32, kind="ExternalInput").ap()
    g8 = dt_("g8", [128, NCB], F32, kind="ExternalInput").ap()
    b8 = dt_("b8", [128, NCB], F32, kind="ExternalInput").ap()
    bo8 = dt_("bo8", [128, NCB], F32, kind="ExternalInput").ap()
    acol = dt_("acol", [128, 1], F32, kind="ExternalInput").ap()
    w_in = dt_("w_in", [D, DD], F32, kind="ExternalInput").ap()
    w_out = dt_("w_out", [D, D], F32, kind="ExternalInput").ap()
    rwt = dt_("rwt", [D, E], F32, kind="ExternalInput").ap()
    wsgu = dt_("wsgu", [D, 2 * SF], F32, kind="ExternalInput").ap()
    wsd = dt_("wsd", [SF, D], F32, kind="ExternalInput").ap()
    wg_all = dt_("wg_all", [EPC * D, FF], F32, kind="ExternalInput").ap()
    wu_all = dt_("wu_all", [EPC * D, FF], F32, kind="ExternalInput").ap()
    wd_all = dt_("wd_all", [EPC * FF, D], F32, kind="ExternalInput").ap()
    ones_bf = dt_("ones_bf", [128, 128], BF16, kind="ExternalInput").ap()
    ones_f = dt_("ones_f", [128, 128], F32, kind="ExternalInput").ap()
    lt_strict = dt_("lt_strict", [128, 128], F32, kind="ExternalInput").ap()
    ident_f = dt_("ident_f", [128, 128], F32, kind="ExternalInput").ap()
    iota64 = dt_("iota64", [128, E], F32, kind="ExternalInput").ap()
    detie = dt_("detie", [128, E], F32, kind="ExternalInput").ap()
    iota128 = dt_("iota128", [128, 1], F32, kind="ExternalInput").ap()
    rowbase = dt_("rowbase", [128, 1], F32, kind="ExternalInput").ap()
    slotb = dt_("slotb", [128, 1], F32, kind="ExternalInput").ap()
    presel = dt_("presel", [16, 2], F32, kind="ExternalInput").ap()
    ub8 = dt_("ub8", [128, NCB], F32, kind="ExternalInput").ap()
    epsc = dt_("epsc", [128, 1], F32, kind="ExternalInput").ap()
    gb8 = dt_("gb8", [128, NCB], F32, kind="ExternalInput").ap()

    ycm = dt_("ycm", [D, TPC], F32, kind="ExternalOutput").ap()

    x1_dram = dt_("x1_dram", [D, TPC], F32, kind="Internal")
    x2_dram = dt_("x2_dram", [D, TPC], F32, kind="Internal")
    x3cm_dram = dt_("x3cm_dram", [D, TPC], BF16, kind="Internal")
    shared_dram = dt_("shared_dram", [D, TPC], F32, kind="Internal")
    routed_dram = dt_("routed_dram", [TPC, D], F32, kind="Internal")
    routedcm_dram = dt_("routedcm_dram", [D, TPC], F32, kind="Internal")
    xb_dram = dt_("xb_dram", [EPC * CAP, D], BF16, kind="Internal")
    yb_dram = dt_("yb_dram", [2 * D, CAP], BF16, kind="Internal")
    ug_a2a_in = dt_("ug_a2a_in", [NCORES * 256, TPC], F32, kind="Internal")
    ug_a2a_out = dt_("ug_a2a_out", [NCORES * 256, TPC], F32, kind="Internal")
    hs_a2a_in = dt_("hs_a2a_in", [NCORES * 128, TPC], BF16, kind="Internal")
    hs_a2a_out = dt_("hs_a2a_out", [NCORES * 128, TPC], BF16, kind="Internal")
    x3row_dram = dt_("x3row_dram", [TPC, D], BF16, kind="Internal")
    x3rep = dt_("x3rep", [NCORES * TPC, D], BF16, kind="Internal", addr_space="Shared")
    slot_out = dt_("slot_out", [EPC * CAP, D], BF16, kind="Internal")
    slot_rep = dt_("slot_rep", [E * CAP, D], BF16, kind="Internal", addr_space="Shared")
    meta_in = dt_("meta_in", [E * CAP, 2], F32, kind="Internal")
    meta_out = dt_("meta_out", [E * CAP, 2], F32, kind="Internal")
    cnt_in = dt_("cnt_in", [2, E], F32, kind="Internal")
    cnt_out = dt_("cnt_out", [2 * NCORES, E], F32, kind="Internal", addr_space="Shared")

    tap_aps = {}

    def tap(name, shape, dtype):
        if name in taps:
            tap_aps[name] = dt_("tap_" + name, shape, dtype, kind="ExternalOutput").ap()
            return tap_aps[name]
        return None

    t_x1 = tap("x1", [D, TPC], F32)
    t_ug = tap("ug", [NCORES * 256, TPC], F32)
    t_hs = tap("hs", [NCORES * 128, TPC], BF16)
    t_x2 = tap("x2", [D, TPC], F32)
    t_x3 = tap("x3", [D, TPC], BF16)
    t_eid = tap("eid", [128, NTILE * TK], F32)
    t_wts = tap("wts", [128, NTILE * TK], F32)
    t_idx = tap("idx", [128, NTILE * TK], I32)
    t_meta = tap("meta", [E * CAP, 2], F32)
    t_routed = tap("routed", [TPC, D], F32)
    t_shared = tap("shared", [D, TPC], F32)

    with ExitStack() as OUTER:
        block = OUTER.enter_context(nc.Block())
        SEMQ = {}
        for nm in ("TE", "VE", "SE", "PO", "SY", "GD", "SC", "CC", "G0", "G1", "G2", "G3"):
            SEMQ[nm] = OUTER.enter_context(nc.semaphore("s" + nm))
        CNT = {nm: 0 for nm in SEMQ}
        WAITED = {e: {nm: 0 for nm in SEMQ} for e in
                  ("tensor", "vector", "scalar", "gpsimd", "sync")}
        QSEL = {"tensor": None, "vector": None, "scalar": None, "gpsimd": None,
                "sync": None}

        def _engsel(q):
            return {"tensor": block.tensor, "vector": block.vector,
                    "scalar": block.scalar, "gpsimd": block.gpsimd,
                    "sync": block.sync}[q]

        ENG_SEM = {"tensor": "TE", "vector": "VE", "scalar": "SE",
                   "gpsimd": "PO", "sync": "SY"}

        def step(queue, fn, deps=(), dma=0, sem=None, cc=False):
            mysem = sem or (("GD" if queue == "gpsimd" else
                             "SC" if queue == "scalar" else "SY")
                            if dma else ENG_SEM[queue])
            if cc:
                mysem = "CC"
            w = WAITED[queue]
            dep_list = [(s, v) for (s, v) in deps if v > w[s]]
            for s, v in dep_list:
                w[s] = max(w[s], v)
            if dma:
                CNT[mysem] += 16 * dma
            elif cc:
                CNT[mysem] += 1
            else:
                CNT[mysem] += 1
            after = CNT[mysem]
            handle = SEMQ[mysem]
            inc_each = 16 if dma else 1

            @_engsel(queue)
            def _(eng):
                for s, v in dep_list:
                    eng.wait_ge(SEMQ[s], v)
                if dma:
                    insts = fn(eng)
                    if not isinstance(insts, (list, tuple)):
                        insts = [insts]
                    assert len(insts) == dma, f"dma count {len(insts)} != {dma}"
                    for i_ in insts:
                        i_.then_inc(handle, 16)
                else:
                    last = fn(eng)
                    if last is None:
                        last = eng.nop()
                    last.then_inc(handle, 1)
            return (mysem, after)

        ALLQ = ("TE", "VE", "SE", "PO", "SY", "GD", "SC", "CC", "G0", "G1", "G2", "G3")

        def now():
            return [(s, CNT[s]) for s in ALLQ if CNT[s] > 0]

        def sstep(queue, fn, dma=0, cc=False, exclude=(), extra=()):
            deps = [t for t in now() if t[0] not in exclude] + list(extra)
            return step(queue, fn, deps=deps, dma=dma, cc=cc)

        def vseq(fns):
            for f in fns:
                sstep("vector", f)

        def sbuf(stk, name, shape, dtype):
            return stk.enter_context(nc.sbuf_tensor(name, shape, dtype))

        def psum(stk, name, shape, dtype=F32):
            return stk.enter_context(nc.psum_tensor(name, shape, dtype))

        c_onesbf = sbuf(OUTER, "c_onesbf", [128, 128], BF16)
        c_onesf = sbuf(OUTER, "c_onesf", [128, 128], F32)
        c_lt = sbuf(OUTER, "c_lt", [128, 128], F32)
        c_idf = sbuf(OUTER, "c_idf", [128, 128], F32)
        c_iota64 = sbuf(OUTER, "c_iota64", [128, E], F32)
        c_detie = sbuf(OUTER, "c_detie", [128, E], F32)
        c_iota128 = sbuf(OUTER, "c_iota128", [128, 1], F32)
        c_rowbase = sbuf(OUTER, "c_rowbase", [128, 1], F32)
        c_slotb = sbuf(OUTER, "c_slotb", [128, 1], F32)
        c_g8 = sbuf(OUTER, "c_g8", [128, NCB], F32)
        c_b8 = sbuf(OUTER, "c_b8", [128, NCB], F32)
        c_bo8 = sbuf(OUTER, "c_bo8", [128, NCB], F32)
        c_acol = sbuf(OUTER, "c_acol", [128, 1], F32)
        c_presel = sbuf(OUTER, "c_presel", [16, 2], F32)
        c_ub8 = sbuf(OUTER, "c_ub8", [128, NCB], F32)
        c_eps = sbuf(OUTER, "c_eps", [128, 1], F32)
        c_gb8 = sbuf(OUTER, "c_gb8", [128, NCB], F32)
        p_idxi = sbuf(OUTER, "p_idxi", [128, NTILE * TK], I32)
        p_wts = sbuf(OUTER, "p_wts", [128, NTILE * TK], F32)
        p_zero = sbuf(OUTER, "p_zero", [128, 512], F32)

        def _loads(sy):
            return [sy.dma_start(out=dst[:], in_=src)
                    for dst, src in [(c_onesbf, ones_bf), (c_onesf, ones_f),
                                     (c_lt, lt_strict), (c_idf, ident_f),
                                     (c_iota64, iota64), (c_detie, detie),
                                     (c_iota128, iota128), (c_rowbase, rowbase),
                                     (c_slotb, slotb), (c_g8, g8), (c_b8, b8),
                                     (c_bo8, bo8), (c_acol, acol), (c_presel, presel),
                                     (c_ub8, ub8), (c_gb8, gb8), (c_eps, epsc)]]
        sstep("sync", _loads, dma=17)
        breg = []

        def _breg(g):
            r = g.alloc_register(name="bnd")
            g.reg_mov(r, E * CAP - 1)
            breg.append(r)
            return g.nop()
        sstep("gpsimd", _breg)
        sstep("vector", lambda v: v.memset(p_zero[:], 0.0))
        sstep("sync", lambda sy: sy.dma_start(
            out=meta_in.ap().rearrange("(p r) c -> p (r c)", p=128), in_=p_zero[:, 0:512]),
            dma=1)



        def emit_ln_tok(P, xap, outap, dep_in, out_war=()):
            bf, sq = P["bf"], P["sq"]
            q_ = P["rot"][0]
            P["rot"][0] = 1 - q_
            mu, m2 = P["mu"][q_], P["m2"]
            var, rs, scr = P["var"], P["rs"][q_], P["scr"]
            ps1, ps2 = P["ps1"], P["ps2"]
            t_cast = step("scalar", lambda se: se.activation(out=bf[:], in_=xap, func=AF.Copy),
                          deps=list(dep_in) + P.get("war_bf", []))
            t_sq = step("vector", lambda ve: ve.tensor_tensor(out=sq[:], in0=bf[:],
                                                              in1=bf[:], op=ALU.mult),
                        deps=[t_cast])

            def _mm(te):
                te.ldweights(c_onesbf[:])
                for cb in range(NCB):
                    te.matmul(ps1[:], c_onesbf[:], bf[:, cb * TCH:(cb + 1) * TCH],
                              start=(cb == 0), stop=(cb == NCB - 1))
                for cb in range(NCB):
                    last = te.matmul(ps2[:], c_onesbf[:], sq[:, cb * TCH:(cb + 1) * TCH],
                                     start=(cb == 0), stop=(cb == NCB - 1))
                return last
            t_mm = step("tensor", _mm, deps=[t_sq] + P.get("war_ps", []))

            def _m1(ve):
                ve.tensor_scalar(out=mu[:], in0=ps1[:], scalar1=INV_D, scalar2=None,
                                 op0=ALU.mult)
                ve.tensor_scalar(out=m2[:], in0=ps2[:], scalar1=INV_D, scalar2=None,
                                 op0=ALU.mult)
                ve.tensor_tensor(out=scr[:], in0=mu[:], in1=mu[:], op=ALU.mult)
                return ve.tensor_tensor(out=var[:], in0=m2[:], in1=scr[:], op=ALU.subtract)
            t_m1 = step("vector", _m1, deps=[t_mm] + P["warq"].get(q_, []) +
                        P.get("war_v", []))
            t_sqrt = step("scalar", lambda se: se.activation(out=scr[:], in_=var[:],
                                                             func=AF.Sqrt, bias=c_eps[:]),
                          deps=[t_m1])
            t_rs = step("vector", lambda ve: ve.reciprocal(out=rs[:], in_=scr[:]),
                        deps=[t_sqrt])

            def _apv(ve):
                for cb in range(5):
                    xi = xap[:, cb * TCH:(cb + 1) * TCH]
                    oi = outap[:, cb * TCH:(cb + 1) * TCH]
                    ve.tensor_tensor(out=oi, in0=xi, in1=mu[:], op=ALU.subtract)
                    last = ve.tensor_tensor(out=oi, in0=oi, in1=rs[:], op=ALU.mult)
                return last

            def _app(po):
                for cb in range(5, NCB):
                    xi = xap[:, cb * TCH:(cb + 1) * TCH]
                    oi = outap[:, cb * TCH:(cb + 1) * TCH]
                    po.tensor_tensor(out=oi, in0=xi, in1=mu[:], op=ALU.subtract)
                    last = po.tensor_tensor(out=oi, in0=oi, in1=rs[:], op=ALU.mult)
                return last
            t_apv = step("vector", _apv, deps=[t_rs] + list(out_war))
            t_app = step("gpsimd", _app, deps=[t_rs] + list(out_war))
            P["war_bf"] = [t_mm]
            P["war_ps"] = [t_m1]
            P["war_v"] = [t_rs]
            P["warq"][q_] = [t_apv, t_app]
            return [t_apv, t_app]

        def emit_ln(P, xap, outap):
            bf, sq = P["bf"], P["sq"]
            mu, m2, var, rs, scr = P["mu"], P["m2"], P["var"], P["rs"], P["scr"]
            ps1, ps2 = P["ps1"], P["ps2"]

            def _cast(se):
                se.activation(out=bf[:], in_=xap, func=AF.Copy)
                return se.activation(out=sq[:], in_=bf[:], func=AF.Square)
            sstep("scalar", _cast)

            def _mm(te):
                te.ldweights(c_onesbf[:])
                for cb in range(NCB):
                    te.matmul(ps1[:], c_onesbf[:], bf[:, cb * TCH:(cb + 1) * TCH],
                              start=(cb == 0), stop=(cb == NCB - 1))
                for cb in range(NCB):
                    last = te.matmul(ps2[:], c_onesbf[:], sq[:, cb * TCH:(cb + 1) * TCH],
                                     start=(cb == 0), stop=(cb == NCB - 1))
                return last
            sstep("tensor", _mm)

            def _m1(ve):
                ve.tensor_scalar(out=mu[:], in0=ps1[:], scalar1=INV_D, scalar2=None, op0=ALU.mult)
                ve.tensor_scalar(out=m2[:], in0=ps2[:], scalar1=INV_D, scalar2=None, op0=ALU.mult)
                ve.tensor_tensor(out=scr[:], in0=mu[:], in1=mu[:], op=ALU.mult)
                return ve.tensor_tensor(out=var[:], in0=m2[:], in1=scr[:], op=ALU.subtract)
            sstep("vector", _m1)
            sstep("scalar", lambda se: se.activation(out=scr[:], in_=var[:], func=AF.Sqrt,
                                                        bias=c_eps[:]))

            def _m2(ve):
                last = ve.reciprocal(out=rs[:], in_=scr[:])
                for cb in range(NCB):
                    xi = xap[:, cb * TCH:(cb + 1) * TCH]
                    oi = outap[:, cb * TCH:(cb + 1) * TCH]
                    ve.tensor_tensor(out=oi, in0=xi, in1=mu[:], op=ALU.subtract)
                    last = ve.tensor_tensor(out=oi, in0=oi, in1=rs[:], op=ALU.mult)
                return last
            sstep("vector", _m2)

        def ln_tiles(stk, pfx):
            return dict(
                bf=sbuf(stk, pfx + "_bf", [128, NCB * TCH], BF16),
                sq=sbuf(stk, pfx + "_sq", [128, NCB * TCH], BF16),
                mu=[sbuf(stk, pfx + "_mu%d" % q_, [128, TCH], F32) for q_ in range(2)],
                m2=sbuf(stk, pfx + "_m2", [128, TCH], F32),
                var=sbuf(stk, pfx + "_var", [128, TCH], F32),
                rs=[sbuf(stk, pfx + "_rs%d" % q_, [128, TCH], F32) for q_ in range(2)],
                scr=sbuf(stk, pfx + "_scr", [128, TCH], F32),
                ps1=psum(stk, pfx + "_ps1", [128, TCH]),
                ps2=psum(stk, pfx + "_ps2", [128, TCH]),
                rot=[0],
                warq={},
            )

        xcm_v = xcm.rearrange("(cb p) n -> p cb n", p=128)
        x1_v = x1_dram.ap().rearrange("(cb p) n -> p cb n", p=128)
        x2_v = x2_dram.ap().rearrange("(cb p) n -> p cb n", p=128)
        x3_v = x3cm_dram.ap().rearrange("(cb p) n -> p cb n", p=128)

        with ExitStack() as S:
          if nphases >= 1:
            x0c = [sbuf(S, "p1_x0%d" % p_, [128, NCB * TCH], F32) for p_ in range(2)]
            x1c = [sbuf(S, "p1_x1%d" % p_, [128, NCB * TCH], F32) for p_ in range(2)]
            znc = sbuf(S, "p1_zn", [128, NCB * TCH], F32)
            P = ln_tiles(S, "p1")
            zhalf = sbuf(S, "p1_z", [128, NCB * 4 * TCH], BF16)
            winb = sbuf(S, "p1_win", [128, NCB * DD], BF16)
            evb = [sbuf(S, "p1_ev%d" % p_, [128, 4 * TCH], F32) for p_ in range(2)]
            mmps = psum(S, "p1_mm", [128, 4 * TCH])

            t_win = step("gpsimd", lambda g: g.dma_start(
                out=winb[:].rearrange("p (k m) -> p k m", k=NCB),
                in_=w_in.rearrange("(k p) m -> p k m", p=128)), deps=now(), dma=1)

            ln1_done = {}
            ln2_done = {}
            aff_tok = {}
            ld_tok = {}
            ev_tok = {}
            st_tok = {}
            wm_last = {}
            for i in range(NTCH):
                par = i % 2
                d = list(now()) if i == 0 else []
                if i == 1:
                    d = [ld_tok[0]]
                if i >= 2:
                    d = ln1_done[i - 2] + ln2_done[i - 2]
                ld_tok[i] = step("sync", lambda sy, i=i, par=par: sy.dma_start(
                    out=x0c[par][:].rearrange("p (cb n) -> p cb n", cb=NCB),
                    in_=xcm_v[:, :, i * TCH:(i + 1) * TCH]), deps=d, dma=1)
                ow = ln2_done[i - 2] if i >= 2 else []
                ln1_done[i] = emit_ln_tok(P, x0c[par][:], x1c[par][:],
                                          [ld_tok[i]], out_war=ow)
                st_x1 = step("sync", lambda sy, i=i, par=par: sy.dma_start(
                    out=x1_v[:, :, i * TCH:(i + 1) * TCH],
                    in_=x1c[par][:].rearrange("p (cb n) -> p cb n", cb=NCB)),
                    deps=ln1_done[i], dma=1)
                q_ = 1 - P["rot"][0]
                varT, rsT = P["var"], P["rs"][q_]
                v2 = P["mu"][q_]
                t_v1 = step("vector", lambda ve: ve.tensor_tensor(
                    out=v2[:], in0=varT[:], in1=rsT[:], op=ALU.mult), deps=ln1_done[i])
                t_v2 = step("vector", lambda ve: ve.tensor_tensor(
                    out=v2[:], in0=v2[:], in1=rsT[:], op=ALU.mult), deps=[t_v1])
                t_sq2 = step("scalar", lambda se: se.activation(
                    out=v2[:], in_=v2[:], func=AF.Sqrt, bias=c_eps[:]), deps=[t_v2])
                t_rs2 = step("vector", lambda ve: ve.reciprocal(out=v2[:], in_=v2[:]),
                             deps=[t_sq2])

                def _zn(ve, par=par):
                    for cb in range(NCB):
                        last = ve.tensor_tensor(out=znc[:, cb * TCH:(cb + 1) * TCH],
                                                in0=x1c[par][:, cb * TCH:(cb + 1) * TCH],
                                                in1=v2[:], op=ALU.mult)
                    return last
                ow = [aff_tok[i - 1]] if i >= 1 else []
                t_znc = step("vector", _zn, deps=[t_rs2] + ow)
                ln2_done[i] = [t_znc]
                P["warq"][q_] = [t_znc]

                def _aff(se, i=i):
                    for cb in range(NCB):
                        last = se.activation(
                            out=zhalf[:, cb * 4 * TCH + (i % 4) * TCH: cb * 4 * TCH + (i % 4 + 1) * TCH],
                            in_=znc[:, cb * TCH:(cb + 1) * TCH], func=AF.Identity,
                            scale=c_g8[:, cb:cb + 1], bias=c_b8[:, cb:cb + 1])
                    return last
                d = list(ln2_done[i])
                if i >= 4:
                    d.append(wm_last[i // 4 - 1])
                aff_tok[i] = step("scalar", _aff, deps=d)

                if i % 4 == 3:
                    half = i // 4
                    zdeps = [aff_tok[j] for j in range(half * 4, half * 4 + 4)]
                    for m in range(16):
                        def _wmm(te, m=m):
                            for k in range(NCB):
                                wt = winb[:, k * DD + m * 128: k * DD + (m + 1) * 128]
                                te.ldweights(wt)
                                for c in range(4):
                                    last = te.matmul(
                                        mmps[:, c * TCH:(c + 1) * TCH], wt,
                                        zhalf[:, k * 4 * TCH + c * TCH: k * 4 * TCH + (c + 1) * TCH],
                                        start=(k == 0), stop=(k == NCB - 1))
                            return last
                        d = zdeps + [t_win]
                        if m >= 1:
                            d.append(ev_tok[(half, m - 1)])
                        elif half == 1:
                            d.append(ev_tok[(0, 15)])
                        t_wm = step("tensor", _wmm, deps=d)
                        fn = AF.Identity if m < 8 else AF.Sigmoid
                        bcol = c_ub8[:, m:m + 1] if m < 8 else c_gb8[:, m - 8:m - 7]

                        def _ev(se, fn=fn, bcol=bcol, par=m % 2):
                            for c in range(4):
                                last = se.activation(out=evb[par][:, c * TCH:(c + 1) * TCH],
                                                     in_=mmps[:, c * TCH:(c + 1) * TCH],
                                                     func=fn, bias=bcol)
                            return last
                        d = [t_wm]
                        if m >= 2:
                            d.append(st_tok[(half, m - 2)])
                        ev_tok[(half, m)] = step("scalar", _ev, deps=d)
                        r0 = 256 * m if m < 8 else 256 * (m - 8) + 128
                        st_tok[(half, m)] = step(
                            "sync", lambda sy, r0=r0, half=half, par=m % 2: sy.dma_start(
                                out=ug_a2a_in.ap()[r0:r0 + 128, half * 2048:(half + 1) * 2048],
                                in_=evb[par][:]), deps=[ev_tok[(half, m)]], dma=1)
                    wm_last[half] = t_wm

            def _cc_ug(g):
                with allow_ici_all_to_all():
                    return g.collective_compute(
                        "AllToAll", ALU.bypass, replica_groups=RG,
                        ins=[ug_a2a_in.ap().opt()], outs=[ug_a2a_out.ap().opt()])
            sstep("gpsimd", _cc_ug, cc=True)

        with ExitStack() as S:
          if nphases >= 2:
            u = [sbuf(S, "p2_u%d" % p_, [128, 2 * LCH], F32) for p_ in range(2)]
            g_ = [sbuf(S, "p2_g%d" % p_, [128, 2 * LCH], F32) for p_ in range(2)]
            gp = [sbuf(S, "p2_gp%d" % p_, [128, 2 * LCH], F32) for p_ in range(2)]
            d1 = [sbuf(S, "p2_d1%d" % p_, [128, 2 * LCH], F32) for p_ in range(2)]
            G1 = [sbuf(S, "p2_G1%d" % p_, [128, 2 * LCH], F32) for p_ in range(2)]
            h = sbuf(S, "p2_h", [128, 2 * LCH], F32)
            m_ = sbuf(S, "p2_m", [128, 2 * LCH], F32)
            w_ = sbuf(S, "p2_w", [128, 2 * LCH], F32)
            s_ = sbuf(S, "p2_s", [128, 2 * LCH], F32)
            qq = sbuf(S, "p2_q", [128, 2 * LCH], F32)
            aa = sbuf(S, "p2_a", [128, 2 * LCH], F32)
            rr = sbuf(S, "p2_r", [128, 2 * LCH], F32)
            c1 = sbuf(S, "p2_c1", [128, 2 * LCH], F32)
            c2 = sbuf(S, "p2_c2", [128, 2 * LCH], F32)
            cc = sbuf(S, "p2_cc", [128, 2 * LCH], F32)
            car = [sbuf(S, "p2_car0", [128, 1], F32), sbuf(S, "p2_car1", [128, 1], F32)]

            p2_entry = now()
            t_car = step("vector", lambda ve: (ve.memset(car[0][:], 0.0),
                                               ve.memset(car[1][:], 0.0))[1],
                         deps=p2_entry)
            ld_tok = {}
            free_tok = {}

            def _mk_ld(ci, par):
                j = ci // 2
                lo = (ci % 2) * LCH

                def _ld(sy):
                    out = []
                    for b_i in range(B):
                        out.append(sy.dma_start(
                            out=u[par][:, b_i * LCH:(b_i + 1) * LCH],
                            in_=ug_a2a_out.ap()[256 * j:256 * j + 128,
                                                b_i * 2048 + lo: b_i * 2048 + lo + LCH]))
                        out.append(sy.dma_start(
                            out=g_[par][:, b_i * LCH:(b_i + 1) * LCH],
                            in_=ug_a2a_out.ap()[256 * j + 128:256 * j + 256,
                                                b_i * 2048 + lo: b_i * 2048 + lo + LCH]))
                    return out
                return _ld

            ld_tok[0] = step("sync", _mk_ld(0, 0), deps=p2_entry, dma=4)
            ld_tok[1] = step("sync", _mk_ld(1, 1), deps=p2_entry, dma=4)

            pre_tok = {}
            t_scan = t_car
            t_m_last = None
            for ci in range(NLCH):
                par = ci % 2
                def _pre1(ve, par=par):
                    return ve.tensor_scalar(out=gp[par][:], in0=g_[par][:], scalar1=-1.0,
                                            scalar2=1.0, op0=ALU.mult, op1=ALU.add)
                d = [ld_tok[ci]]
                if ci >= 2:
                    d.append(free_tok[ci - 2])
                t_p1 = step("vector", _pre1, deps=d)
                t_p2 = step("vector", lambda ve, par=par: ve.tensor_scalar(
                    out=d1[par][:], in0=gp[par][:], scalar1=c_acol[:], scalar2=None,
                    op0=ALU.mult), deps=[t_p1])
                t_p3 = step("gpsimd", lambda po, par=par: po.tensor_tensor(
                    out=G1[par][:], in0=g_[par][:], in1=d1[par][:], op=ALU.add),
                    deps=[t_p2])
                pre_tok[ci] = t_p3

                t_ms = step("vector", lambda ve: ve.memset(h[:], 0.0), deps=[t_scan])

                for si, kind in enumerate(("P", "N", "N")):
                    def _m(ve, par=par, first=(si == 0)):
                        for b_i in range(B):
                            c0 = b_i * LCH
                            ve.scalar_tensor_tensor(out=m_[:, c0:c0 + 1], in0=car[b_i][:],
                                                    scalar=c_acol[:],
                                                    in1=u[par][:, c0:c0 + 1],
                                                    op0=ALU.mult, op1=ALU.add)
                            last = ve.scalar_tensor_tensor(
                                out=m_[:, c0 + 1:c0 + LCH], in0=h[:, c0:c0 + LCH - 1],
                                scalar=c_acol[:], in1=u[par][:, c0 + 1:c0 + LCH],
                                op0=ALU.mult, op1=ALU.add)
                        return last
                    d = [ld_tok[ci], t_ms if si == 0 else t_scan]
                    t_m = step("vector", _m, deps=d)
                    t_m_last = t_m
                    t_w = step("scalar", lambda se: se.activation(
                        out=w_[:], in_=m_[:], func=AF.Tanh), deps=[t_m])
                    if kind == "P":
                        t_c1 = step("gpsimd", lambda po, par=par: po.tensor_tensor(
                            out=cc[:], in0=gp[par][:], in1=w_[:], op=ALU.mult),
                            deps=[t_w, pre_tok[ci]])

                        def _scanp(ve, par=par):
                            for b_i in range(B):
                                c0 = b_i * LCH
                                last = ve.tensor_tensor_scan(
                                    out=h[:, c0:c0 + LCH], data0=g_[par][:, c0:c0 + LCH],
                                    data1=cc[:, c0:c0 + LCH], initial=car[b_i][:],
                                    op0=ALU.mult, op1=ALU.add)
                            return last
                        t_scan = step("vector", _scanp, deps=[t_c1])
                    else:
                        t_s = step("vector", lambda ve: ve.tensor_tensor(
                            out=s_[:], in0=w_[:], in1=w_[:], op=ALU.mult), deps=[t_w])
                        t_q = step("gpsimd", lambda po, par=par: po.tensor_tensor(
                            out=qq[:], in0=d1[par][:], in1=s_[:], op=ALU.mult),
                            deps=[t_s, pre_tok[ci]])
                        t_r = step("vector", lambda ve, par=par: ve.tensor_tensor(
                            out=rr[:], in0=d1[par][:], in1=qq[:], op=ALU.subtract),
                            deps=[t_q])
                        t_c1 = step("gpsimd", lambda po, par=par: po.tensor_tensor(
                            out=c1[:], in0=gp[par][:], in1=w_[:], op=ALU.mult),
                            deps=[t_w])

                        def _c2(ve):
                            for b_i in range(B):
                                c0 = b_i * LCH
                                ve.tensor_tensor(out=c2[:, c0:c0 + 1], in0=rr[:, c0:c0 + 1],
                                                 in1=car[b_i][:], op=ALU.mult)
                                last = ve.tensor_tensor(out=c2[:, c0 + 1:c0 + LCH],
                                                        in0=rr[:, c0 + 1:c0 + LCH],
                                                        in1=h[:, c0:c0 + LCH - 1],
                                                        op=ALU.mult)
                            return last
                        t_c2 = step("vector", _c2, deps=[t_r])
                        t_a = step("gpsimd", lambda po, par=par: po.tensor_tensor(
                            out=aa[:], in0=G1[par][:], in1=qq[:], op=ALU.subtract),
                            deps=[t_q])
                        t_c = step("vector", lambda ve: ve.tensor_tensor(
                            out=cc[:], in0=c1[:], in1=c2[:], op=ALU.subtract),
                            deps=[t_c2, t_c1])

                        def _scann(ve):
                            for b_i in range(B):
                                c0 = b_i * LCH
                                last = ve.tensor_tensor_scan(
                                    out=h[:, c0:c0 + LCH], data0=aa[:, c0:c0 + LCH],
                                    data1=cc[:, c0:c0 + LCH], initial=car[b_i][:],
                                    op0=ALU.mult, op1=ALU.add)
                            return last
                        t_scan = step("vector", _scann, deps=[t_a, t_c])

                free_tok[ci] = t_m_last
                free_tok[ci] = t_scan

                def _carry(ve):
                    ve.tensor_copy(car[0][:], h[:, LCH - 1:LCH])
                    return ve.tensor_copy(car[1][:], h[:, 2 * LCH - 1:2 * LCH])
                t_scan = step("vector", _carry, deps=[t_scan])

                def _hsout(g, ci=ci):
                    j = ci // 2
                    lo = (ci % 2) * LCH
                    return [g.dma_start(
                        out=hs_a2a_in.ap()[128 * j:128 * (j + 1),
                                           b_i * 2048 + lo: b_i * 2048 + lo + LCH],
                        in_=h[:, b_i * LCH:(b_i + 1) * LCH]) for b_i in range(B)]
                t_hsdma = step("gpsimd", _hsout, deps=[t_scan], dma=2)
                t_scan = t_scan
                hs_free = t_hsdma
                if ci + 2 < NLCH:
                    ld_tok[ci + 2] = step("sync", _mk_ld(ci + 2, par),
                                          deps=[free_tok[ci]], dma=4)

                t_scan = (t_scan[0], t_scan[1])
                last_hs = t_hsdma
                t_scan_chain = t_scan

                def _noop():
                    pass
                
                if ci < NLCH - 1:
                    t_scan = ("VE", t_scan_chain[1])
                    t_scan = step("vector", lambda ve: ve.nop(), deps=[t_scan_chain, last_hs])

            def _cc_hs(g):
                with allow_ici_all_to_all():
                    return g.collective_compute(
                        "AllToAll", ALU.bypass, replica_groups=RG,
                        ins=[hs_a2a_in.ap().opt()], outs=[hs_a2a_out.ap().opt()])
            sstep("gpsimd", _cc_hs, cc=True)

        with ExitStack() as S:
          if nphases >= 3:
            x3bf = sbuf(S, "p3_x3bf", [128, NCB * TPC], BF16)
            trt_s = sbuf(S, "p3_tr0", [128, D], BF16)
            trt = [trt_s, trt_s]
            SC = S.enter_context(ExitStack())
            woutb = sbuf(SC, "p3_wout", [128, NCB * D], BF16)
            hsc_s = sbuf(SC, "p3_hs", [128, NCB * TCH], BF16)
            hsc = [hsc_s, hsc_s]
            x0c = [sbuf(SC, "p3_x0%d" % p_, [128, NCB * TCH], F32) for p_ in range(2)]
            x1c_s = sbuf(SC, "p3_x1", [128, NCB * TCH], F32)
            x1c = [x1c_s, x1c_s]
            arc_s = sbuf(SC, "p3_ar", [128, NCB * TCH], F32)
            arc = [arc_s, arc_s]
            r3c_s = sbuf(SC, "p3_r3", [128, NCB * TCH], F32)
            r3c = [r3c_s, r3c_s]
            P = ln_tiles(SC, "p3")
            mmps = psum(SC, "p3_mm", [128, 4 * TCH])

            trt_free = [[], []]
            x3row_toks = []
            t_wout = step("gpsimd", lambda g: g.dma_start(
                out=woutb[:].rearrange("p (k m) -> p k m", k=NCB),
                in_=w_out.rearrange("(k p) m -> p k m", p=128)), deps=now(), dma=1)

            hs_v = hs_a2a_out.ap().rearrange("(k p) n -> p k n", p=128)
            hs_tok = {}
            x1_tok = {}
            x0_tok = {}
            wo_ev = {}
            wo_last = {}
            a1_tok = {}
            ln3_done = {}
            ln4_done = {}
            x2st_tok = {}
            x3c_tok = {}
            tok_cc_hs = [t for t in now() if t[0] == "CC"]
            for i in range(NTCH):
                par = i % 2
                d = list(tok_cc_hs) if i < 1 else [wo_last[i - 1]]
                hs_tok[i] = step("sync", lambda sy, i=i, par=par: sy.dma_start(
                    out=hsc[par][:].rearrange("p (k n) -> p k n", k=NCB),
                    in_=hs_v[:, :, i * TCH:(i + 1) * TCH]), deps=d, dma=1)
                d = [] if i < 1 else list(a1_tok[i - 1])
                x1_tok[i] = step("sync", lambda sy, i=i, par=par: sy.dma_start(
                    out=x1c[par][:].rearrange("p (cb n) -> p cb n", cb=NCB),
                    in_=x1_v[:, :, i * TCH:(i + 1) * TCH]), deps=d, dma=1)
                d = [] if i < 2 else [x2st_tok[i - 2]] + ln4_done[i - 2]
                x0_tok[i] = step("sync", lambda sy, i=i, par=par: sy.dma_start(
                    out=x0c[par][:].rearrange("p (cb n) -> p cb n", cb=NCB),
                    in_=xcm_v[:, :, i * TCH:(i + 1) * TCH]), deps=d, dma=1)
                for grp in range(2):
                    def _wo(te, grp=grp, par=par):
                        for mi in range(4):
                            m = grp * 4 + mi
                            for k in range(NCB):
                                wt = woutb[:, k * D + m * 128: k * D + (m + 1) * 128]
                                te.ldweights(wt)
                                last = te.matmul(
                                    mmps[:, mi * TCH:(mi + 1) * TCH], wt,
                                    hsc[par][:, k * TCH:(k + 1) * TCH],
                                    start=(k == 0), stop=(k == NCB - 1))
                        return last
                    d = [hs_tok[i], t_wout]
                    if grp == 1:
                        d.append(wo_ev[(i, 0)])
                    elif i >= 1:
                        d.append(wo_ev[(i - 1, 1)])
                    t_wo = step("tensor", _wo, deps=d)
                    wo_last[i] = t_wo

                    def _ev(se, grp=grp, par=par):
                        for mi in range(4):
                            m = grp * 4 + mi
                            last = se.activation(out=arc[par][:, m * TCH:(m + 1) * TCH],
                                                 in_=mmps[:, mi * TCH:(mi + 1) * TCH],
                                                 func=AF.Identity, bias=c_bo8[:, m:m + 1])
                        return last
                    d = [t_wo]
                    if i >= 1:
                        d += ln3_done[i - 1]
                    wo_ev[(i, grp)] = step("scalar", _ev, deps=d)

                def _addx1(ve, par=par):
                    for cb in range(NCB):
                        last = ve.tensor_tensor(out=arc[par][:, cb * TCH:(cb + 1) * TCH],
                                                in0=arc[par][:, cb * TCH:(cb + 1) * TCH],
                                                in1=x1c[par][:, cb * TCH:(cb + 1) * TCH],
                                                op=ALU.add)
                    return last
                a1 = step("vector", _addx1, deps=[wo_ev[(i, 0)], wo_ev[(i, 1)], x1_tok[i]])
                a1_tok[i] = [a1]
                ow = [] if i < 1 else [x3c_tok[i - 1]]
                ln3_done[i] = emit_ln_tok(P, arc[par][:], r3c[par][:],
                                          [a1], out_war=ow)

                def _addx0(ve, par=par):
                    for cb in range(NCB):
                        last = ve.tensor_tensor(out=x0c[par][:, cb * TCH:(cb + 1) * TCH],
                                                in0=x0c[par][:, cb * TCH:(cb + 1) * TCH],
                                                in1=r3c[par][:, cb * TCH:(cb + 1) * TCH],
                                                op=ALU.add)
                    return last
                a0 = step("vector", _addx0, deps=ln3_done[i] + [x0_tok[i]])
                x2st_tok[i] = step("sync", lambda sy, i=i, par=par: sy.dma_start(
                    out=x2_v[:, :, i * TCH:(i + 1) * TCH],
                    in_=x0c[par][:].rearrange("p (cb n) -> p cb n", cb=NCB)),
                    deps=[a0], dma=1)
                ln4_done[i] = emit_ln_tok(P, x0c[par][:], r3c[par][:],
                                          [a0], out_war=ln3_done[i])

                def _x3cast(se, i=i, par=par):
                    for cb in range(NCB):
                        last = se.activation(
                            out=x3bf[:, cb * TPC + i * TCH: cb * TPC + (i + 1) * TCH],
                            in_=r3c[par][:, cb * TCH:(cb + 1) * TCH], func=AF.Copy)
                    return last
                x3c_tok[i] = step("scalar", _x3cast, deps=ln4_done[i])

                def _x3st(sy, i=i):
                    return sy.dma_start(
                        out=x3_v[:, :, i * TCH:(i + 1) * TCH],
                        in_=x3bf[:].rearrange("p (cb n) -> p cb n", cb=NCB)[:, :, i * TCH:(i + 1) * TCH])
                t_x3st = step("sync", _x3st, deps=[x3c_tok[i]], dma=1)
                for blk in range(i * 4, (i + 1) * 4):
                    tr_t = step("sync", lambda sy, blk=blk: sy.dma_start(
                        out=trt[0][:], in_=x3cm_dram.ap()[:, blk * 128:(blk + 1) * 128],
                        transpose=True), deps=[t_x3st] + trt_free[0], dma=1)
                    st_t = step("sync", lambda sy, blk=blk: sy.dma_start(
                        out=x3row_dram.ap()[blk * 128:(blk + 1) * 128, :], in_=trt[0][:]),
                        deps=[tr_t], dma=1)
                    trt_free[0] = [st_t]
                    x3row_toks.append(st_t)

            SC.close()
            rwtb = sbuf(S, "p3_rwt", [128, NCB * E], BF16)
            lgt = sbuf(S, "p3_lg", [128, E], F32)
            mx = sbuf(S, "p3_mx", [128, TK], F32)
            mi_ = sbuf(S, "p3_mi", [128, TK], U32)
            nmx = sbuf(S, "p3_nmx", [128, 1], F32)
            exw = sbuf(S, "p3_ex", [128, TK], F32)
            s8 = sbuf(S, "p3_s8", [128, 1], F32)
            wv = sbuf(S, "p3_wv", [128, 1], F32)
            eid = sbuf(S, "p3_eid", [128, NTILE * TK], F32)
            mkt = sbuf(S, "p3_mk", [128, E], F32)
            tot = sbuf(S, "p3_tot", [128, NTILE * E], F32)
            scr64 = sbuf(S, "p3_scr64", [128, E], F32)
            SA = S.enter_context(ExitStack())
            ps_lg = psum(SA, "ps_lg", [128, E])
            ps_cnt = [psum(SA, "ps_cnt0", [128, E]), psum(SA, "ps_cnt1", [128, E])]
            cnt01 = [sbuf(S, "p3_cnt0", [1, E], F32), sbuf(S, "p3_cnt1", [1, E], F32)]

            t_rwt = step("gpsimd", lambda g: g.dma_start(
                out=rwtb[:].rearrange("p (k m) -> p k m", k=NCB),
                in_=rwt.rearrange("(k p) m -> p k m", p=128)), deps=[], dma=1)

            vprev = None
            lgt_free = None
            cnt_prev = {}
            for t32 in range(NTILE):
                seg = t32 // 16

                def _lmm(te, t32=t32):
                    for k in range(NCB):
                        te.ldweights(x3bf[:, k * TPC + t32 * 128: k * TPC + (t32 + 1) * 128])
                        last = te.matmul(ps_lg[:],
                                         x3bf[:, k * TPC + t32 * 128: k * TPC + (t32 + 1) * 128],
                                         rwtb[:, k * E:(k + 1) * E],
                                         start=(k == 0), stop=(k == NCB - 1))
                    return last
                d = [x3c_tok[t32 // 4], t_rwt]
                if lgt_free is not None:
                    d.append(lgt_free)
                t_lmm = step("tensor", _lmm, deps=d)
                d = [t_lmm]
                if vprev is not None:
                    d.append(vprev)
                t_lgt = step("vector", lambda ve: ve.tensor_tensor(
                    out=lgt[:], in0=ps_lg[:], in1=c_detie[:], op=ALU.add), deps=d)
                lgt_free = t_lgt

                def _mwi(ve):
                    ve.max_with_indices(out_max=mx[:], out_indices=mi_[:], in_=lgt[:])
                    return ve.nop()
                t_mwi = step("vector", _mwi, deps=[t_lgt])
                t_nmx = step("vector", lambda ve: ve.tensor_scalar(
                    out=nmx[:], in0=mx[:, 0:1], scalar1=-1.0, scalar2=None, op0=ALU.mult),
                    deps=[t_mwi])
                t_exp = step("scalar", lambda se: se.activation(
                    out=exw[:], in_=mx[:], func=AF.Exp, bias=nmx[:]), deps=[t_nmx])
                t_rsm = step("vector", lambda ve: ve.reduce_sum(
                    out=s8[:], in_=exw[:], axis=mybir.AxisListType.X), deps=[t_exp])
                t_rc = step("vector", lambda ve: ve.reciprocal(out=wv[:], in_=s8[:]),
                            deps=[t_rsm])
                t_wtsA = step("vector", lambda ve, t32=t32: ve.tensor_scalar(
                    out=p_wts[:, t32 * TK:(t32 + 1) * TK], in0=exw[:],
                    scalar1=wv[:], scalar2=None, op0=ALU.mult), deps=[t_rc])
                tprev = t_wtsA
                for k in range(TK):
                    t_mk = step("vector", lambda ve, t32=t32, k=k: ve.tensor_scalar(
                        out=mkt[:], in0=lgt[:], scalar1=mx[:, k:k + 1],
                        scalar2=None, op0=ALU.is_equal), deps=[tprev])
                    t_sc = step("vector", lambda ve: ve.tensor_tensor(
                        out=scr64[:], in0=mkt[:], in1=c_iota64[:], op=ALU.mult),
                        deps=[t_mk])
                    tprev = step("vector", lambda ve, t32=t32, k=k: ve.reduce_sum(
                        out=eid[:, t32 * TK + k: t32 * TK + k + 1],
                        in_=scr64[:], axis=mybir.AxisListType.X), deps=[t_sc])
                t_tot = step("vector", lambda ve, t32=t32: ve.tensor_scalar(
                    out=tot[:, t32 * E:(t32 + 1) * E], in0=lgt[:],
                    scalar1=mx[:, TK - 1:TK], scalar2=None, op0=ALU.is_ge), deps=[tprev])
                vprev = t_tot

                def _cmm(te, t32=t32, seg=seg):
                    return te.matmul(ps_cnt[seg][:], c_onesf[:], tot[:, t32 * E:(t32 + 1) * E],
                                     start=(t32 % 16 == 0), stop=(t32 % 16 == 15))
                d = [t_tot]
                cnt_prev[t32] = step("tensor", _cmm, deps=d)

            def _cntev(se):
                se.activation(out=cnt01[0][:], in_=ps_cnt[0][0:1, :], func=AF.Copy)
                return se.activation(out=cnt01[1][:], in_=ps_cnt[1][0:1, :], func=AF.Copy)
            sstep("scalar", _cntev, exclude=("CC",))
            passA_done = now()
            SA.close()

            def _cntst(sy):
                return [sy.dma_start(out=cnt_in.ap()[0:1, :], in_=cnt01[0][:]),
                        sy.dma_start(out=cnt_in.ap()[1:2, :], in_=cnt01[1][:])]
            sstep("sync", _cntst, dma=2)
            tok_cnt_ag = sstep("gpsimd", lambda g: g.collective_compute(
                "AllGather", ALU.bypass, replica_groups=RG,
                ins=[cnt_in.ap().opt()], outs=[cnt_out.ap().opt()]), cc=True)
            tok_x3rep = step("gpsimd", lambda g: g.collective_compute(
                "AllGather", ALU.bypass, replica_groups=RG,
                ins=[x3row_dram.ap().opt()], outs=[x3rep.ap().opt()]),
                deps=x3row_toks + [tok_cnt_ag], cc=True)

            _sstep_real = sstep

            def sstep(queue, fn, dma=0, cc=False, exclude=(), extra=()):
                return _sstep_real(queue, fn, dma=dma, cc=cc,
                                   exclude=tuple(set(exclude) | {"CC"}),
                                   extra=list(extra) + [tok_cnt_ag])
            cnta = sbuf(S, "p3_cnta", [16, E], F32)
            pre01 = [sbuf(S, "p3_pre0", [1, E], F32), sbuf(S, "p3_pre1", [1, E], F32)]
            pfull = sbuf(S, "p3_pfull", [128, E], F32)
            sr = sbuf(S, "p3_sr", [128, E], F32)
            psk = sbuf(S, "p3_psk", [128, E], F32)
            pos = sbuf(S, "p3_pos", [128, TK], F32)
            slotf = sbuf(S, "p3_slotf", [128, TK], F32)
            vmk = sbuf(S, "p3_vm", [128, TK], F32)
            idxf = sbuf(S, "p3_idxf", [128, TK], F32)
            rowid = sbuf(S, "p3_rowid", [128, 1], F32)
            payA = sbuf(S, "p3_payA", [128, NTILE * 2 * TK], F32)
            e1s = sbuf(S, "p3_e1s", [128, E], F32)
            ps_tp = psum(S, "ps_tp", [128, E])
            ps_run = [psum(S, "ps_run0", [128, E]), psum(S, "ps_run1", [128, E])]
            ps_pre = psum(S, "ps_pre", [1, E])

            t_cnta = sstep("sync", lambda sy: sy.dma_start(out=cnta[:], in_=cnt_out.ap()),
                           dma=1, exclude=("CC",), extra=[tok_cnt_ag])
            pre_ev = {}
            t_pev = None
            for b_i in range(B):
                def _pmm(te, b_i=b_i):
                    return te.matmul(ps_pre[:], c_presel[:, b_i:b_i + 1], cnta[:],
                                     start=True, stop=True)
                d = [t_cnta] + ([t_pev] if t_pev else [])
                t_pmm = step("tensor", _pmm, deps=d)
                t_pev = step("scalar", lambda se, b_i=b_i: se.activation(
                    out=pre01[b_i][:], in_=ps_pre[:], func=AF.Copy), deps=[t_pmm])
                pre_ev[b_i] = t_pev

            vtok = [sstep("vector", lambda ve: ve.nop(), exclude=("CC",),
                          extra=[tok_cnt_ag])]
            e1_tok = {}
            pf_tok = {}
            acc_tok = {}
            base_done = {}
            sc_glist = []
            for t32 in range(NTILE):
                seg = t32 // 16
                if t32 % 16 == 0:
                    def _base(te, seg=seg):
                        return te.matmul(ps_run[seg][:], c_onesf[0:1, :], pre01[seg][:],
                                         start=True, stop=False)
                    d = [pre_ev[seg]]
                    base_done[seg] = step("tensor", _base, deps=d)

                def _tp(te, t32=t32):
                    return te.matmul(ps_tp[:], c_lt[:], tot[:, t32 * E:(t32 + 1) * E],
                                     start=True, stop=True)
                d = [base_done[seg]]
                if t32 >= 1:
                    d.append(e1_tok[t32 - 1])
                t_tp = step("tensor", _tp, deps=d)
                d = [t_tp]
                if t32 >= 1:
                    d.append(pf_tok[t32 - 1])
                e1_tok[t32] = step("scalar", lambda se: se.activation(
                    out=e1s[:], in_=ps_tp[:], func=AF.Copy), deps=d)

                def _pf(ve, seg=seg):
                    ve.tensor_tensor(out=pfull[:], in0=e1s[:], in1=ps_run[seg][:],
                                     op=ALU.add)
                    return ve.memset(sr[:], 0.0)
                d = [e1_tok[t32], vtok[-1]]
                if t32 % 16 != 0:
                    d.append(acc_tok[t32 - 1])
                pf_tok[t32] = step("vector", _pf, deps=d)
                vtok.append(pf_tok[t32])

                def _acc(te, t32=t32, seg=seg):
                    return te.matmul(ps_run[seg][:], c_onesf[:], tot[:, t32 * E:(t32 + 1) * E],
                                     start=False, stop=(t32 % 16 == 15))
                acc_tok[t32] = step("tensor", _acc, deps=[pf_tok[t32]])

                for k in range(TK):
                    def _s1(ve, t32=t32, k=k):
                        if k > 0:
                            ve.tensor_tensor(out=sr[:], in0=sr[:], in1=mkt[:], op=ALU.add)
                        return ve.tensor_scalar(out=mkt[:], in0=c_iota64[:],
                                                scalar1=eid[:, t32 * TK + k: t32 * TK + k + 1],
                                                scalar2=None, op0=ALU.is_equal)
                    t1 = step("vector", _s1, deps=[vtok[-1]])

                    def _s1b(ve):
                        ve.tensor_tensor(out=psk[:], in0=pfull[:], in1=sr[:], op=ALU.add)
                        return ve.tensor_tensor(out=scr64[:], in0=mkt[:], in1=psk[:],
                                                op=ALU.mult)
                    t1b = step("vector", _s1b, deps=[t1])
                    t2 = step("vector", lambda ve, k=k: ve.reduce_sum(
                        out=pos[:, k:k + 1], in_=scr64[:], axis=mybir.AxisListType.X),
                        deps=[t1b])
                    vtok.append(t2)

                t3a = step("vector", lambda ve, t32=t32: ve.scalar_tensor_tensor(
                    out=slotf[:], in0=eid[:, t32 * TK:(t32 + 1) * TK],
                    scalar=float(CAP), in1=pos[:], op0=ALU.mult, op1=ALU.add),
                    deps=[vtok[-1]])
                t3b = step("vector", lambda ve: ve.tensor_scalar(
                    out=vmk[:], in0=pos[:], scalar1=float(CAP), scalar2=None,
                    op0=ALU.is_ge), deps=[t3a])
                t3 = step("vector", lambda ve: ve.scalar_tensor_tensor(
                    out=idxf[:], in0=vmk[:], scalar=BIGIDX, in1=slotf[:],
                    op0=ALU.mult, op1=ALU.add), deps=[t3b])
                t4 = step("vector", lambda ve, t32=t32: ve.tensor_copy(
                    p_idxi[:, t32 * TK:(t32 + 1) * TK], idxf[:]), deps=[t3])

                t5a = step("vector", lambda ve, t32=t32: ve.tensor_scalar(
                    out=rowid[:], in0=c_iota128[:], scalar1=c_rowbase[:],
                    scalar2=float(t32 * 128 + 1), op0=ALU.add, op1=ALU.add), deps=[t4])
                pview = payA[:].rearrange("p (t k c) -> p t k c", t=NTILE, k=TK)

                def _s3(ve, t32=t32):
                    ve.tensor_scalar(
                        out=pview[:, t32, :, 0], in0=p_wts[:, t32 * TK:(t32 + 1) * TK],
                        scalar1=0.0, scalar2=rowid[:], op0=ALU.mult, op1=ALU.add)
                    return ve.tensor_copy(pview[:, t32, :, 1],
                                          p_wts[:, t32 * TK:(t32 + 1) * TK])
                t5 = step("vector", _s3, deps=[t5a])
                vtok.append(t5)

                def _sc8(g, t32=t32):
                    return [g.indirect_dma_start(
                        out=meta_in.ap(),
                        out_offset=IndirectOffsetOnAxis(
                            ap=p_idxi[:, t32 * TK + k: t32 * TK + k + 1], axis=0),
                        in_=payA[:, (t32 * TK + k) * 2:(t32 * TK + k) * 2 + 2],
                        in_offset=None,
                        bounds_check=breg[0], oob_is_err=False) for k in range(TK)]
                d = [t5]
                if len(sc_glist) >= 1:
                    d.append(sc_glist[-1])
                sc_glist.append(step("gpsimd", _sc8, deps=d, dma=TK))

            sstep = _sstep_real
            tok_meta = sstep("gpsimd", lambda g: g.collective_compute(
                "AllReduce", ALU.add, replica_groups=RG,
                ins=[meta_in.ap().opt()], outs=[meta_out.ap().opt()]), cc=True)

            if nphases >= 4:
                wgub = sbuf(S, "p4_wgu", [128, NCB * 2 * SF], BF16)
                wsdb = sbuf(S, "p4_wsd", [128, 6 * D], BF16)
                hb = sbuf(S, "p4_h", [128, 6 * TPC], BF16)
                shc = sbuf(S, "p4_sh", [128, 2 * TCH], F32)
                hgs4 = sbuf(S, "p4_hgs", [128, TCH], BF16)
                ps_g = psum(S, "p4_psg", [128, TCH])
                ps_u = psum(S, "p4_psu", [128, TCH])
                ps_d = psum(S, "p4_psd", [128, 2 * TCH])

                t_wgu = step("gpsimd", lambda g: g.dma_start(
                    out=wgub[:].rearrange("p (k m) -> p k m", k=NCB),
                    in_=wsgu.rearrange("(k p) m -> p k m", p=128)), deps=[], dma=1)
                t_wsd = step("gpsimd", lambda g: g.dma_start(
                    out=wsdb[:].rearrange("p (k m) -> p k m", k=6),
                    in_=wsd.rearrange("(k p) m -> p k m", p=128)), deps=[], dma=1)

                t_se = None
                t_ve = None
                t_te = None
                for m in range(6):
                    for c in range(NTCH):
                        def _gu(te, m=m, c=c):
                            for k in range(NCB):
                                wt = wgub[:, k * 2 * SF + m * 128: k * 2 * SF + (m + 1) * 128]
                                te.ldweights(wt)
                                te.matmul(ps_g[:], wt,
                                          x3bf[:, k * TPC + c * TCH: k * TPC + (c + 1) * TCH],
                                          start=(k == 0), stop=(k == NCB - 1))
                            for k in range(NCB):
                                wt = wgub[:, k * 2 * SF + (m + 6) * 128: k * 2 * SF + (m + 7) * 128]
                                te.ldweights(wt)
                                last = te.matmul(
                                    ps_u[:], wt,
                                    x3bf[:, k * TPC + c * TCH: k * TPC + (c + 1) * TCH],
                                    start=(k == 0), stop=(k == NCB - 1))
                            return last
                        d = [t_wgu] + passA_done
                        if t_se is not None:
                            d += [t_se, t_ve]
                        t_te = step("tensor", _gu, deps=d)
                        t_se = step("scalar", lambda se: se.activation(
                            out=hgs4[:], in_=ps_g[:], func=AF.Silu), deps=[t_te])
                        t_ve = step("vector", lambda ve, m=m, c=c: ve.tensor_tensor(
                            out=hb[:, m * TPC + c * TCH: m * TPC + (c + 1) * TCH],
                            in0=hgs4[:], in1=ps_u[:], op=ALU.mult), deps=[t_te, t_se])

                sh_v = shared_dram.ap().rearrange("(cb p) n -> p cb n", p=128)
                t_hdone = t_ve
                t_dev = None
                t_dst = None
                p4_sh_toks = []
                for m in range(NCB):
                    for quad in range(4):
                        def _dn(te, m=m, quad=quad):
                            for k in range(6):
                                wt = wsdb[:, k * D + m * 128: k * D + (m + 1) * 128]
                                te.ldweights(wt)
                                for c in range(2):
                                    cg = quad * 2 + c
                                    last = te.matmul(
                                        ps_d[:, c * TCH:(c + 1) * TCH], wt,
                                        hb[:, k * TPC + cg * TCH: k * TPC + (cg + 1) * TCH],
                                        start=(k == 0), stop=(k == 5))
                            return last
                        d = [t_wsd, t_hdone]
                        if t_dev is not None:
                            d.append(t_dev)
                        t_te = step("tensor", _dn, deps=d)

                        def _dev(se):
                            for c in range(2):
                                last = se.activation(out=shc[:, c * TCH:(c + 1) * TCH],
                                                     in_=ps_d[:, c * TCH:(c + 1) * TCH],
                                                     func=AF.Copy)
                            return last
                        d = [t_te]
                        if t_dst is not None:
                            d.append(t_dst)
                        t_dev = step("scalar", _dev, deps=d)
                        t_dst = step("sync", lambda sy, m=m, quad=quad: sy.dma_start(
                            out=sh_v[:, m, quad * 1024:(quad + 1) * 1024], in_=shc[:]),
                            deps=[t_dev], dma=1)
                        p4_sh_toks.append(t_dst)
                p4_done = now()

        with ExitStack() as S:
          if nphases >= 5:
            meta_g = [sbuf(S, "p5_meta%d" % i_, [128, 2], F32) for i_ in range(32)]
            gidx = [sbuf(S, "p5_gidx%d" % p_, [128, 4], I32) for p_ in range(2)]
            scr1 = sbuf(S, "p5_scr1", [128, 1], F32)
            scr2 = sbuf(S, "p5_scr2", [128, 1], F32)
            xbs = [sbuf(S, "p5_xbs%d" % i_, [128, D], BF16) for i_ in range(8)]
            xbt = [sbuf(S, "p5_xbt%d" % p_, [128, NCB * CAP], BF16) for p_ in range(2)]
            wgb = [sbuf(S, "p5_wg%d" % p_, [128, NCB * FF], BF16) for p_ in range(2)]
            wub = [sbuf(S, "p5_wu%d" % p_, [128, NCB * FF], BF16) for p_ in range(2)]
            wdb = [sbuf(S, "p5_wd%d" % p_, [128, 6 * D], BF16) for p_ in range(2)]
            hgs = sbuf(S, "p5_hgs", [128, CAP], BF16)
            hbe = sbuf(S, "p5_h", [128, 6 * CAP], BF16)
            ycmb = [sbuf(S, "p5_ycm%d" % p_, [128, NCB * CAP], BF16) for p_ in range(2)]
            yt = [sbuf(S, "p5_yt%d" % p_, [128, D], BF16) for p_ in range(2)]
            ys = [sbuf(S, "p5_ys%d" % p_, [128, D], BF16) for p_ in range(2)]
            ps_a = psum(S, "p5_psa", [128, CAP])
            ps_b = psum(S, "p5_psb", [128, CAP])
            ps_c = psum(S, "p5_psc", [128, CAP])

            p5_entry = now()
            xbst_tok = {}
            meta_tok = {}
            vprev = None
            gx_prev = {}
            for e in range(EPC):
                tprev = None
                for g_i in range(4):
                    if e == 0 and g_i == 0:
                        d = list(p5_entry)
                    else:
                        d = [tprev or vprev]
                    if g_i == 0 and e >= 2:
                        d.append(gx_prev[e - 2])
                    t1 = step("vector", lambda ve, e=e, g_i=g_i: ve.tensor_scalar(
                        out=scr1[:], in0=c_iota128[:], scalar1=c_slotb[:],
                        scalar2=float(e * CAP + g_i * 128), op0=ALU.add, op1=ALU.add),
                        deps=d)
                    tprev = step("vector", lambda ve, e=e, g_i=g_i: ve.tensor_copy(
                        gidx[e % 2][:, g_i:g_i + 1], scr1[:]), deps=[t1])
                d = [tprev, tok_meta]
                if e >= 1:
                    d.append(gx_prev[e - 1])

                def _gmeta(g, e=e):
                    return [g.indirect_dma_start(
                        out=meta_g[e * 4 + g_i][:], out_offset=None,
                        in_=meta_out.ap(),
                        in_offset=IndirectOffsetOnAxis(ap=gidx[e % 2][:, g_i:g_i + 1], axis=0),
                        bounds_check=breg[0], oob_is_err=False) for g_i in range(4)]
                t_gmeta = step("gpsimd", _gmeta, deps=d, dma=4)

                st_toks5 = []
                tprev = t_gmeta
                for g_i in range(4):
                    t1 = step("vector", lambda ve, e=e, g_i=g_i: ve.tensor_scalar(
                        out=scr1[:], in0=meta_g[e * 4 + g_i][:, 0:1], scalar1=0.0,
                        scalar2=None, op0=ALU.is_equal), deps=[tprev])
                    t2 = step("vector", lambda ve, e=e, g_i=g_i: ve.scalar_tensor_tensor(
                        out=scr2[:], in0=scr1[:], scalar=BIGIDX,
                        in1=meta_g[e * 4 + g_i][:, 0:1], op0=ALU.mult, op1=ALU.add),
                        deps=[t1])
                    t3 = step("vector", lambda ve: ve.tensor_scalar(
                        out=scr2[:], in0=scr2[:], scalar1=1.0, scalar2=None,
                        op0=ALU.subtract), deps=[t2])
                    tprev = step("vector", lambda ve, e=e, g_i=g_i: ve.tensor_copy(
                        gidx[e % 2][:, g_i:g_i + 1], scr2[:]), deps=[t3])
                    bi = (e * 4 + g_i) % 8
                    d = [tprev, tok_x3rep]
                    if (e * 4 + g_i) >= 4:
                        d.append(xbst_tok[e * 4 + g_i - 4])
                    t_gx = step("gpsimd", lambda g, e=e, g_i=g_i, bi=bi: g.indirect_dma_start(
                        out=xbs[bi][:], out_offset=None,
                        in_=x3rep.ap(),
                        in_offset=IndirectOffsetOnAxis(ap=gidx[e % 2][:, g_i:g_i + 1], axis=0),
                        bounds_check=breg[0], oob_is_err=False), deps=d, dma=1)
                    gx_prev[e] = t_gx
                    t_st = step("sync", lambda sy, e=e, g_i=g_i, bi=bi: sy.dma_start(
                        out=xb_dram.ap()[e * CAP + g_i * 128: e * CAP + (g_i + 1) * 128, :],
                        in_=xbs[bi][:]), deps=[t_gx], dma=1)
                    xbst_tok[e * 4 + g_i] = t_st
                    st_toks5.append(t_st)
                meta_tok[e] = st_toks5
                vprev = tprev

            def _mk_wld(e, par):
                def _wld(g):
                    return [
                        g.dma_start(out=wgb[par][:].rearrange("p (k m) -> p k m", k=NCB),
                                    in_=wg_all[e * D:(e + 1) * D, :]
                                    .rearrange("(k p) m -> p k m", p=128)),
                        g.dma_start(out=wub[par][:].rearrange("p (k m) -> p k m", k=NCB),
                                    in_=wu_all[e * D:(e + 1) * D, :]
                                    .rearrange("(k p) m -> p k m", p=128)),
                        g.dma_start(out=wdb[par][:].rearrange("p (k m) -> p k m", k=6),
                                    in_=wd_all[e * FF:(e + 1) * FF, :]
                                    .rearrange("(k p) m -> p k m", p=128))]
                return _wld
            tok_wld = {}
            tok_free_w = {}
            tok_xbt = {}
            tok_ys = {}
            tok_ybst = {}
            slot_toks = []
            t_se = None
            t_ve = None
            t_ev = None
            tok_ytr_all = {}
            for e in (0, 1):
                tok_wld[e] = step("gpsimd", _mk_wld(e, e % 2), deps=p5_entry, dma=3)
            for e in range(EPC):
                par = e % 2
                d = list(meta_tok[e])
                if e >= 2:
                    d.append(tok_free_w[e - 2])

                def _xbt(sy, e=e, par=par):
                    return [sy.dma_start(out=xbt[par][:, cb * CAP:(cb + 1) * CAP],
                                         in_=xb_dram.ap()[e * CAP:(e + 1) * CAP,
                                                          cb * 128:(cb + 1) * 128],
                                         transpose=True) for cb in range(NCB)]
                tok_xbt[e] = step("sync", _xbt, deps=d, dma=NCB)

                for m in range(6):
                    def _ge(te, m=m, par=par):
                        for k in range(NCB):
                            wt = wgb[par][:, k * FF + m * 128: k * FF + (m + 1) * 128]
                            te.ldweights(wt)
                            te.matmul(ps_a[:], wt, xbt[par][:, k * CAP:(k + 1) * CAP],
                                      start=(k == 0), stop=(k == NCB - 1))
                        for k in range(NCB):
                            wt = wub[par][:, k * FF + m * 128: k * FF + (m + 1) * 128]
                            te.ldweights(wt)
                            last = te.matmul(ps_b[:], wt, xbt[par][:, k * CAP:(k + 1) * CAP],
                                             start=(k == 0), stop=(k == NCB - 1))
                        return last
                    d = [tok_xbt[e], tok_wld[e]]
                    if t_se is not None:
                        d += [t_se, t_ve]
                    t_te = step("tensor", _ge, deps=d)
                    t_se = step("scalar", lambda se: se.activation(
                        out=hgs[:], in_=ps_a[:], func=AF.Silu), deps=[t_te])
                    t_ve = step("vector", lambda ve, m=m: ve.tensor_tensor(
                        out=hbe[:, m * CAP:(m + 1) * CAP], in0=hgs[:], in1=ps_b[:],
                        op=ALU.mult), deps=[t_te, t_se])

                t_hdone = t_ve
                first_ev = True
                for m in range(NCB):
                    def _de(te, m=m, par=par):
                        for k in range(6):
                            wt = wdb[par][:, k * D + m * 128: k * D + (m + 1) * 128]
                            te.ldweights(wt)
                            last = te.matmul(ps_c[:], wt, hbe[:, k * CAP:(k + 1) * CAP],
                                             start=(k == 0), stop=(k == 5))
                        return last
                    d = [t_hdone]
                    if t_ev is not None:
                        d.append(t_ev)
                    t_te = step("tensor", _de, deps=d)
                    dev = [t_te]
                    if first_ev and e >= 2:
                        dev.append(tok_ybst[e - 2])
                        first_ev = False
                    t_ev = step("scalar", lambda se, m=m, par=par: se.activation(
                        out=ycmb[par][:, m * CAP:(m + 1) * CAP], in_=ps_c[:], func=AF.Copy),
                        deps=dev)
                tok_free_w[e] = t_te
                if e + 2 < EPC:
                    tok_wld[e + 2] = step("gpsimd", _mk_wld(e + 2, par), deps=[t_te], dma=3)

                d = [t_ev]
                if e >= 2 and e - 2 in tok_ybst:
                    d.append(tok_ytr_all[e - 2])
                t_ybst = step("sync", lambda sy, par=par: sy.dma_start(
                    out=yb_dram.ap().rearrange("(q m p) c -> q p m c", q=2, p=128)[par],
                    in_=ycmb[par][:].rearrange("p (m c) -> p m c", m=NCB)), deps=d, dma=1)
                tok_ybst[e] = t_ybst
                t_last = None
                for g_i in range(4):
                    yp = g_i % 2
                    d = [t_ybst]
                    if g_i >= 2:
                        d.append(tok_ysl[g_i - 2])
                    elif e >= 1:
                        d.append(tok_ys[e - 1])
                    t_tr = step("sync", lambda sy, g_i=g_i, par=par, yp=yp: sy.dma_start(
                        out=yt[yp][:], in_=yb_dram.ap()[par * D:(par + 1) * D,
                                                        g_i * 128:(g_i + 1) * 128],
                        transpose=True), deps=d, dma=1)
                    t_w = step("vector", lambda ve, e=e, g_i=g_i, yp=yp: ve.tensor_scalar(
                        out=ys[yp][:], in0=yt[yp][:], scalar1=meta_g[e * 4 + g_i][:, 1:2],
                        scalar2=None, op0=ALU.mult), deps=[t_tr])
                    t_sl = step("sync", lambda sy, e=e, g_i=g_i, yp=yp: sy.dma_start(
                        out=slot_out.ap()[e * CAP + g_i * 128: e * CAP + (g_i + 1) * 128, :],
                        in_=ys[yp][:]), deps=[t_w], dma=1)
                    if g_i == 0:
                        tok_ysl = {}
                    tok_ysl[g_i] = t_sl
                    t_last = t_sl
                    slot_toks.append(t_sl)
                tok_ytr_all[e] = t_last
                tok_ys[e] = t_last

            tok_slot_ag = step("gpsimd", lambda g: g.collective_compute(
                "AllGather", ALU.bypass, replica_groups=RG,
                ins=[slot_out.ap().opt()], outs=[slot_rep.ap().opt()]),
                deps=slot_toks[-1:] + [slot_toks[-5]], cc=True)

        with ExitStack() as S:
          if nphases >= 6:
            gbuf = [sbuf(S, "p6_gb%d" % p_, [128, TK * D], BF16) for p_ in range(2)]
            p6r = [sbuf(S, "p6_r%d" % p_, [128, D], F32) for p_ in range(2)]
            sA_s = sbuf(S, "p6_sA", [128, D], F32)
            sB_s = sbuf(S, "p6_sB", [128, D], F32)
            sA = [sA_s, sA_s]
            sB = [sB_s, sB_s]
            p6rc = [sbuf(S, "p6_rc%d" % p_, [128, D], F32) for p_ in range(2)]
            ps_t = [psum(S, "p6_pst%d" % p_, [128, 4 * 128]) for p_ in range(2)]
            rview = routed_dram.ap().rearrange("(a p) c -> a p c", p=128)
            rcm_v = routedcm_dram.ap().rearrange("(cb p) n -> p cb n", p=128)
            fc = [sbuf(S, "p6_f%d" % p_, [128, NCB * TCH], F32) for p_ in range(2)]
            sc_s = sbuf(S, "p6_s", [128, NCB * TCH], F32)
            sc_ = [sc_s, sc_s]
            x2c_s = sbuf(S, "p6_x2", [128, NCB * TCH], F32)
            x2c = [x2c_s, x2c_s]
            r5c = sbuf(S, "p6_r5", [128, NCB * TCH], F32)
            P = ln_tiles(S, "p6")
            y_v = ycm.rearrange("(cb p) n -> p cb n", p=128)
            sh_v = shared_dram.ap().rearrange("(cb p) n -> p cb n", p=128)
            f_tok = {}
            ya_tok = {}
            yst_tok = {}
            ln5_tok = {}

            def emit_final_chunk(i):
                par = i % 2
                d = [st_toks[4 * i + jj] for jj in range(4)]
                if i >= 2:
                    d += ln5_tok[i - 2]
                tf_ = step("sync", lambda sy, i=i, par=par: sy.dma_start(
                    out=fc[par][:].rearrange("p (cb n) -> p cb n", cb=NCB),
                    in_=rcm_v[:, :, i * TCH:(i + 1) * TCH]), deps=d, dma=1)
                d = [p4_sh_toks[-1]]
                if i >= 1:
                    d.append(f_tok[i - 1])
                tss_ = step("sync", lambda sy, i=i, par=par: sy.dma_start(
                    out=sc_[par][:].rearrange("p (cb n) -> p cb n", cb=NCB),
                    in_=sh_v[:, :, i * TCH:(i + 1) * TCH]), deps=d, dma=1)
                d = [] if i < 1 else [yst_tok[i - 1]]
                tx2_ = step("sync", lambda sy, i=i, par=par: sy.dma_start(
                    out=x2c[par][:].rearrange("p (cb n) -> p cb n", cb=NCB),
                    in_=x2_v[:, :, i * TCH:(i + 1) * TCH]), deps=d, dma=1)

                def _fadd(ve, par=par):
                    for cb in range(NCB):
                        last = ve.tensor_tensor(out=fc[par][:, cb * TCH:(cb + 1) * TCH],
                                                in0=fc[par][:, cb * TCH:(cb + 1) * TCH],
                                                in1=sc_[par][:, cb * TCH:(cb + 1) * TCH],
                                                op=ALU.add)
                    return last
                t_fa = step("vector", _fadd, deps=[tf_, tss_])
                f_tok[i] = t_fa
                ow = [ya_tok[i - 1]] if i >= 1 else []
                ln5 = emit_ln_tok(P, fc[par][:], r5c[:], [t_fa], out_war=ow)
                ln5_tok[i] = ln5

                def _yadd(ve, par=par):
                    for cb in range(NCB):
                        last = ve.tensor_tensor(out=x2c[par][:, cb * TCH:(cb + 1) * TCH],
                                                in0=x2c[par][:, cb * TCH:(cb + 1) * TCH],
                                                in1=r5c[:, cb * TCH:(cb + 1) * TCH],
                                                op=ALU.add)
                    return last
                ya = step("vector", _yadd, deps=ln5 + [tx2_])
                ya_tok[i] = ya
                yst_tok[i] = step("sync", lambda sy, i=i, par=par: sy.dma_start(
                    out=y_v[:, :, i * TCH:(i + 1) * TCH],
                    in_=x2c[par][:].rearrange("p (cb n) -> p cb n", cb=NCB)),
                    deps=[ya], dma=1)

            p6_entry = now()
            tr_tok = {}
            ev_tok = {}
            st_toks = {}
            sum_tok = {}
            glist = []
            for i in range(NTILE):
                par = i % 2
                d = [tok_slot_ag]
                if i >= 2:
                    d += [sum_tok[i - 2][0], sum_tok[i - 2][1]]
                t_gz = step("vector", lambda ve, par=par: ve.memset(gbuf[par][:], 0.0),
                            deps=d)
                gt = []
                for k in range(TK):
                    d = [t_gz]
                    if len(glist) >= 8:
                        d.append(glist[-8])
                    t_g = step("gpsimd", lambda g, i=i, k=k, par=par: g.indirect_dma_start(
                        out=gbuf[par][:, k * D:(k + 1) * D],
                        out_offset=None,
                        in_=slot_rep.ap(),
                        in_offset=IndirectOffsetOnAxis(
                            ap=p_idxi[:, i * TK + k: i * TK + k + 1], axis=0),
                        bounds_check=breg[0], oob_is_err=False), deps=d, dma=1)
                    gt.append(t_g)
                    glist.append(t_g)

                def _sa(ve, par=par):
                    ve.tensor_tensor(out=sA[par][:], in0=gbuf[par][:, 0:D],
                                     in1=gbuf[par][:, D:2 * D], op=ALU.add)
                    ve.tensor_tensor(out=sA[par][:], in0=sA[par][:],
                                     in1=gbuf[par][:, 2 * D:3 * D], op=ALU.add)
                    return ve.tensor_tensor(out=sA[par][:], in0=sA[par][:],
                                            in1=gbuf[par][:, 3 * D:4 * D], op=ALU.add)
                d = gt[0:4] + ([sum_tok[i - 1][2]] if i >= 1 else [])
                t_sa = step("vector", _sa, deps=d)

                def _sb(po, par=par):
                    po.tensor_tensor(out=sB[par][:], in0=gbuf[par][:, 4 * D:5 * D],
                                     in1=gbuf[par][:, 5 * D:6 * D], op=ALU.add)
                    po.tensor_tensor(out=sB[par][:], in0=sB[par][:],
                                     in1=gbuf[par][:, 6 * D:7 * D], op=ALU.add)
                    return po.tensor_tensor(out=sB[par][:], in0=sB[par][:],
                                            in1=gbuf[par][:, 7 * D:8 * D], op=ALU.add)
                d = gt[4:8] + ([sum_tok[i - 1][2]] if i >= 1 else [])
                t_sb = step("gpsimd", _sb, deps=d)
                d = [t_sa, t_sb]
                if i >= 2:
                    d.append(tr_tok[i - 2])
                t_sum = step("vector", lambda ve, par=par: ve.tensor_tensor(
                    out=p6r[par][:], in0=sA[par][:], in1=sB[par][:], op=ALU.add), deps=d)
                sum_tok[i] = (t_sa, t_sb, t_sum)
                if t_routed is not None:
                    step("sync", lambda sy, i=i, par=par: sy.dma_start(
                        out=rview[i], in_=p6r[par][:]), deps=[t_sum], dma=1)

                def _tr4(te, i=i, par=par, grp=0):
                    for cb in range(4 * grp, 4 * grp + 4):
                        last = te.transpose(
                            ps_t[par][:, (cb % 4) * 128:((cb % 4) + 1) * 128],
                            p6r[par][:, cb * 128:(cb + 1) * 128], c_idf[:])
                    return last

                def _tev4(se, i=i, par=par, grp=0):
                    for cb in range(4 * grp, 4 * grp + 4):
                        last = se.activation(
                            out=p6rc[par][:, cb * 128:(cb + 1) * 128],
                            in_=ps_t[par][:, (cb % 4) * 128:((cb % 4) + 1) * 128],
                            func=AF.Copy)
                    return last
                tprev_ev = ev_tok.get(i - 2)
                for grp in range(2):
                    d = [t_sum]
                    if tprev_ev is not None:
                        d.append(tprev_ev)
                    t_t = step("tensor", lambda te, i=i, par=par, grp=grp: _tr4(te, i, par, grp),
                               deps=d)
                    dev = [t_t]
                    if grp == 0 and i >= 2:
                        dev.append(st_toks[i - 2])
                    tprev_ev = step("scalar", lambda se, i=i, par=par, grp=grp: _tev4(se, i, par, grp),
                                    deps=dev)
                tr_tok[i] = t_t
                ev_tok[i] = tprev_ev
                st_toks[i] = step("sync", lambda sy, i=i, par=par: sy.dma_start(
                    out=rcm_v[:, :, i * 128:(i + 1) * 128],
                    in_=p6rc[par][:].rearrange("p (cb n) -> p cb n", cb=NCB)),
                    deps=[tprev_ev], dma=1)
                if i % 4 == 3:
                    emit_final_chunk(i // 4)

        def _taps(sy):
            out = []
            for t_ap, srcap in [(t_x1, x1_dram.ap()), (t_ug, ug_a2a_out.ap()),
                                (t_hs, hs_a2a_out.ap()), (t_x2, x2_dram.ap()),
                                (t_x3, x3cm_dram.ap()), (t_meta, meta_out.ap()),
                                (t_routed, routed_dram.ap()), (t_shared, shared_dram.ap())]:
                if t_ap is not None:
                    out.append(sy.dma_start(out=t_ap, in_=srcap))
            if t_wts is not None:
                out.append(sy.dma_start(out=t_wts, in_=p_wts[:]))
            if t_idx is not None:
                out.append(sy.dma_start(out=t_idx, in_=p_idxi[:]))
            return out
        ntap = sum(x is not None for x in
                   (t_x1, t_ug, t_hs, t_x2, t_x3, t_meta, t_routed, t_shared, t_wts, t_idx))
        if ntap:
            sstep("sync", _taps, dma=ntap)

        fin_deps = now()

        @block.sync
        def _(sy):
            for s, v in fin_deps:
                sy.wait_ge(SEMQ[s], v)

    return nc, tap_aps


def _mk_inputs(L0_attn_x, L0_attn_gamma, L0_attn_beta, L0_attn_A, L0_attn_W_in,
               L0_attn_b_in, L0_attn_W_out, L0_attn_b_out, L0_ffn_router_w,
               L0_ffn_w_gate, L0_ffn_w_up, L0_ffn_w_down, L0_ffn_ws_gate_up,
               L0_ffn_ws_down):
    f32 = np.float32
    bf16 = ml_dtypes.bfloat16
    x = np.asarray(L0_attn_x, f32)
    gamma = np.asarray(L0_attn_gamma, f32)
    beta = np.asarray(L0_attn_beta, f32)
    A = np.asarray(L0_attn_A, f32)
    W_in = np.ascontiguousarray(np.asarray(L0_attn_W_in, f32))
    b_in = np.asarray(L0_attn_b_in, f32)
    W_out = np.ascontiguousarray(np.asarray(L0_attn_W_out, f32))
    b_out = np.asarray(L0_attn_b_out, f32)
    router_w = np.asarray(L0_ffn_router_w, f32)
    w_gate = np.asarray(L0_ffn_w_gate, f32)
    w_up = np.asarray(L0_ffn_w_up, f32)
    w_down = np.asarray(L0_ffn_w_down, f32)
    ws_gate_up = np.ascontiguousarray(np.asarray(L0_ffn_ws_gate_up, f32))
    ws_down = np.ascontiguousarray(np.asarray(L0_ffn_ws_down, f32))

    ones_bf = np.ones((128, 128), bf16)
    ones_f = np.ones((128, 128), f32)
    lt = (np.arange(128)[:, None] < np.arange(128)[None, :]).astype(f32)
    ident = np.eye(128, dtype=f32)
    iota64_ = np.tile(np.arange(E, dtype=f32), (128, 1))
    detie_ = (-1e-6 * np.arange(E, dtype=f32)).astype(f32)
    detie_ = np.tile(detie_, (128, 1))
    iota128_ = np.arange(128, dtype=f32).reshape(128, 1)

    in_maps = []
    for j in range(NCORES):
        xs = x[:, j * LPC:(j + 1) * LPC, :].reshape(B * LPC, D)
        xcm_j = np.ascontiguousarray(xs.T)
        presel_j = np.zeros((16, 2), f32)
        for bp in range(B):
            for jp in range(NCORES):
                for b_i in range(B):
                    if bp * NCORES + jp < b_i * NCORES + j:
                        presel_j[2 * jp + bp, b_i] = 1.0
        d = {
            "xcm": xcm_j,
            "g8": np.ascontiguousarray(gamma.reshape(NCB, 128).T),
            "b8": np.ascontiguousarray(beta.reshape(NCB, 128).T),
            "bo8": np.ascontiguousarray(b_out.reshape(NCB, 128).T),
            "acol": np.ascontiguousarray(A[j * 128:(j + 1) * 128].reshape(128, 1)),
            "w_in": W_in,
            "w_out": W_out,
            "rwt": np.ascontiguousarray(router_w.T),
            "wsgu": ws_gate_up,
            "wsd": ws_down,
            "wg_all": np.ascontiguousarray(w_gate[j * EPC:(j + 1) * EPC].reshape(EPC * D, FF)),
            "wu_all": np.ascontiguousarray(w_up[j * EPC:(j + 1) * EPC].reshape(EPC * D, FF)),
            "wd_all": np.ascontiguousarray(w_down[j * EPC:(j + 1) * EPC].reshape(EPC * FF, D)),
            "ones_bf": ones_bf,
            "ones_f": ones_f,
            "lt_strict": lt,
            "ident_f": ident,
            "iota64": iota64_,
            "detie": detie_,
            "iota128": iota128_,
            "rowbase": np.full((128, 1), j * TPC, f32),
            "slotb": np.full((128, 1), j * EPC * CAP, f32),
            "presel": presel_j,
            "ub8": np.ascontiguousarray(b_in[:D].reshape(NCB, 128).T),
            "epsc": np.full((128, 1), LN_EPS, f32),
            "gb8": np.ascontiguousarray(b_in[D:].reshape(NCB, 128).T),
        }
        in_maps.append(d)
    return in_maps


def kernel(**inputs):
    in_maps = _mk_inputs(**inputs)
    nc, _ = build_nc()
    res = run_bass_kernel_spmd(nc, in_maps, core_ids=list(range(NCORES)))
    y = np.empty((B, L, D), np.float32)
    for j in range(NCORES):
        blk = res.results[j]["ycm"]
        y[:, j * LPC:(j + 1) * LPC, :] = blk.T.reshape(B, LPC, D)
    return y


# version 20
# speedup vs baseline: 1.1717x; 1.1717x over previous
import os
import numpy as np
import ml_dtypes
from contextlib import ExitStack
import concourse.bass as bass
import concourse.mybir as mybir
from concourse.bass_utils import run_bass_kernel_spmd
from concourse.bass_core_rust import IndirectOffsetOnAxis
from concourse.aot_env import allow_ici_all_to_all

F32 = mybir.dt.float32
BF16 = mybir.dt.bfloat16
I32 = mybir.dt.int32
U32 = mybir.dt.uint32
AF = mybir.ActivationFunctionType
ALU = mybir.AluOpType

NCORES = 8
B = 2
L = 16384
D = 1024
DD = 2048
TPC = 4096
LPC = 2048
NCB = 8
E = 64
TK = 8
FF = 768
CAP = 512
SF = 768
EPC = 8
TCH = 512
NTCH = TPC // TCH
LCH = 1024
NLCH = L // LCH
NTILE = TPC // 128
LN_EPS = 1e-5
BIGIDX = 1.0e9
RG = [list(range(NCORES))]
INV_D = 1.0 / D
SWEEPS = ("P", "N", "N", "N", "N")


def build_nc(taps=(), nphases=6):
    nc = bass.Bass("TRN2", target_bir_lowering=False)
    dt_ = nc.dram_tensor

    xcm = dt_("xcm", [D, TPC], F32, kind="ExternalInput").ap()
    g8 = dt_("g8", [128, NCB], F32, kind="ExternalInput").ap()
    b8 = dt_("b8", [128, NCB], F32, kind="ExternalInput").ap()
    bo8 = dt_("bo8", [128, NCB], F32, kind="ExternalInput").ap()
    acol = dt_("acol", [128, 1], F32, kind="ExternalInput").ap()
    w_in = dt_("w_in", [D, DD], F32, kind="ExternalInput").ap()
    w_out = dt_("w_out", [D, D], F32, kind="ExternalInput").ap()
    rwt = dt_("rwt", [D, E], F32, kind="ExternalInput").ap()
    wsgu = dt_("wsgu", [D, 2 * SF], F32, kind="ExternalInput").ap()
    wsd = dt_("wsd", [SF, D], F32, kind="ExternalInput").ap()
    wg_all = dt_("wg_all", [EPC * D, FF], F32, kind="ExternalInput").ap()
    wu_all = dt_("wu_all", [EPC * D, FF], F32, kind="ExternalInput").ap()
    wd_all = dt_("wd_all", [EPC * FF, D], F32, kind="ExternalInput").ap()
    ones_bf = dt_("ones_bf", [128, 128], BF16, kind="ExternalInput").ap()
    ones_f = dt_("ones_f", [128, 128], F32, kind="ExternalInput").ap()
    lt_strict = dt_("lt_strict", [128, 128], F32, kind="ExternalInput").ap()
    ident_f = dt_("ident_f", [128, 128], F32, kind="ExternalInput").ap()
    iota64 = dt_("iota64", [128, E], F32, kind="ExternalInput").ap()
    detie = dt_("detie", [128, E], F32, kind="ExternalInput").ap()
    iota128 = dt_("iota128", [128, 1], F32, kind="ExternalInput").ap()
    rowbase = dt_("rowbase", [128, 1], F32, kind="ExternalInput").ap()
    slotb = dt_("slotb", [128, 1], F32, kind="ExternalInput").ap()
    presel = dt_("presel", [16, 2], F32, kind="ExternalInput").ap()
    ub8 = dt_("ub8", [128, NCB], F32, kind="ExternalInput").ap()
    epsc = dt_("epsc", [128, 1], F32, kind="ExternalInput").ap()
    gb8 = dt_("gb8", [128, NCB], F32, kind="ExternalInput").ap()

    ycm = dt_("ycm", [D, TPC], F32, kind="ExternalOutput").ap()

    x1_dram = dt_("x1_dram", [D, TPC], F32, kind="Internal")
    x2_dram = dt_("x2_dram", [D, TPC], F32, kind="Internal")
    x3cm_dram = dt_("x3cm_dram", [D, TPC], BF16, kind="Internal")
    shared_dram = dt_("shared_dram", [D, TPC], F32, kind="Internal")
    routed_dram = dt_("routed_dram", [TPC, D], F32, kind="Internal")
    routedcm_dram = dt_("routedcm_dram", [D, TPC], F32, kind="Internal")
    xb_dram = dt_("xb_dram", [EPC * CAP, D], BF16, kind="Internal")
    yb_dram = dt_("yb_dram", [2 * D, CAP], BF16, kind="Internal")
    ug_a2a_in = dt_("ug_a2a_in", [NCORES * 256, TPC], F32, kind="Internal")
    ug_a2a_out = dt_("ug_a2a_out", [NCORES * 256, TPC], F32, kind="Internal")
    hs_a2a_in = dt_("hs_a2a_in", [NCORES * 128, TPC], BF16, kind="Internal")
    hs_a2a_out = dt_("hs_a2a_out", [NCORES * 128, TPC], BF16, kind="Internal")
    x3row_dram = dt_("x3row_dram", [TPC, D], BF16, kind="Internal")
    x3rep = dt_("x3rep", [NCORES * TPC, D], BF16, kind="Internal", addr_space="Shared")
    slot_out = dt_("slot_out", [EPC * CAP, D], BF16, kind="Internal")
    slot_rep = dt_("slot_rep", [E * CAP, D], BF16, kind="Internal", addr_space="Shared")
    meta_in = dt_("meta_in", [E * CAP, 2], F32, kind="Internal")
    meta_out = dt_("meta_out", [E * CAP, 2], F32, kind="Internal")
    cnt_in = dt_("cnt_in", [2, E], F32, kind="Internal")
    cnt_out = dt_("cnt_out", [2 * NCORES, E], F32, kind="Internal", addr_space="Shared")

    tap_aps = {}

    def tap(name, shape, dtype):
        if name in taps:
            tap_aps[name] = dt_("tap_" + name, shape, dtype, kind="ExternalOutput").ap()
            return tap_aps[name]
        return None

    t_x1 = tap("x1", [D, TPC], F32)
    t_ug = tap("ug", [NCORES * 256, TPC], F32)
    t_hs = tap("hs", [NCORES * 128, TPC], BF16)
    t_x2 = tap("x2", [D, TPC], F32)
    t_x3 = tap("x3", [D, TPC], BF16)
    t_eid = tap("eid", [128, NTILE * TK], F32)
    t_wts = tap("wts", [128, NTILE * TK], F32)
    t_idx = tap("idx", [128, NTILE * TK], I32)
    t_meta = tap("meta", [E * CAP, 2], F32)
    t_routed = tap("routed", [TPC, D], F32)
    t_shared = tap("shared", [D, TPC], F32)

    with ExitStack() as OUTER:
        block = OUTER.enter_context(nc.Block())
        SEMQ = {}
        for nm in ("TE", "VE", "SE", "PO", "SY", "GD", "SC", "CC", "G0", "G1", "G2", "G3"):
            SEMQ[nm] = OUTER.enter_context(nc.semaphore("s" + nm))
        CNT = {nm: 0 for nm in SEMQ}
        WAITED = {e: {nm: 0 for nm in SEMQ} for e in
                  ("tensor", "vector", "scalar", "gpsimd", "sync")}
        QSEL = {"tensor": None, "vector": None, "scalar": None, "gpsimd": None,
                "sync": None}

        def _engsel(q):
            return {"tensor": block.tensor, "vector": block.vector,
                    "scalar": block.scalar, "gpsimd": block.gpsimd,
                    "sync": block.sync}[q]

        ENG_SEM = {"tensor": "TE", "vector": "VE", "scalar": "SE",
                   "gpsimd": "PO", "sync": "SY"}

        def step(queue, fn, deps=(), dma=0, sem=None, cc=False):
            mysem = sem or (("GD" if queue == "gpsimd" else
                             "SC" if queue == "scalar" else "SY")
                            if dma else ENG_SEM[queue])
            if cc:
                mysem = "CC"
            w = WAITED[queue]
            dep_list = [(s, v) for (s, v) in deps if v > w[s]]
            for s, v in dep_list:
                w[s] = max(w[s], v)
            if dma:
                CNT[mysem] += 16 * dma
            elif cc:
                CNT[mysem] += 1
            else:
                CNT[mysem] += 1
            after = CNT[mysem]
            handle = SEMQ[mysem]
            inc_each = 16 if dma else 1

            @_engsel(queue)
            def _(eng):
                for s, v in dep_list:
                    eng.wait_ge(SEMQ[s], v)
                if dma:
                    insts = fn(eng)
                    if not isinstance(insts, (list, tuple)):
                        insts = [insts]
                    assert len(insts) == dma, f"dma count {len(insts)} != {dma}"
                    for i_ in insts:
                        i_.then_inc(handle, 16)
                else:
                    last = fn(eng)
                    if last is None:
                        last = eng.nop()
                    last.then_inc(handle, 1)
            return (mysem, after)

        ALLQ = ("TE", "VE", "SE", "PO", "SY", "GD", "SC", "CC", "G0", "G1", "G2", "G3")

        def now():
            return [(s, CNT[s]) for s in ALLQ if CNT[s] > 0]

        def sstep(queue, fn, dma=0, cc=False, exclude=(), extra=()):
            deps = [t for t in now() if t[0] not in exclude] + list(extra)
            return step(queue, fn, deps=deps, dma=dma, cc=cc)

        def vseq(fns):
            for f in fns:
                sstep("vector", f)

        def sbuf(stk, name, shape, dtype):
            return stk.enter_context(nc.sbuf_tensor(name, shape, dtype))

        def psum(stk, name, shape, dtype=F32):
            return stk.enter_context(nc.psum_tensor(name, shape, dtype))

        c_onesbf = sbuf(OUTER, "c_onesbf", [128, 128], BF16)
        c_onesf = sbuf(OUTER, "c_onesf", [128, 128], F32)
        c_lt = sbuf(OUTER, "c_lt", [128, 128], F32)
        c_idf = sbuf(OUTER, "c_idf", [128, 128], F32)
        c_iota64 = sbuf(OUTER, "c_iota64", [128, E], F32)
        c_detie = sbuf(OUTER, "c_detie", [128, E], F32)
        c_iota128 = sbuf(OUTER, "c_iota128", [128, 1], F32)
        c_rowbase = sbuf(OUTER, "c_rowbase", [128, 1], F32)
        c_slotb = sbuf(OUTER, "c_slotb", [128, 1], F32)
        c_g8 = sbuf(OUTER, "c_g8", [128, NCB], F32)
        c_b8 = sbuf(OUTER, "c_b8", [128, NCB], F32)
        c_bo8 = sbuf(OUTER, "c_bo8", [128, NCB], F32)
        c_acol = sbuf(OUTER, "c_acol", [128, 1], F32)
        c_presel = sbuf(OUTER, "c_presel", [16, 2], F32)
        c_ub8 = sbuf(OUTER, "c_ub8", [128, NCB], F32)
        c_eps = sbuf(OUTER, "c_eps", [128, 1], F32)
        c_gb8 = sbuf(OUTER, "c_gb8", [128, NCB], F32)
        p_idxi = sbuf(OUTER, "p_idxi", [128, NTILE * TK], I32)
        p_wts = sbuf(OUTER, "p_wts", [128, NTILE * TK], F32)
        p_zero = sbuf(OUTER, "p_zero", [128, 512], F32)

        def _loads(sy):
            return [sy.dma_start(out=dst[:], in_=src)
                    for dst, src in [(c_onesbf, ones_bf), (c_onesf, ones_f),
                                     (c_lt, lt_strict), (c_idf, ident_f),
                                     (c_iota64, iota64), (c_detie, detie),
                                     (c_iota128, iota128), (c_rowbase, rowbase),
                                     (c_slotb, slotb), (c_g8, g8), (c_b8, b8),
                                     (c_bo8, bo8), (c_acol, acol), (c_presel, presel),
                                     (c_ub8, ub8), (c_gb8, gb8), (c_eps, epsc)]]
        sstep("sync", _loads, dma=17)
        breg = []

        def _breg(g):
            r = g.alloc_register(name="bnd")
            g.reg_mov(r, E * CAP - 1)
            breg.append(r)
            return g.nop()
        sstep("gpsimd", _breg)
        sstep("vector", lambda v: v.memset(p_zero[:], 0.0))
        sstep("sync", lambda sy: sy.dma_start(
            out=meta_in.ap().rearrange("(p r) c -> p (r c)", p=128), in_=p_zero[:, 0:512]),
            dma=1)



        def emit_ln_tok(P, xap, outap, dep_in, out_war=()):
            bf, sq = P["bf"], P["sq"]
            q_ = P["rot"][0]
            P["rot"][0] = 1 - q_
            mu, m2 = P["mu"][q_], P["m2"]
            var, rs, scr = P["var"], P["rs"][q_], P["scr"]
            ps1, ps2 = P["ps1"], P["ps2"]
            t_cast = step("scalar", lambda se: se.activation(out=bf[:], in_=xap, func=AF.Copy),
                          deps=list(dep_in) + P.get("war_bf", []))
            t_sq = step("vector", lambda ve: ve.tensor_tensor(out=sq[:], in0=bf[:],
                                                              in1=bf[:], op=ALU.mult),
                        deps=[t_cast])

            def _mm(te):
                te.ldweights(c_onesbf[:])
                for cb in range(NCB):
                    te.matmul(ps1[:], c_onesbf[:], bf[:, cb * TCH:(cb + 1) * TCH],
                              start=(cb == 0), stop=(cb == NCB - 1))
                for cb in range(NCB):
                    last = te.matmul(ps2[:], c_onesbf[:], sq[:, cb * TCH:(cb + 1) * TCH],
                                     start=(cb == 0), stop=(cb == NCB - 1))
                return last
            t_mm = step("tensor", _mm, deps=[t_sq] + P.get("war_ps", []))

            def _m1(ve):
                ve.tensor_scalar(out=mu[:], in0=ps1[:], scalar1=INV_D, scalar2=None,
                                 op0=ALU.mult)
                ve.tensor_scalar(out=m2[:], in0=ps2[:], scalar1=INV_D, scalar2=None,
                                 op0=ALU.mult)
                ve.tensor_tensor(out=scr[:], in0=mu[:], in1=mu[:], op=ALU.mult)
                return ve.tensor_tensor(out=var[:], in0=m2[:], in1=scr[:], op=ALU.subtract)
            t_m1 = step("vector", _m1, deps=[t_mm] + P["warq"].get(q_, []) +
                        P.get("war_v", []))
            t_sqrt = step("scalar", lambda se: se.activation(out=scr[:], in_=var[:],
                                                             func=AF.Sqrt, bias=c_eps[:]),
                          deps=[t_m1])
            t_rs = step("vector", lambda ve: ve.reciprocal(out=rs[:], in_=scr[:]),
                        deps=[t_sqrt])

            def _apv(ve):
                for cb in range(5):
                    xi = xap[:, cb * TCH:(cb + 1) * TCH]
                    oi = outap[:, cb * TCH:(cb + 1) * TCH]
                    ve.tensor_tensor(out=oi, in0=xi, in1=mu[:], op=ALU.subtract)
                    last = ve.tensor_tensor(out=oi, in0=oi, in1=rs[:], op=ALU.mult)
                return last

            def _app(po):
                for cb in range(5, NCB):
                    xi = xap[:, cb * TCH:(cb + 1) * TCH]
                    oi = outap[:, cb * TCH:(cb + 1) * TCH]
                    po.tensor_tensor(out=oi, in0=xi, in1=mu[:], op=ALU.subtract)
                    last = po.tensor_tensor(out=oi, in0=oi, in1=rs[:], op=ALU.mult)
                return last
            t_apv = step("vector", _apv, deps=[t_rs] + list(out_war))
            t_app = step("gpsimd", _app, deps=[t_rs] + list(out_war))
            P["war_bf"] = [t_mm]
            P["war_ps"] = [t_m1]
            P["war_v"] = [t_rs]
            P["warq"][q_] = [t_apv, t_app]
            return [t_apv, t_app]

        def emit_ln(P, xap, outap):
            bf, sq = P["bf"], P["sq"]
            mu, m2, var, rs, scr = P["mu"], P["m2"], P["var"], P["rs"], P["scr"]
            ps1, ps2 = P["ps1"], P["ps2"]

            def _cast(se):
                se.activation(out=bf[:], in_=xap, func=AF.Copy)
                return se.activation(out=sq[:], in_=bf[:], func=AF.Square)
            sstep("scalar", _cast)

            def _mm(te):
                te.ldweights(c_onesbf[:])
                for cb in range(NCB):
                    te.matmul(ps1[:], c_onesbf[:], bf[:, cb * TCH:(cb + 1) * TCH],
                              start=(cb == 0), stop=(cb == NCB - 1))
                for cb in range(NCB):
                    last = te.matmul(ps2[:], c_onesbf[:], sq[:, cb * TCH:(cb + 1) * TCH],
                                     start=(cb == 0), stop=(cb == NCB - 1))
                return last
            sstep("tensor", _mm)

            def _m1(ve):
                ve.tensor_scalar(out=mu[:], in0=ps1[:], scalar1=INV_D, scalar2=None, op0=ALU.mult)
                ve.tensor_scalar(out=m2[:], in0=ps2[:], scalar1=INV_D, scalar2=None, op0=ALU.mult)
                ve.tensor_tensor(out=scr[:], in0=mu[:], in1=mu[:], op=ALU.mult)
                return ve.tensor_tensor(out=var[:], in0=m2[:], in1=scr[:], op=ALU.subtract)
            sstep("vector", _m1)
            sstep("scalar", lambda se: se.activation(out=scr[:], in_=var[:], func=AF.Sqrt,
                                                        bias=c_eps[:]))

            def _m2(ve):
                last = ve.reciprocal(out=rs[:], in_=scr[:])
                for cb in range(NCB):
                    xi = xap[:, cb * TCH:(cb + 1) * TCH]
                    oi = outap[:, cb * TCH:(cb + 1) * TCH]
                    ve.tensor_tensor(out=oi, in0=xi, in1=mu[:], op=ALU.subtract)
                    last = ve.tensor_tensor(out=oi, in0=oi, in1=rs[:], op=ALU.mult)
                return last
            sstep("vector", _m2)

        def ln_tiles(stk, pfx):
            return dict(
                bf=sbuf(stk, pfx + "_bf", [128, NCB * TCH], BF16),
                sq=sbuf(stk, pfx + "_sq", [128, NCB * TCH], BF16),
                mu=[sbuf(stk, pfx + "_mu%d" % q_, [128, TCH], F32) for q_ in range(2)],
                m2=sbuf(stk, pfx + "_m2", [128, TCH], F32),
                var=sbuf(stk, pfx + "_var", [128, TCH], F32),
                rs=[sbuf(stk, pfx + "_rs%d" % q_, [128, TCH], F32) for q_ in range(2)],
                scr=sbuf(stk, pfx + "_scr", [128, TCH], F32),
                ps1=psum(stk, pfx + "_ps1", [128, TCH]),
                ps2=psum(stk, pfx + "_ps2", [128, TCH]),
                rot=[0],
                warq={},
            )

        xcm_v = xcm.rearrange("(cb p) n -> p cb n", p=128)
        x1_v = x1_dram.ap().rearrange("(cb p) n -> p cb n", p=128)
        x2_v = x2_dram.ap().rearrange("(cb p) n -> p cb n", p=128)
        x3_v = x3cm_dram.ap().rearrange("(cb p) n -> p cb n", p=128)

        with ExitStack() as S:
          if nphases >= 1:
            x0c = [sbuf(S, "p1_x0%d" % p_, [128, NCB * TCH], F32) for p_ in range(2)]
            x1c = [sbuf(S, "p1_x1%d" % p_, [128, NCB * TCH], F32) for p_ in range(2)]
            znc = sbuf(S, "p1_zn", [128, NCB * TCH], F32)
            P = ln_tiles(S, "p1")
            zhalf = sbuf(S, "p1_z", [128, NCB * 4 * TCH], BF16)
            winb = sbuf(S, "p1_win", [128, NCB * DD], BF16)
            evb = [sbuf(S, "p1_ev%d" % p_, [128, 4 * TCH], F32) for p_ in range(2)]
            mmps = psum(S, "p1_mm", [128, 4 * TCH])

            t_win = step("gpsimd", lambda g: g.dma_start(
                out=winb[:].rearrange("p (k m) -> p k m", k=NCB),
                in_=w_in.rearrange("(k p) m -> p k m", p=128)), deps=now(), dma=1)

            ln1_done = {}
            ln2_done = {}
            aff_tok = {}
            ld_tok = {}
            ev_tok = {}
            st_tok = {}
            wm_last = {}
            for i in range(NTCH):
                par = i % 2
                d = list(now()) if i == 0 else []
                if i == 1:
                    d = [ld_tok[0]]
                if i >= 2:
                    d = ln1_done[i - 2] + ln2_done[i - 2]
                ld_tok[i] = step("sync", lambda sy, i=i, par=par: sy.dma_start(
                    out=x0c[par][:].rearrange("p (cb n) -> p cb n", cb=NCB),
                    in_=xcm_v[:, :, i * TCH:(i + 1) * TCH]), deps=d, dma=1)
                ow = ln2_done[i - 2] if i >= 2 else []
                ln1_done[i] = emit_ln_tok(P, x0c[par][:], x1c[par][:],
                                          [ld_tok[i]], out_war=ow)
                st_x1 = step("sync", lambda sy, i=i, par=par: sy.dma_start(
                    out=x1_v[:, :, i * TCH:(i + 1) * TCH],
                    in_=x1c[par][:].rearrange("p (cb n) -> p cb n", cb=NCB)),
                    deps=ln1_done[i], dma=1)
                q_ = 1 - P["rot"][0]
                varT, rsT = P["var"], P["rs"][q_]
                v2 = P["mu"][q_]
                t_v1 = step("vector", lambda ve: ve.tensor_tensor(
                    out=v2[:], in0=varT[:], in1=rsT[:], op=ALU.mult), deps=ln1_done[i])
                t_v2 = step("vector", lambda ve: ve.tensor_tensor(
                    out=v2[:], in0=v2[:], in1=rsT[:], op=ALU.mult), deps=[t_v1])
                t_sq2 = step("scalar", lambda se: se.activation(
                    out=v2[:], in_=v2[:], func=AF.Sqrt, bias=c_eps[:]), deps=[t_v2])
                t_rs2 = step("vector", lambda ve: ve.reciprocal(out=v2[:], in_=v2[:]),
                             deps=[t_sq2])

                def _zn(ve, par=par):
                    for cb in range(NCB):
                        last = ve.tensor_tensor(out=znc[:, cb * TCH:(cb + 1) * TCH],
                                                in0=x1c[par][:, cb * TCH:(cb + 1) * TCH],
                                                in1=v2[:], op=ALU.mult)
                    return last
                ow = [aff_tok[i - 1]] if i >= 1 else []
                t_znc = step("vector", _zn, deps=[t_rs2] + ow)
                ln2_done[i] = [t_znc]
                P["warq"][q_] = [t_znc]

                def _aff(se, i=i):
                    for cb in range(NCB):
                        last = se.activation(
                            out=zhalf[:, cb * 4 * TCH + (i % 4) * TCH: cb * 4 * TCH + (i % 4 + 1) * TCH],
                            in_=znc[:, cb * TCH:(cb + 1) * TCH], func=AF.Identity,
                            scale=c_g8[:, cb:cb + 1], bias=c_b8[:, cb:cb + 1])
                    return last
                d = list(ln2_done[i])
                if i >= 4:
                    d.append(wm_last[i // 4 - 1])
                aff_tok[i] = step("scalar", _aff, deps=d)

                if i % 4 == 3:
                    half = i // 4
                    zdeps = [aff_tok[j] for j in range(half * 4, half * 4 + 4)]
                    for m in range(16):
                        def _wmm(te, m=m):
                            for k in range(NCB):
                                wt = winb[:, k * DD + m * 128: k * DD + (m + 1) * 128]
                                te.ldweights(wt)
                                for c in range(4):
                                    last = te.matmul(
                                        mmps[:, c * TCH:(c + 1) * TCH], wt,
                                        zhalf[:, k * 4 * TCH + c * TCH: k * 4 * TCH + (c + 1) * TCH],
                                        start=(k == 0), stop=(k == NCB - 1))
                            return last
                        d = zdeps + [t_win]
                        if m >= 1:
                            d.append(ev_tok[(half, m - 1)])
                        elif half == 1:
                            d.append(ev_tok[(0, 15)])
                        t_wm = step("tensor", _wmm, deps=d)
                        fn = AF.Identity if m < 8 else AF.Sigmoid
                        bcol = c_ub8[:, m:m + 1] if m < 8 else c_gb8[:, m - 8:m - 7]

                        def _ev(se, fn=fn, bcol=bcol, par=m % 2):
                            for c in range(4):
                                last = se.activation(out=evb[par][:, c * TCH:(c + 1) * TCH],
                                                     in_=mmps[:, c * TCH:(c + 1) * TCH],
                                                     func=fn, bias=bcol)
                            return last
                        d = [t_wm]
                        if m >= 2:
                            d.append(st_tok[(half, m - 2)])
                        ev_tok[(half, m)] = step("scalar", _ev, deps=d)
                        r0 = 256 * m if m < 8 else 256 * (m - 8) + 128
                        st_tok[(half, m)] = step(
                            "sync", lambda sy, r0=r0, half=half, par=m % 2: sy.dma_start(
                                out=ug_a2a_in.ap()[r0:r0 + 128, half * 2048:(half + 1) * 2048],
                                in_=evb[par][:]), deps=[ev_tok[(half, m)]], dma=1)
                    wm_last[half] = t_wm

            def _cc_ug(g):
                with allow_ici_all_to_all():
                    return g.collective_compute(
                        "AllToAll", ALU.bypass, replica_groups=RG,
                        ins=[ug_a2a_in.ap().opt()], outs=[ug_a2a_out.ap().opt()])
            sstep("gpsimd", _cc_ug, cc=True)

        with ExitStack() as S:
          if nphases >= 2:
            u = [sbuf(S, "p2_u%d" % p_, [128, 2 * LCH], F32) for p_ in range(2)]
            g_ = [sbuf(S, "p2_g%d" % p_, [128, 2 * LCH], F32) for p_ in range(2)]
            gp = [sbuf(S, "p2_gp%d" % p_, [128, 2 * LCH], F32) for p_ in range(2)]
            d1 = [sbuf(S, "p2_d1%d" % p_, [128, 2 * LCH], F32) for p_ in range(2)]
            G1 = [sbuf(S, "p2_G1%d" % p_, [128, 2 * LCH], F32) for p_ in range(2)]
            h = sbuf(S, "p2_h", [128, 2 * LCH], F32)
            m_ = sbuf(S, "p2_m", [128, 2 * LCH], F32)
            w_ = sbuf(S, "p2_w", [128, 2 * LCH], F32)
            s_ = sbuf(S, "p2_s", [128, 2 * LCH], F32)
            qq = sbuf(S, "p2_q", [128, 2 * LCH], F32)
            aa = sbuf(S, "p2_a", [128, 2 * LCH], F32)
            rr = sbuf(S, "p2_r", [128, 2 * LCH], F32)
            c1 = sbuf(S, "p2_c1", [128, 2 * LCH], F32)
            c2 = sbuf(S, "p2_c2", [128, 2 * LCH], F32)
            cc = sbuf(S, "p2_cc", [128, 2 * LCH], F32)
            car = [sbuf(S, "p2_car0", [128, 1], F32), sbuf(S, "p2_car1", [128, 1], F32)]

            p2_entry = now()
            t_car = step("vector", lambda ve: (ve.memset(car[0][:], 0.0),
                                               ve.memset(car[1][:], 0.0))[1],
                         deps=p2_entry)
            ld_tok = {}
            free_tok = {}

            def _mk_ld(ci, par):
                j = ci // 2
                lo = (ci % 2) * LCH

                def _ld(sy):
                    out = []
                    for b_i in range(B):
                        out.append(sy.dma_start(
                            out=u[par][:, b_i * LCH:(b_i + 1) * LCH],
                            in_=ug_a2a_out.ap()[256 * j:256 * j + 128,
                                                b_i * 2048 + lo: b_i * 2048 + lo + LCH]))
                        out.append(sy.dma_start(
                            out=g_[par][:, b_i * LCH:(b_i + 1) * LCH],
                            in_=ug_a2a_out.ap()[256 * j + 128:256 * j + 256,
                                                b_i * 2048 + lo: b_i * 2048 + lo + LCH]))
                    return out
                return _ld

            ld_tok[0] = step("sync", _mk_ld(0, 0), deps=p2_entry, dma=4)
            ld_tok[1] = step("sync", _mk_ld(1, 1), deps=p2_entry, dma=4)

            pre_tok = {}
            t_scan = t_car
            t_m_last = None
            for ci in range(NLCH):
                par = ci % 2
                def _pre1(ve, par=par):
                    return ve.tensor_scalar(out=gp[par][:], in0=g_[par][:], scalar1=-1.0,
                                            scalar2=1.0, op0=ALU.mult, op1=ALU.add)
                d = [ld_tok[ci]]
                if ci >= 2:
                    d.append(free_tok[ci - 2])
                t_p1 = step("vector", _pre1, deps=d)
                t_p2 = step("vector", lambda ve, par=par: ve.tensor_scalar(
                    out=d1[par][:], in0=gp[par][:], scalar1=c_acol[:], scalar2=None,
                    op0=ALU.mult), deps=[t_p1])
                t_p3 = step("gpsimd", lambda po, par=par: po.tensor_tensor(
                    out=G1[par][:], in0=g_[par][:], in1=d1[par][:], op=ALU.add),
                    deps=[t_p2])
                pre_tok[ci] = t_p3

                t_ms = step("vector", lambda ve: ve.memset(h[:], 0.0), deps=[t_scan])

                for si, kind in enumerate(("P", "N", "N")):
                    def _m(ve, par=par, first=(si == 0)):
                        for b_i in range(B):
                            c0 = b_i * LCH
                            ve.scalar_tensor_tensor(out=m_[:, c0:c0 + 1], in0=car[b_i][:],
                                                    scalar=c_acol[:],
                                                    in1=u[par][:, c0:c0 + 1],
                                                    op0=ALU.mult, op1=ALU.add)
                            last = ve.scalar_tensor_tensor(
                                out=m_[:, c0 + 1:c0 + LCH], in0=h[:, c0:c0 + LCH - 1],
                                scalar=c_acol[:], in1=u[par][:, c0 + 1:c0 + LCH],
                                op0=ALU.mult, op1=ALU.add)
                        return last
                    d = [ld_tok[ci], t_ms if si == 0 else t_scan]
                    t_m = step("vector", _m, deps=d)
                    t_m_last = t_m
                    t_w = step("scalar", lambda se: se.activation(
                        out=w_[:], in_=m_[:], func=AF.Tanh), deps=[t_m])
                    if kind == "P":
                        t_c1 = step("gpsimd", lambda po, par=par: po.tensor_tensor(
                            out=cc[:], in0=gp[par][:], in1=w_[:], op=ALU.mult),
                            deps=[t_w, pre_tok[ci]])

                        def _scanp(ve, par=par):
                            for b_i in range(B):
                                c0 = b_i * LCH
                                last = ve.tensor_tensor_scan(
                                    out=h[:, c0:c0 + LCH], data0=g_[par][:, c0:c0 + LCH],
                                    data1=cc[:, c0:c0 + LCH], initial=car[b_i][:],
                                    op0=ALU.mult, op1=ALU.add)
                            return last
                        t_scan = step("vector", _scanp, deps=[t_c1])
                    else:
                        t_s = step("vector", lambda ve: ve.tensor_tensor(
                            out=s_[:], in0=w_[:], in1=w_[:], op=ALU.mult), deps=[t_w])
                        t_q = step("gpsimd", lambda po, par=par: po.tensor_tensor(
                            out=qq[:], in0=d1[par][:], in1=s_[:], op=ALU.mult),
                            deps=[t_s, pre_tok[ci]])
                        t_r = step("vector", lambda ve, par=par: ve.tensor_tensor(
                            out=rr[:], in0=d1[par][:], in1=qq[:], op=ALU.subtract),
                            deps=[t_q])
                        t_c1 = step("gpsimd", lambda po, par=par: po.tensor_tensor(
                            out=c1[:], in0=gp[par][:], in1=w_[:], op=ALU.mult),
                            deps=[t_w])

                        def _c2(ve):
                            for b_i in range(B):
                                c0 = b_i * LCH
                                ve.tensor_tensor(out=c2[:, c0:c0 + 1], in0=rr[:, c0:c0 + 1],
                                                 in1=car[b_i][:], op=ALU.mult)
                                last = ve.tensor_tensor(out=c2[:, c0 + 1:c0 + LCH],
                                                        in0=rr[:, c0 + 1:c0 + LCH],
                                                        in1=h[:, c0:c0 + LCH - 1],
                                                        op=ALU.mult)
                            return last
                        t_c2 = step("vector", _c2, deps=[t_r])
                        t_a = step("gpsimd", lambda po, par=par: po.tensor_tensor(
                            out=aa[:], in0=G1[par][:], in1=qq[:], op=ALU.subtract),
                            deps=[t_q])
                        t_c = step("vector", lambda ve: ve.tensor_tensor(
                            out=cc[:], in0=c1[:], in1=c2[:], op=ALU.subtract),
                            deps=[t_c2, t_c1])

                        def _scann(ve):
                            for b_i in range(B):
                                c0 = b_i * LCH
                                last = ve.tensor_tensor_scan(
                                    out=h[:, c0:c0 + LCH], data0=aa[:, c0:c0 + LCH],
                                    data1=cc[:, c0:c0 + LCH], initial=car[b_i][:],
                                    op0=ALU.mult, op1=ALU.add)
                            return last
                        t_scan = step("vector", _scann, deps=[t_a, t_c])

                free_tok[ci] = t_m_last
                free_tok[ci] = t_scan

                def _carry(ve):
                    ve.tensor_copy(car[0][:], h[:, LCH - 1:LCH])
                    return ve.tensor_copy(car[1][:], h[:, 2 * LCH - 1:2 * LCH])
                t_scan = step("vector", _carry, deps=[t_scan])

                def _hsout(g, ci=ci):
                    j = ci // 2
                    lo = (ci % 2) * LCH
                    return [g.dma_start(
                        out=hs_a2a_in.ap()[128 * j:128 * (j + 1),
                                           b_i * 2048 + lo: b_i * 2048 + lo + LCH],
                        in_=h[:, b_i * LCH:(b_i + 1) * LCH]) for b_i in range(B)]
                t_hsdma = step("gpsimd", _hsout, deps=[t_scan], dma=2)
                t_scan = t_scan
                hs_free = t_hsdma
                if ci + 2 < NLCH:
                    ld_tok[ci + 2] = step("sync", _mk_ld(ci + 2, par),
                                          deps=[free_tok[ci]], dma=4)

                t_scan = (t_scan[0], t_scan[1])
                last_hs = t_hsdma
                t_scan_chain = t_scan

                def _noop():
                    pass
                
                if ci < NLCH - 1:
                    t_scan = ("VE", t_scan_chain[1])
                    t_scan = step("vector", lambda ve: ve.nop(), deps=[t_scan_chain, last_hs])

            def _cc_hs(g):
                with allow_ici_all_to_all():
                    return g.collective_compute(
                        "AllToAll", ALU.bypass, replica_groups=RG,
                        ins=[hs_a2a_in.ap().opt()], outs=[hs_a2a_out.ap().opt()])
            sstep("gpsimd", _cc_hs, cc=True)

        with ExitStack() as S:
          if nphases >= 3:
            x3bf = sbuf(S, "p3_x3bf", [128, NCB * TPC], BF16)
            trt_s = sbuf(S, "p3_tr0", [128, D], BF16)
            trt = [trt_s, trt_s]
            rwtb = sbuf(S, "p3_rwt", [128, NCB * E], BF16)
            lgt = sbuf(S, "p3_lg", [128, E], F32)
            mx = sbuf(S, "p3_mx", [128, TK], F32)
            mi_ = sbuf(S, "p3_mi", [128, TK], U32)
            nmx = sbuf(S, "p3_nmx", [128, 1], F32)
            exw = sbuf(S, "p3_ex", [128, TK], F32)
            s8 = sbuf(S, "p3_s8", [128, 1], F32)
            wv = sbuf(S, "p3_wv", [128, 1], F32)
            eid = sbuf(S, "p3_eid", [128, NTILE * TK], F32)
            mkt = sbuf(S, "p3_mk", [128, E], F32)
            tot = sbuf(S, "p3_tot", [128, NTILE * E], F32)
            scr64 = sbuf(S, "p3_scr64", [128, E], F32)
            cnt01 = [sbuf(S, "p3_cnt0", [1, E], F32), sbuf(S, "p3_cnt1", [1, E], F32)]
            SA = S.enter_context(ExitStack())
            ps_lg = psum(SA, "ps_lg", [128, E])
            ps_cnt2 = psum(SA, "ps_cnt2", [128, 2 * E])
            SC = S.enter_context(ExitStack())
            woutb = sbuf(SC, "p3_wout", [128, NCB * D], BF16)
            hsc_s = sbuf(SC, "p3_hs", [128, NCB * TCH], BF16)
            hsc = [hsc_s, hsc_s]
            x0c_s = sbuf(SC, "p3_x0", [128, NCB * TCH], F32)
            x0c = [x0c_s, x0c_s]
            x1c_s = sbuf(SC, "p3_x1", [128, NCB * TCH], F32)
            x1c = [x1c_s, x1c_s]
            arc_s = sbuf(SC, "p3_ar", [128, NCB * TCH], F32)
            arc = [arc_s, arc_s]
            r3c_s = sbuf(SC, "p3_r3", [128, NCB * TCH], F32)
            r3c = [r3c_s, r3c_s]
            P = ln_tiles(SC, "p3")
            mmps = psum(SC, "p3_mm", [128, 4 * TCH])

            trt_free = [[], []]
            x3row_toks = []
            t_wout = step("gpsimd", lambda g: g.dma_start(
                out=woutb[:].rearrange("p (k m) -> p k m", k=NCB),
                in_=w_out.rearrange("(k p) m -> p k m", p=128)), deps=now(), dma=1)
            t_rwt = step("gpsimd", lambda g: g.dma_start(
                out=rwtb[:].rearrange("p (k m) -> p k m", k=NCB),
                in_=rwt.rearrange("(k p) m -> p k m", p=128)), deps=[], dma=1)

            pa = {"vprev": None, "lgt_free": None}

            def emit_passA_tile(t32):
                def _lmm(te, t32=t32):
                    for k in range(NCB):
                        te.ldweights(x3bf[:, k * TPC + t32 * 128: k * TPC + (t32 + 1) * 128])
                        last = te.matmul(ps_lg[:],
                                         x3bf[:, k * TPC + t32 * 128: k * TPC + (t32 + 1) * 128],
                                         rwtb[:, k * E:(k + 1) * E],
                                         start=(k == 0), stop=(k == NCB - 1))
                    return last
                d = [x3c_tok[t32 // 4], t_rwt]
                if pa["lgt_free"] is not None:
                    d.append(pa["lgt_free"])
                t_lmm = step("tensor", _lmm, deps=d)
                d = [t_lmm]
                if pa["vprev"] is not None:
                    d.append(pa["vprev"])
                t_lgt = step("vector", lambda ve: ve.tensor_tensor(
                    out=lgt[:], in0=ps_lg[:], in1=c_detie[:], op=ALU.add), deps=d)
                pa["lgt_free"] = t_lgt

                def _mwi(ve):
                    ve.max_with_indices(out_max=mx[:], out_indices=mi_[:], in_=lgt[:])
                    return ve.nop()
                t_mwi = step("vector", _mwi, deps=[t_lgt])
                t_nmx = step("vector", lambda ve: ve.tensor_scalar(
                    out=nmx[:], in0=mx[:, 0:1], scalar1=-1.0, scalar2=None, op0=ALU.mult),
                    deps=[t_mwi])
                t_exp = step("scalar", lambda se: se.activation(
                    out=exw[:], in_=mx[:], func=AF.Exp, bias=nmx[:]), deps=[t_nmx])
                t_rsm = step("vector", lambda ve: ve.reduce_sum(
                    out=s8[:], in_=exw[:], axis=mybir.AxisListType.X), deps=[t_exp])
                t_rc = step("vector", lambda ve: ve.reciprocal(out=wv[:], in_=s8[:]),
                            deps=[t_rsm])
                t_wtsA = step("vector", lambda ve, t32=t32: ve.tensor_scalar(
                    out=p_wts[:, t32 * TK:(t32 + 1) * TK], in0=exw[:],
                    scalar1=wv[:], scalar2=None, op0=ALU.mult), deps=[t_rc])
                tprev = t_wtsA
                for k in range(TK):
                    t_mk = step("vector", lambda ve, t32=t32, k=k: ve.tensor_scalar(
                        out=mkt[:], in0=lgt[:], scalar1=mx[:, k:k + 1],
                        scalar2=None, op0=ALU.is_equal), deps=[tprev])
                    t_sc = step("vector", lambda ve: ve.tensor_tensor(
                        out=scr64[:], in0=mkt[:], in1=c_iota64[:], op=ALU.mult),
                        deps=[t_mk])
                    tprev = step("vector", lambda ve, t32=t32, k=k: ve.reduce_sum(
                        out=eid[:, t32 * TK + k: t32 * TK + k + 1],
                        in_=scr64[:], axis=mybir.AxisListType.X), deps=[t_sc])
                t_tot = step("vector", lambda ve, t32=t32: ve.tensor_scalar(
                    out=tot[:, t32 * E:(t32 + 1) * E], in0=lgt[:],
                    scalar1=mx[:, TK - 1:TK], scalar2=None, op0=ALU.is_ge), deps=[tprev])
                pa["vprev"] = t_tot

            def emit_cnt_seg(seg):
                def _cmm(te, seg=seg):
                    for t32 in range(seg * 16, seg * 16 + 16):
                        last = te.matmul(ps_cnt2[:, seg * E:(seg + 1) * E], c_onesf[:],
                                         tot[:, t32 * E:(t32 + 1) * E],
                                         start=(t32 % 16 == 0), stop=(t32 % 16 == 15))
                    return last
                return step("tensor", _cmm, deps=[pa["vprev"]])

            hs_v = hs_a2a_out.ap().rearrange("(k p) n -> p k n", p=128)
            hs_tok = {}
            x1_tok = {}
            x0_tok = {}
            wo_ev = {}
            wo_last = {}
            a1_tok = {}
            ln3_done = {}
            ln4_done = {}
            x2st_tok = {}
            x3c_tok = {}
            tok_cc_hs = [t for t in now() if t[0] == "CC"]
            for i in range(NTCH):
                par = i % 2
                d = list(tok_cc_hs) if i < 1 else [wo_last[i - 1]]
                hs_tok[i] = step("sync", lambda sy, i=i, par=par: sy.dma_start(
                    out=hsc[par][:].rearrange("p (k n) -> p k n", k=NCB),
                    in_=hs_v[:, :, i * TCH:(i + 1) * TCH]), deps=d, dma=1)
                d = [] if i < 1 else list(a1_tok[i - 1])
                x1_tok[i] = step("sync", lambda sy, i=i, par=par: sy.dma_start(
                    out=x1c[par][:].rearrange("p (cb n) -> p cb n", cb=NCB),
                    in_=x1_v[:, :, i * TCH:(i + 1) * TCH]), deps=d, dma=1)
                d = [] if i < 1 else [x2st_tok[i - 1]] + ln4_done[i - 1]
                x0_tok[i] = step("sync", lambda sy, i=i, par=par: sy.dma_start(
                    out=x0c[par][:].rearrange("p (cb n) -> p cb n", cb=NCB),
                    in_=xcm_v[:, :, i * TCH:(i + 1) * TCH]), deps=d, dma=1)
                for grp in range(2):
                    def _wo(te, grp=grp, par=par):
                        for mi in range(4):
                            m = grp * 4 + mi
                            for k in range(NCB):
                                wt = woutb[:, k * D + m * 128: k * D + (m + 1) * 128]
                                te.ldweights(wt)
                                last = te.matmul(
                                    mmps[:, mi * TCH:(mi + 1) * TCH], wt,
                                    hsc[par][:, k * TCH:(k + 1) * TCH],
                                    start=(k == 0), stop=(k == NCB - 1))
                        return last
                    d = [hs_tok[i], t_wout]
                    if grp == 1:
                        d.append(wo_ev[(i, 0)])
                    elif i >= 1:
                        d.append(wo_ev[(i - 1, 1)])
                    t_wo = step("tensor", _wo, deps=d)
                    wo_last[i] = t_wo

                    def _ev(se, grp=grp, par=par):
                        for mi in range(4):
                            m = grp * 4 + mi
                            last = se.activation(out=arc[par][:, m * TCH:(m + 1) * TCH],
                                                 in_=mmps[:, mi * TCH:(mi + 1) * TCH],
                                                 func=AF.Identity, bias=c_bo8[:, m:m + 1])
                        return last
                    d = [t_wo]
                    if i >= 1:
                        d += ln3_done[i - 1]
                    wo_ev[(i, grp)] = step("scalar", _ev, deps=d)

                def _addx1(ve, par=par):
                    for cb in range(NCB):
                        last = ve.tensor_tensor(out=arc[par][:, cb * TCH:(cb + 1) * TCH],
                                                in0=arc[par][:, cb * TCH:(cb + 1) * TCH],
                                                in1=x1c[par][:, cb * TCH:(cb + 1) * TCH],
                                                op=ALU.add)
                    return last
                a1 = step("vector", _addx1, deps=[wo_ev[(i, 0)], wo_ev[(i, 1)], x1_tok[i]])
                a1_tok[i] = [a1]
                ow = [] if i < 1 else [x3c_tok[i - 1]]
                ln3_done[i] = emit_ln_tok(P, arc[par][:], r3c[par][:],
                                          [a1], out_war=ow)

                def _addx0(ve, par=par):
                    for cb in range(NCB):
                        last = ve.tensor_tensor(out=x0c[par][:, cb * TCH:(cb + 1) * TCH],
                                                in0=x0c[par][:, cb * TCH:(cb + 1) * TCH],
                                                in1=r3c[par][:, cb * TCH:(cb + 1) * TCH],
                                                op=ALU.add)
                    return last
                a0 = step("vector", _addx0, deps=ln3_done[i] + [x0_tok[i]])
                x2st_tok[i] = step("sync", lambda sy, i=i, par=par: sy.dma_start(
                    out=x2_v[:, :, i * TCH:(i + 1) * TCH],
                    in_=x0c[par][:].rearrange("p (cb n) -> p cb n", cb=NCB)),
                    deps=[a0], dma=1)
                ln4_done[i] = emit_ln_tok(P, x0c[par][:], r3c[par][:],
                                          [a0], out_war=ln3_done[i])

                def _x3cast(se, i=i, par=par):
                    for cb in range(NCB):
                        last = se.activation(
                            out=x3bf[:, cb * TPC + i * TCH: cb * TPC + (i + 1) * TCH],
                            in_=r3c[par][:, cb * TCH:(cb + 1) * TCH], func=AF.Copy)
                    return last
                x3c_tok[i] = step("scalar", _x3cast, deps=ln4_done[i])

                def _x3st(sy, i=i):
                    return sy.dma_start(
                        out=x3_v[:, :, i * TCH:(i + 1) * TCH],
                        in_=x3bf[:].rearrange("p (cb n) -> p cb n", cb=NCB)[:, :, i * TCH:(i + 1) * TCH])
                t_x3st = step("sync", _x3st, deps=[x3c_tok[i]], dma=1)
                for blk in range(i * 4, (i + 1) * 4):
                    tr_t = step("sync", lambda sy, blk=blk: sy.dma_start(
                        out=trt[0][:], in_=x3cm_dram.ap()[:, blk * 128:(blk + 1) * 128],
                        transpose=True), deps=[t_x3st] + trt_free[0], dma=1)
                    st_t = step("sync", lambda sy, blk=blk: sy.dma_start(
                        out=x3row_dram.ap()[blk * 128:(blk + 1) * 128, :], in_=trt[0][:]),
                        deps=[tr_t], dma=1)
                    trt_free[0] = [st_t]
                    x3row_toks.append(st_t)

                if i >= 2:
                    for t32 in range(4 * (i - 2), 4 * (i - 1)):
                        emit_passA_tile(t32)
                    if 4 * (i - 1) == 16:
                        emit_cnt_seg(0)

            for t32 in range(4 * (NTCH - 2), NTILE):
                emit_passA_tile(t32)
            emit_cnt_seg(1)

            SC.close()

            def _cntev(se):
                se.activation(out=cnt01[0][:], in_=ps_cnt2[0:1, 0:E], func=AF.Copy)
                return se.activation(out=cnt01[1][:], in_=ps_cnt2[0:1, E:2 * E], func=AF.Copy)
            sstep("scalar", _cntev, exclude=("CC",))
            passA_done = now()
            SA.close()

            def _cntst(sy):
                return [sy.dma_start(out=cnt_in.ap()[0:1, :], in_=cnt01[0][:]),
                        sy.dma_start(out=cnt_in.ap()[1:2, :], in_=cnt01[1][:])]
            sstep("sync", _cntst, dma=2)
            tok_cnt_ag = sstep("gpsimd", lambda g: g.collective_compute(
                "AllGather", ALU.bypass, replica_groups=RG,
                ins=[cnt_in.ap().opt()], outs=[cnt_out.ap().opt()]), cc=True)
            tok_x3rep = step("gpsimd", lambda g: g.collective_compute(
                "AllGather", ALU.bypass, replica_groups=RG,
                ins=[x3row_dram.ap().opt()], outs=[x3rep.ap().opt()]),
                deps=x3row_toks + [tok_cnt_ag], cc=True)

            _sstep_real = sstep

            def sstep(queue, fn, dma=0, cc=False, exclude=(), extra=()):
                return _sstep_real(queue, fn, dma=dma, cc=cc,
                                   exclude=tuple(set(exclude) | {"CC"}),
                                   extra=list(extra) + [tok_cnt_ag])
            cnta = sbuf(S, "p3_cnta", [16, E], F32)
            pre01 = [sbuf(S, "p3_pre0", [1, E], F32), sbuf(S, "p3_pre1", [1, E], F32)]
            pfull = sbuf(S, "p3_pfull", [128, E], F32)
            sr = sbuf(S, "p3_sr", [128, E], F32)
            psk = sbuf(S, "p3_psk", [128, E], F32)
            pos = sbuf(S, "p3_pos", [128, TK], F32)
            slotf = sbuf(S, "p3_slotf", [128, TK], F32)
            vmk = sbuf(S, "p3_vm", [128, TK], F32)
            idxf = sbuf(S, "p3_idxf", [128, TK], F32)
            rowid = sbuf(S, "p3_rowid", [128, 1], F32)
            payA = sbuf(S, "p3_payA", [128, NTILE * 2 * TK], F32)
            e1s = sbuf(S, "p3_e1s", [128, E], F32)
            ps_tp = psum(S, "ps_tp", [128, E])
            ps_run = [psum(S, "ps_run0", [128, E]), psum(S, "ps_run1", [128, E])]
            ps_pre = psum(S, "ps_pre", [1, E])

            t_cnta = sstep("sync", lambda sy: sy.dma_start(out=cnta[:], in_=cnt_out.ap()),
                           dma=1, exclude=("CC",), extra=[tok_cnt_ag])
            pre_ev = {}
            t_pev = None
            for b_i in range(B):
                def _pmm(te, b_i=b_i):
                    return te.matmul(ps_pre[:], c_presel[:, b_i:b_i + 1], cnta[:],
                                     start=True, stop=True)
                d = [t_cnta] + ([t_pev] if t_pev else [])
                t_pmm = step("tensor", _pmm, deps=d)
                t_pev = step("scalar", lambda se, b_i=b_i: se.activation(
                    out=pre01[b_i][:], in_=ps_pre[:], func=AF.Copy), deps=[t_pmm])
                pre_ev[b_i] = t_pev

            vtok = [sstep("vector", lambda ve: ve.nop(), exclude=("CC",),
                          extra=[tok_cnt_ag])]
            e1_tok = {}
            pf_tok = {}
            acc_tok = {}
            base_done = {}
            sc_glist = []
            for t32 in range(NTILE):
                seg = t32 // 16
                if t32 % 16 == 0:
                    def _base(te, seg=seg):
                        return te.matmul(ps_run[seg][:], c_onesf[0:1, :], pre01[seg][:],
                                         start=True, stop=False)
                    d = [pre_ev[seg]]
                    base_done[seg] = step("tensor", _base, deps=d)

                def _tp(te, t32=t32):
                    return te.matmul(ps_tp[:], c_lt[:], tot[:, t32 * E:(t32 + 1) * E],
                                     start=True, stop=True)
                d = [base_done[seg]]
                if t32 >= 1:
                    d.append(e1_tok[t32 - 1])
                t_tp = step("tensor", _tp, deps=d)
                d = [t_tp]
                if t32 >= 1:
                    d.append(pf_tok[t32 - 1])
                e1_tok[t32] = step("scalar", lambda se: se.activation(
                    out=e1s[:], in_=ps_tp[:], func=AF.Copy), deps=d)

                def _pf(ve, seg=seg):
                    ve.tensor_tensor(out=pfull[:], in0=e1s[:], in1=ps_run[seg][:],
                                     op=ALU.add)
                    return ve.memset(sr[:], 0.0)
                d = [e1_tok[t32], vtok[-1]]
                if t32 % 16 != 0:
                    d.append(acc_tok[t32 - 1])
                pf_tok[t32] = step("vector", _pf, deps=d)
                vtok.append(pf_tok[t32])

                def _acc(te, t32=t32, seg=seg):
                    return te.matmul(ps_run[seg][:], c_onesf[:], tot[:, t32 * E:(t32 + 1) * E],
                                     start=False, stop=(t32 % 16 == 15))
                acc_tok[t32] = step("tensor", _acc, deps=[pf_tok[t32]])

                for k in range(TK):
                    def _s1(ve, t32=t32, k=k):
                        if k > 0:
                            ve.tensor_tensor(out=sr[:], in0=sr[:], in1=mkt[:], op=ALU.add)
                        return ve.tensor_scalar(out=mkt[:], in0=c_iota64[:],
                                                scalar1=eid[:, t32 * TK + k: t32 * TK + k + 1],
                                                scalar2=None, op0=ALU.is_equal)
                    t1 = step("vector", _s1, deps=[vtok[-1]])

                    def _s1b(ve):
                        ve.tensor_tensor(out=psk[:], in0=pfull[:], in1=sr[:], op=ALU.add)
                        return ve.tensor_tensor(out=scr64[:], in0=mkt[:], in1=psk[:],
                                                op=ALU.mult)
                    t1b = step("vector", _s1b, deps=[t1])
                    t2 = step("vector", lambda ve, k=k: ve.reduce_sum(
                        out=pos[:, k:k + 1], in_=scr64[:], axis=mybir.AxisListType.X),
                        deps=[t1b])
                    vtok.append(t2)

                t3a = step("vector", lambda ve, t32=t32: ve.scalar_tensor_tensor(
                    out=slotf[:], in0=eid[:, t32 * TK:(t32 + 1) * TK],
                    scalar=float(CAP), in1=pos[:], op0=ALU.mult, op1=ALU.add),
                    deps=[vtok[-1]])
                t3b = step("vector", lambda ve: ve.tensor_scalar(
                    out=vmk[:], in0=pos[:], scalar1=float(CAP), scalar2=None,
                    op0=ALU.is_ge), deps=[t3a])
                t3 = step("vector", lambda ve: ve.scalar_tensor_tensor(
                    out=idxf[:], in0=vmk[:], scalar=BIGIDX, in1=slotf[:],
                    op0=ALU.mult, op1=ALU.add), deps=[t3b])
                t4 = step("vector", lambda ve, t32=t32: ve.tensor_copy(
                    p_idxi[:, t32 * TK:(t32 + 1) * TK], idxf[:]), deps=[t3])

                t5a = step("vector", lambda ve, t32=t32: ve.tensor_scalar(
                    out=rowid[:], in0=c_iota128[:], scalar1=c_rowbase[:],
                    scalar2=float(t32 * 128 + 1), op0=ALU.add, op1=ALU.add), deps=[t4])
                pview = payA[:].rearrange("p (t k c) -> p t k c", t=NTILE, k=TK)

                def _s3(ve, t32=t32):
                    ve.tensor_scalar(
                        out=pview[:, t32, :, 0], in0=p_wts[:, t32 * TK:(t32 + 1) * TK],
                        scalar1=0.0, scalar2=rowid[:], op0=ALU.mult, op1=ALU.add)
                    return ve.tensor_copy(pview[:, t32, :, 1],
                                          p_wts[:, t32 * TK:(t32 + 1) * TK])
                t5 = step("vector", _s3, deps=[t5a])
                vtok.append(t5)

                def _sc8(g, t32=t32):
                    return [g.indirect_dma_start(
                        out=meta_in.ap(),
                        out_offset=IndirectOffsetOnAxis(
                            ap=p_idxi[:, t32 * TK + k: t32 * TK + k + 1], axis=0),
                        in_=payA[:, (t32 * TK + k) * 2:(t32 * TK + k) * 2 + 2],
                        in_offset=None,
                        bounds_check=breg[0], oob_is_err=False) for k in range(TK)]
                d = [t5]
                if len(sc_glist) >= 1:
                    d.append(sc_glist[-1])
                sc_glist.append(step("gpsimd", _sc8, deps=d, dma=TK))

            sstep = _sstep_real
            tok_meta = sstep("gpsimd", lambda g: g.collective_compute(
                "AllReduce", ALU.add, replica_groups=RG,
                ins=[meta_in.ap().opt()], outs=[meta_out.ap().opt()]), cc=True)

            if nphases >= 4:
                wgub = sbuf(S, "p4_wgu", [128, NCB * 2 * SF], BF16)
                wsdb = sbuf(S, "p4_wsd", [128, 6 * D], BF16)
                hb = sbuf(S, "p4_h", [128, 6 * TPC], BF16)
                shc = sbuf(S, "p4_sh", [128, 2 * TCH], F32)
                hgs4 = sbuf(S, "p4_hgs", [128, TCH], BF16)
                ps_g = psum(S, "p4_psg", [128, TCH])
                ps_u = psum(S, "p4_psu", [128, TCH])
                ps_d = psum(S, "p4_psd", [128, 2 * TCH])

                t_wgu = step("gpsimd", lambda g: g.dma_start(
                    out=wgub[:].rearrange("p (k m) -> p k m", k=NCB),
                    in_=wsgu.rearrange("(k p) m -> p k m", p=128)), deps=[], dma=1)
                t_wsd = step("gpsimd", lambda g: g.dma_start(
                    out=wsdb[:].rearrange("p (k m) -> p k m", k=6),
                    in_=wsd.rearrange("(k p) m -> p k m", p=128)), deps=[], dma=1)

                t_se = None
                t_ve = None
                t_te = None
                for m in range(6):
                    for c in range(NTCH):
                        def _gu(te, m=m, c=c):
                            for k in range(NCB):
                                wt = wgub[:, k * 2 * SF + m * 128: k * 2 * SF + (m + 1) * 128]
                                te.ldweights(wt)
                                te.matmul(ps_g[:], wt,
                                          x3bf[:, k * TPC + c * TCH: k * TPC + (c + 1) * TCH],
                                          start=(k == 0), stop=(k == NCB - 1))
                            for k in range(NCB):
                                wt = wgub[:, k * 2 * SF + (m + 6) * 128: k * 2 * SF + (m + 7) * 128]
                                te.ldweights(wt)
                                last = te.matmul(
                                    ps_u[:], wt,
                                    x3bf[:, k * TPC + c * TCH: k * TPC + (c + 1) * TCH],
                                    start=(k == 0), stop=(k == NCB - 1))
                            return last
                        d = [t_wgu] + passA_done
                        if t_se is not None:
                            d += [t_se, t_ve]
                        t_te = step("tensor", _gu, deps=d)
                        t_se = step("scalar", lambda se: se.activation(
                            out=hgs4[:], in_=ps_g[:], func=AF.Silu), deps=[t_te])
                        t_ve = step("vector", lambda ve, m=m, c=c: ve.tensor_tensor(
                            out=hb[:, m * TPC + c * TCH: m * TPC + (c + 1) * TCH],
                            in0=hgs4[:], in1=ps_u[:], op=ALU.mult), deps=[t_te, t_se])

                sh_v = shared_dram.ap().rearrange("(cb p) n -> p cb n", p=128)
                t_hdone = t_ve
                t_dev = None
                t_dst = None
                p4_sh_toks = []
                for m in range(NCB):
                    for quad in range(4):
                        def _dn(te, m=m, quad=quad):
                            for k in range(6):
                                wt = wsdb[:, k * D + m * 128: k * D + (m + 1) * 128]
                                te.ldweights(wt)
                                for c in range(2):
                                    cg = quad * 2 + c
                                    last = te.matmul(
                                        ps_d[:, c * TCH:(c + 1) * TCH], wt,
                                        hb[:, k * TPC + cg * TCH: k * TPC + (cg + 1) * TCH],
                                        start=(k == 0), stop=(k == 5))
                            return last
                        d = [t_wsd, t_hdone]
                        if t_dev is not None:
                            d.append(t_dev)
                        t_te = step("tensor", _dn, deps=d)

                        def _dev(se):
                            for c in range(2):
                                last = se.activation(out=shc[:, c * TCH:(c + 1) * TCH],
                                                     in_=ps_d[:, c * TCH:(c + 1) * TCH],
                                                     func=AF.Copy)
                            return last
                        d = [t_te]
                        if t_dst is not None:
                            d.append(t_dst)
                        t_dev = step("scalar", _dev, deps=d)
                        t_dst = step("sync", lambda sy, m=m, quad=quad: sy.dma_start(
                            out=sh_v[:, m, quad * 1024:(quad + 1) * 1024], in_=shc[:]),
                            deps=[t_dev], dma=1)
                        p4_sh_toks.append(t_dst)
                p4_done = now()

        with ExitStack() as S:
          if nphases >= 5:
            meta_g = [sbuf(S, "p5_meta%d" % i_, [128, 2], F32) for i_ in range(32)]
            gidx = [sbuf(S, "p5_gidx%d" % p_, [128, 4], I32) for p_ in range(2)]
            scr1 = sbuf(S, "p5_scr1", [128, 1], F32)
            scr2 = sbuf(S, "p5_scr2", [128, 1], F32)
            xbs = [sbuf(S, "p5_xbs%d" % i_, [128, D], BF16) for i_ in range(8)]
            xbt = [sbuf(S, "p5_xbt%d" % p_, [128, NCB * CAP], BF16) for p_ in range(2)]
            wgb = [sbuf(S, "p5_wg%d" % p_, [128, NCB * FF], BF16) for p_ in range(2)]
            wub = [sbuf(S, "p5_wu%d" % p_, [128, NCB * FF], BF16) for p_ in range(2)]
            wdb = [sbuf(S, "p5_wd%d" % p_, [128, 6 * D], BF16) for p_ in range(2)]
            hgs = sbuf(S, "p5_hgs", [128, CAP], BF16)
            hbe = sbuf(S, "p5_h", [128, 6 * CAP], BF16)
            ycmb = [sbuf(S, "p5_ycm%d" % p_, [128, NCB * CAP], BF16) for p_ in range(2)]
            yt = [sbuf(S, "p5_yt%d" % p_, [128, D], BF16) for p_ in range(2)]
            ys = [sbuf(S, "p5_ys%d" % p_, [128, D], BF16) for p_ in range(2)]
            ps_a = psum(S, "p5_psa", [128, CAP])
            ps_b = psum(S, "p5_psb", [128, CAP])
            ps_c = psum(S, "p5_psc", [128, CAP])

            p5_entry = now()
            xbst_tok = {}
            meta_tok = {}
            vprev = None
            gx_prev = {}
            for e in range(EPC):
                tprev = None
                for g_i in range(4):
                    if e == 0 and g_i == 0:
                        d = list(p5_entry)
                    else:
                        d = [tprev or vprev]
                    if g_i == 0 and e >= 2:
                        d.append(gx_prev[e - 2])
                    t1 = step("vector", lambda ve, e=e, g_i=g_i: ve.tensor_scalar(
                        out=scr1[:], in0=c_iota128[:], scalar1=c_slotb[:],
                        scalar2=float(e * CAP + g_i * 128), op0=ALU.add, op1=ALU.add),
                        deps=d)
                    tprev = step("vector", lambda ve, e=e, g_i=g_i: ve.tensor_copy(
                        gidx[e % 2][:, g_i:g_i + 1], scr1[:]), deps=[t1])
                d = [tprev, tok_meta]
                if e >= 1:
                    d.append(gx_prev[e - 1])

                def _gmeta(g, e=e):
                    return [g.indirect_dma_start(
                        out=meta_g[e * 4 + g_i][:], out_offset=None,
                        in_=meta_out.ap(),
                        in_offset=IndirectOffsetOnAxis(ap=gidx[e % 2][:, g_i:g_i + 1], axis=0),
                        bounds_check=breg[0], oob_is_err=False) for g_i in range(4)]
                t_gmeta = step("gpsimd", _gmeta, deps=d, dma=4)

                st_toks5 = []
                tprev = t_gmeta
                for g_i in range(4):
                    t1 = step("vector", lambda ve, e=e, g_i=g_i: ve.tensor_scalar(
                        out=scr1[:], in0=meta_g[e * 4 + g_i][:, 0:1], scalar1=0.0,
                        scalar2=None, op0=ALU.is_equal), deps=[tprev])
                    t2 = step("vector", lambda ve, e=e, g_i=g_i: ve.scalar_tensor_tensor(
                        out=scr2[:], in0=scr1[:], scalar=BIGIDX,
                        in1=meta_g[e * 4 + g_i][:, 0:1], op0=ALU.mult, op1=ALU.add),
                        deps=[t1])
                    t3 = step("vector", lambda ve: ve.tensor_scalar(
                        out=scr2[:], in0=scr2[:], scalar1=1.0, scalar2=None,
                        op0=ALU.subtract), deps=[t2])
                    tprev = step("vector", lambda ve, e=e, g_i=g_i: ve.tensor_copy(
                        gidx[e % 2][:, g_i:g_i + 1], scr2[:]), deps=[t3])
                    bi = (e * 4 + g_i) % 8
                    d = [tprev, tok_x3rep]
                    if (e * 4 + g_i) >= 4:
                        d.append(xbst_tok[e * 4 + g_i - 4])
                    t_gx = step("gpsimd", lambda g, e=e, g_i=g_i, bi=bi: g.indirect_dma_start(
                        out=xbs[bi][:], out_offset=None,
                        in_=x3rep.ap(),
                        in_offset=IndirectOffsetOnAxis(ap=gidx[e % 2][:, g_i:g_i + 1], axis=0),
                        bounds_check=breg[0], oob_is_err=False), deps=d, dma=1)
                    gx_prev[e] = t_gx
                    t_st = step("sync", lambda sy, e=e, g_i=g_i, bi=bi: sy.dma_start(
                        out=xb_dram.ap()[e * CAP + g_i * 128: e * CAP + (g_i + 1) * 128, :],
                        in_=xbs[bi][:]), deps=[t_gx], dma=1)
                    xbst_tok[e * 4 + g_i] = t_st
                    st_toks5.append(t_st)
                meta_tok[e] = st_toks5
                vprev = tprev

            def _mk_wld(e, par):
                def _wld(g):
                    return [
                        g.dma_start(out=wgb[par][:].rearrange("p (k m) -> p k m", k=NCB),
                                    in_=wg_all[e * D:(e + 1) * D, :]
                                    .rearrange("(k p) m -> p k m", p=128)),
                        g.dma_start(out=wub[par][:].rearrange("p (k m) -> p k m", k=NCB),
                                    in_=wu_all[e * D:(e + 1) * D, :]
                                    .rearrange("(k p) m -> p k m", p=128)),
                        g.dma_start(out=wdb[par][:].rearrange("p (k m) -> p k m", k=6),
                                    in_=wd_all[e * FF:(e + 1) * FF, :]
                                    .rearrange("(k p) m -> p k m", p=128))]
                return _wld
            tok_wld = {}
            tok_free_w = {}
            tok_xbt = {}
            tok_ys = {}
            tok_ybst = {}
            slot_toks = []
            t_se = None
            t_ve = None
            t_ev = None
            tok_ytr_all = {}
            for e in (0, 1):
                tok_wld[e] = step("gpsimd", _mk_wld(e, e % 2), deps=p5_entry, dma=3)
            for e in range(EPC):
                par = e % 2
                d = list(meta_tok[e])
                if e >= 2:
                    d.append(tok_free_w[e - 2])

                def _xbt(sy, e=e, par=par):
                    return [sy.dma_start(out=xbt[par][:, cb * CAP:(cb + 1) * CAP],
                                         in_=xb_dram.ap()[e * CAP:(e + 1) * CAP,
                                                          cb * 128:(cb + 1) * 128],
                                         transpose=True) for cb in range(NCB)]
                tok_xbt[e] = step("sync", _xbt, deps=d, dma=NCB)

                for m in range(6):
                    def _ge(te, m=m, par=par):
                        for k in range(NCB):
                            wt = wgb[par][:, k * FF + m * 128: k * FF + (m + 1) * 128]
                            te.ldweights(wt)
                            te.matmul(ps_a[:], wt, xbt[par][:, k * CAP:(k + 1) * CAP],
                                      start=(k == 0), stop=(k == NCB - 1))
                        for k in range(NCB):
                            wt = wub[par][:, k * FF + m * 128: k * FF + (m + 1) * 128]
                            te.ldweights(wt)
                            last = te.matmul(ps_b[:], wt, xbt[par][:, k * CAP:(k + 1) * CAP],
                                             start=(k == 0), stop=(k == NCB - 1))
                        return last
                    d = [tok_xbt[e], tok_wld[e]]
                    if t_se is not None:
                        d += [t_se, t_ve]
                    t_te = step("tensor", _ge, deps=d)
                    t_se = step("scalar", lambda se: se.activation(
                        out=hgs[:], in_=ps_a[:], func=AF.Silu), deps=[t_te])
                    t_ve = step("vector", lambda ve, m=m: ve.tensor_tensor(
                        out=hbe[:, m * CAP:(m + 1) * CAP], in0=hgs[:], in1=ps_b[:],
                        op=ALU.mult), deps=[t_te, t_se])

                t_hdone = t_ve
                first_ev = True
                for m in range(NCB):
                    def _de(te, m=m, par=par):
                        for k in range(6):
                            wt = wdb[par][:, k * D + m * 128: k * D + (m + 1) * 128]
                            te.ldweights(wt)
                            last = te.matmul(ps_c[:], wt, hbe[:, k * CAP:(k + 1) * CAP],
                                             start=(k == 0), stop=(k == 5))
                        return last
                    d = [t_hdone]
                    if t_ev is not None:
                        d.append(t_ev)
                    t_te = step("tensor", _de, deps=d)
                    dev = [t_te]
                    if first_ev and e >= 2:
                        dev.append(tok_ybst[e - 2])
                        first_ev = False
                    t_ev = step("scalar", lambda se, m=m, par=par: se.activation(
                        out=ycmb[par][:, m * CAP:(m + 1) * CAP], in_=ps_c[:], func=AF.Copy),
                        deps=dev)
                tok_free_w[e] = t_te
                if e + 2 < EPC:
                    tok_wld[e + 2] = step("gpsimd", _mk_wld(e + 2, par), deps=[t_te], dma=3)

                d = [t_ev]
                if e >= 2 and e - 2 in tok_ybst:
                    d.append(tok_ytr_all[e - 2])
                t_ybst = step("sync", lambda sy, par=par: sy.dma_start(
                    out=yb_dram.ap().rearrange("(q m p) c -> q p m c", q=2, p=128)[par],
                    in_=ycmb[par][:].rearrange("p (m c) -> p m c", m=NCB)), deps=d, dma=1)
                tok_ybst[e] = t_ybst
                t_last = None
                for g_i in range(4):
                    yp = g_i % 2
                    d = [t_ybst]
                    if g_i >= 2:
                        d.append(tok_ysl[g_i - 2])
                    elif e >= 1:
                        d.append(tok_ys[e - 1])
                    t_tr = step("sync", lambda sy, g_i=g_i, par=par, yp=yp: sy.dma_start(
                        out=yt[yp][:], in_=yb_dram.ap()[par * D:(par + 1) * D,
                                                        g_i * 128:(g_i + 1) * 128],
                        transpose=True), deps=d, dma=1)
                    t_w = step("vector", lambda ve, e=e, g_i=g_i, yp=yp: ve.tensor_scalar(
                        out=ys[yp][:], in0=yt[yp][:], scalar1=meta_g[e * 4 + g_i][:, 1:2],
                        scalar2=None, op0=ALU.mult), deps=[t_tr])
                    t_sl = step("sync", lambda sy, e=e, g_i=g_i, yp=yp: sy.dma_start(
                        out=slot_out.ap()[e * CAP + g_i * 128: e * CAP + (g_i + 1) * 128, :],
                        in_=ys[yp][:]), deps=[t_w], dma=1)
                    if g_i == 0:
                        tok_ysl = {}
                    tok_ysl[g_i] = t_sl
                    t_last = t_sl
                    slot_toks.append(t_sl)
                tok_ytr_all[e] = t_last
                tok_ys[e] = t_last

            tok_slot_ag = step("gpsimd", lambda g: g.collective_compute(
                "AllGather", ALU.bypass, replica_groups=RG,
                ins=[slot_out.ap().opt()], outs=[slot_rep.ap().opt()]),
                deps=slot_toks[-1:] + [slot_toks[-5]], cc=True)

        with ExitStack() as S:
          if nphases >= 6:
            gbuf = [sbuf(S, "p6_gb%d" % p_, [128, TK * D], BF16) for p_ in range(2)]
            p6r = [sbuf(S, "p6_r%d" % p_, [128, D], F32) for p_ in range(2)]
            sA_s = sbuf(S, "p6_sA", [128, D], F32)
            sB_s = sbuf(S, "p6_sB", [128, D], F32)
            sA = [sA_s, sA_s]
            sB = [sB_s, sB_s]
            p6rc = [sbuf(S, "p6_rc%d" % p_, [128, D], F32) for p_ in range(2)]
            ps_t = [psum(S, "p6_pst%d" % p_, [128, 4 * 128]) for p_ in range(2)]
            rview = routed_dram.ap().rearrange("(a p) c -> a p c", p=128)
            rcm_v = routedcm_dram.ap().rearrange("(cb p) n -> p cb n", p=128)
            fc = [sbuf(S, "p6_f%d" % p_, [128, NCB * TCH], F32) for p_ in range(2)]
            sc_s = sbuf(S, "p6_s", [128, NCB * TCH], F32)
            sc_ = [sc_s, sc_s]
            x2c_s = sbuf(S, "p6_x2", [128, NCB * TCH], F32)
            x2c = [x2c_s, x2c_s]
            r5c = sbuf(S, "p6_r5", [128, NCB * TCH], F32)
            P = ln_tiles(S, "p6")
            y_v = ycm.rearrange("(cb p) n -> p cb n", p=128)
            sh_v = shared_dram.ap().rearrange("(cb p) n -> p cb n", p=128)
            f_tok = {}
            ya_tok = {}
            yst_tok = {}
            ln5_tok = {}

            def emit_final_chunk(i):
                par = i % 2
                d = [st_toks[4 * i + jj] for jj in range(4)]
                if i >= 2:
                    d += ln5_tok[i - 2]
                tf_ = step("sync", lambda sy, i=i, par=par: sy.dma_start(
                    out=fc[par][:].rearrange("p (cb n) -> p cb n", cb=NCB),
                    in_=rcm_v[:, :, i * TCH:(i + 1) * TCH]), deps=d, dma=1)
                d = [p4_sh_toks[-1]]
                if i >= 1:
                    d.append(f_tok[i - 1])
                tss_ = step("sync", lambda sy, i=i, par=par: sy.dma_start(
                    out=sc_[par][:].rearrange("p (cb n) -> p cb n", cb=NCB),
                    in_=sh_v[:, :, i * TCH:(i + 1) * TCH]), deps=d, dma=1)
                d = [] if i < 1 else [yst_tok[i - 1]]
                tx2_ = step("sync", lambda sy, i=i, par=par: sy.dma_start(
                    out=x2c[par][:].rearrange("p (cb n) -> p cb n", cb=NCB),
                    in_=x2_v[:, :, i * TCH:(i + 1) * TCH]), deps=d, dma=1)

                def _fadd(ve, par=par):
                    for cb in range(NCB):
                        last = ve.tensor_tensor(out=fc[par][:, cb * TCH:(cb + 1) * TCH],
                                                in0=fc[par][:, cb * TCH:(cb + 1) * TCH],
                                                in1=sc_[par][:, cb * TCH:(cb + 1) * TCH],
                                                op=ALU.add)
                    return last
                t_fa = step("vector", _fadd, deps=[tf_, tss_])
                f_tok[i] = t_fa
                ow = [ya_tok[i - 1]] if i >= 1 else []
                ln5 = emit_ln_tok(P, fc[par][:], r5c[:], [t_fa], out_war=ow)
                ln5_tok[i] = ln5

                def _yadd(ve, par=par):
                    for cb in range(NCB):
                        last = ve.tensor_tensor(out=x2c[par][:, cb * TCH:(cb + 1) * TCH],
                                                in0=x2c[par][:, cb * TCH:(cb + 1) * TCH],
                                                in1=r5c[:, cb * TCH:(cb + 1) * TCH],
                                                op=ALU.add)
                    return last
                ya = step("vector", _yadd, deps=ln5 + [tx2_])
                ya_tok[i] = ya
                yst_tok[i] = step("sync", lambda sy, i=i, par=par: sy.dma_start(
                    out=y_v[:, :, i * TCH:(i + 1) * TCH],
                    in_=x2c[par][:].rearrange("p (cb n) -> p cb n", cb=NCB)),
                    deps=[ya], dma=1)

            p6_entry = now()
            tr_tok = {}
            ev_tok = {}
            st_toks = {}
            sum_tok = {}
            glist = []
            for i in range(NTILE):
                par = i % 2
                d = [tok_slot_ag]
                if i >= 2:
                    d += [sum_tok[i - 2][0], sum_tok[i - 2][1]]
                t_gz = step("vector", lambda ve, par=par: ve.memset(gbuf[par][:], 0.0),
                            deps=d)
                gt = []
                for k in range(TK):
                    d = [t_gz]
                    if len(glist) >= 8:
                        d.append(glist[-8])
                    t_g = step("gpsimd", lambda g, i=i, k=k, par=par: g.indirect_dma_start(
                        out=gbuf[par][:, k * D:(k + 1) * D],
                        out_offset=None,
                        in_=slot_rep.ap(),
                        in_offset=IndirectOffsetOnAxis(
                            ap=p_idxi[:, i * TK + k: i * TK + k + 1], axis=0),
                        bounds_check=breg[0], oob_is_err=False), deps=d, dma=1)
                    gt.append(t_g)
                    glist.append(t_g)

                def _sa(ve, par=par):
                    ve.tensor_tensor(out=sA[par][:], in0=gbuf[par][:, 0:D],
                                     in1=gbuf[par][:, D:2 * D], op=ALU.add)
                    ve.tensor_tensor(out=sA[par][:], in0=sA[par][:],
                                     in1=gbuf[par][:, 2 * D:3 * D], op=ALU.add)
                    return ve.tensor_tensor(out=sA[par][:], in0=sA[par][:],
                                            in1=gbuf[par][:, 3 * D:4 * D], op=ALU.add)
                d = gt[0:4] + ([sum_tok[i - 1][2]] if i >= 1 else [])
                t_sa = step("vector", _sa, deps=d)

                def _sb(po, par=par):
                    po.tensor_tensor(out=sB[par][:], in0=gbuf[par][:, 4 * D:5 * D],
                                     in1=gbuf[par][:, 5 * D:6 * D], op=ALU.add)
                    po.tensor_tensor(out=sB[par][:], in0=sB[par][:],
                                     in1=gbuf[par][:, 6 * D:7 * D], op=ALU.add)
                    return po.tensor_tensor(out=sB[par][:], in0=sB[par][:],
                                            in1=gbuf[par][:, 7 * D:8 * D], op=ALU.add)
                d = gt[4:8] + ([sum_tok[i - 1][2]] if i >= 1 else [])
                t_sb = step("gpsimd", _sb, deps=d)
                d = [t_sa, t_sb]
                if i >= 2:
                    d.append(tr_tok[i - 2])
                t_sum = step("vector", lambda ve, par=par: ve.tensor_tensor(
                    out=p6r[par][:], in0=sA[par][:], in1=sB[par][:], op=ALU.add), deps=d)
                sum_tok[i] = (t_sa, t_sb, t_sum)
                if t_routed is not None:
                    step("sync", lambda sy, i=i, par=par: sy.dma_start(
                        out=rview[i], in_=p6r[par][:]), deps=[t_sum], dma=1)

                def _tr4(te, i=i, par=par, grp=0):
                    for cb in range(4 * grp, 4 * grp + 4):
                        last = te.transpose(
                            ps_t[par][:, (cb % 4) * 128:((cb % 4) + 1) * 128],
                            p6r[par][:, cb * 128:(cb + 1) * 128], c_idf[:])
                    return last

                def _tev4(se, i=i, par=par, grp=0):
                    for cb in range(4 * grp, 4 * grp + 4):
                        last = se.activation(
                            out=p6rc[par][:, cb * 128:(cb + 1) * 128],
                            in_=ps_t[par][:, (cb % 4) * 128:((cb % 4) + 1) * 128],
                            func=AF.Copy)
                    return last
                tprev_ev = ev_tok.get(i - 2)
                for grp in range(2):
                    d = [t_sum]
                    if tprev_ev is not None:
                        d.append(tprev_ev)
                    t_t = step("tensor", lambda te, i=i, par=par, grp=grp: _tr4(te, i, par, grp),
                               deps=d)
                    dev = [t_t]
                    if grp == 0 and i >= 2:
                        dev.append(st_toks[i - 2])
                    tprev_ev = step("scalar", lambda se, i=i, par=par, grp=grp: _tev4(se, i, par, grp),
                                    deps=dev)
                tr_tok[i] = t_t
                ev_tok[i] = tprev_ev
                st_toks[i] = step("sync", lambda sy, i=i, par=par: sy.dma_start(
                    out=rcm_v[:, :, i * 128:(i + 1) * 128],
                    in_=p6rc[par][:].rearrange("p (cb n) -> p cb n", cb=NCB)),
                    deps=[tprev_ev], dma=1)
                if i % 4 == 3:
                    emit_final_chunk(i // 4)

        def _taps(sy):
            out = []
            for t_ap, srcap in [(t_x1, x1_dram.ap()), (t_ug, ug_a2a_out.ap()),
                                (t_hs, hs_a2a_out.ap()), (t_x2, x2_dram.ap()),
                                (t_x3, x3cm_dram.ap()), (t_meta, meta_out.ap()),
                                (t_routed, routed_dram.ap()), (t_shared, shared_dram.ap())]:
                if t_ap is not None:
                    out.append(sy.dma_start(out=t_ap, in_=srcap))
            if t_wts is not None:
                out.append(sy.dma_start(out=t_wts, in_=p_wts[:]))
            if t_idx is not None:
                out.append(sy.dma_start(out=t_idx, in_=p_idxi[:]))
            return out
        ntap = sum(x is not None for x in
                   (t_x1, t_ug, t_hs, t_x2, t_x3, t_meta, t_routed, t_shared, t_wts, t_idx))
        if ntap:
            sstep("sync", _taps, dma=ntap)

        fin_deps = now()

        @block.sync
        def _(sy):
            for s, v in fin_deps:
                sy.wait_ge(SEMQ[s], v)

    return nc, tap_aps


def _mk_inputs(L0_attn_x, L0_attn_gamma, L0_attn_beta, L0_attn_A, L0_attn_W_in,
               L0_attn_b_in, L0_attn_W_out, L0_attn_b_out, L0_ffn_router_w,
               L0_ffn_w_gate, L0_ffn_w_up, L0_ffn_w_down, L0_ffn_ws_gate_up,
               L0_ffn_ws_down):
    f32 = np.float32
    bf16 = ml_dtypes.bfloat16
    x = np.asarray(L0_attn_x, f32)
    gamma = np.asarray(L0_attn_gamma, f32)
    beta = np.asarray(L0_attn_beta, f32)
    A = np.asarray(L0_attn_A, f32)
    W_in = np.ascontiguousarray(np.asarray(L0_attn_W_in, f32))
    b_in = np.asarray(L0_attn_b_in, f32)
    W_out = np.ascontiguousarray(np.asarray(L0_attn_W_out, f32))
    b_out = np.asarray(L0_attn_b_out, f32)
    router_w = np.asarray(L0_ffn_router_w, f32)
    w_gate = np.asarray(L0_ffn_w_gate, f32)
    w_up = np.asarray(L0_ffn_w_up, f32)
    w_down = np.asarray(L0_ffn_w_down, f32)
    ws_gate_up = np.ascontiguousarray(np.asarray(L0_ffn_ws_gate_up, f32))
    ws_down = np.ascontiguousarray(np.asarray(L0_ffn_ws_down, f32))

    ones_bf = np.ones((128, 128), bf16)
    ones_f = np.ones((128, 128), f32)
    lt = (np.arange(128)[:, None] < np.arange(128)[None, :]).astype(f32)
    ident = np.eye(128, dtype=f32)
    iota64_ = np.tile(np.arange(E, dtype=f32), (128, 1))
    detie_ = (-1e-6 * np.arange(E, dtype=f32)).astype(f32)
    detie_ = np.tile(detie_, (128, 1))
    iota128_ = np.arange(128, dtype=f32).reshape(128, 1)

    in_maps = []
    for j in range(NCORES):
        xs = x[:, j * LPC:(j + 1) * LPC, :].reshape(B * LPC, D)
        xcm_j = np.ascontiguousarray(xs.T)
        presel_j = np.zeros((16, 2), f32)
        for bp in range(B):
            for jp in range(NCORES):
                for b_i in range(B):
                    if bp * NCORES + jp < b_i * NCORES + j:
                        presel_j[2 * jp + bp, b_i] = 1.0
        d = {
            "xcm": xcm_j,
            "g8": np.ascontiguousarray(gamma.reshape(NCB, 128).T),
            "b8": np.ascontiguousarray(beta.reshape(NCB, 128).T),
            "bo8": np.ascontiguousarray(b_out.reshape(NCB, 128).T),
            "acol": np.ascontiguousarray(A[j * 128:(j + 1) * 128].reshape(128, 1)),
            "w_in": W_in,
            "w_out": W_out,
            "rwt": np.ascontiguousarray(router_w.T),
            "wsgu": ws_gate_up,
            "wsd": ws_down,
            "wg_all": np.ascontiguousarray(w_gate[j * EPC:(j + 1) * EPC].reshape(EPC * D, FF)),
            "wu_all": np.ascontiguousarray(w_up[j * EPC:(j + 1) * EPC].reshape(EPC * D, FF)),
            "wd_all": np.ascontiguousarray(w_down[j * EPC:(j + 1) * EPC].reshape(EPC * FF, D)),
            "ones_bf": ones_bf,
            "ones_f": ones_f,
            "lt_strict": lt,
            "ident_f": ident,
            "iota64": iota64_,
            "detie": detie_,
            "iota128": iota128_,
            "rowbase": np.full((128, 1), j * TPC, f32),
            "slotb": np.full((128, 1), j * EPC * CAP, f32),
            "presel": presel_j,
            "ub8": np.ascontiguousarray(b_in[:D].reshape(NCB, 128).T),
            "epsc": np.full((128, 1), LN_EPS, f32),
            "gb8": np.ascontiguousarray(b_in[D:].reshape(NCB, 128).T),
        }
        in_maps.append(d)
    return in_maps


def kernel(**inputs):
    in_maps = _mk_inputs(**inputs)
    nc, _ = build_nc()
    res = run_bass_kernel_spmd(nc, in_maps, core_ids=list(range(NCORES)))
    y = np.empty((B, L, D), np.float32)
    for j in range(NCORES):
        blk = res.results[j]["ycm"]
        y[:, j * LPC:(j + 1) * LPC, :] = blk.T.reshape(B, LPC, D)
    return y


# version 21
# speedup vs baseline: 1.1790x; 1.0063x over previous
import os
import numpy as np
import ml_dtypes
from contextlib import ExitStack
import concourse.bass as bass
import concourse.mybir as mybir
from concourse.bass_utils import run_bass_kernel_spmd
from concourse.bass_core_rust import IndirectOffsetOnAxis
from concourse.aot_env import allow_ici_all_to_all

F32 = mybir.dt.float32
BF16 = mybir.dt.bfloat16
I32 = mybir.dt.int32
U32 = mybir.dt.uint32
AF = mybir.ActivationFunctionType
ALU = mybir.AluOpType

NCORES = 8
B = 2
L = 16384
D = 1024
DD = 2048
TPC = 4096
LPC = 2048
NCB = 8
E = 64
TK = 8
FF = 768
CAP = 512
SF = 768
EPC = 8
TCH = 512
NTCH = TPC // TCH
LCH = 1024
NLCH = L // LCH
NTILE = TPC // 128
LN_EPS = 1e-5
BIGIDX = 1.0e9
RG = [list(range(NCORES))]
INV_D = 1.0 / D
SWEEPS = ("P", "N", "N", "N", "N")


def build_nc(taps=(), nphases=6):
    nc = bass.Bass("TRN2", target_bir_lowering=False)
    dt_ = nc.dram_tensor

    xcm = dt_("xcm", [D, TPC], F32, kind="ExternalInput").ap()
    g8 = dt_("g8", [128, NCB], F32, kind="ExternalInput").ap()
    b8 = dt_("b8", [128, NCB], F32, kind="ExternalInput").ap()
    bo8 = dt_("bo8", [128, NCB], F32, kind="ExternalInput").ap()
    acol = dt_("acol", [128, 1], F32, kind="ExternalInput").ap()
    w_in = dt_("w_in", [D, DD], F32, kind="ExternalInput").ap()
    w_out = dt_("w_out", [D, D], F32, kind="ExternalInput").ap()
    rwt = dt_("rwt", [D, E], F32, kind="ExternalInput").ap()
    wsgu = dt_("wsgu", [D, 2 * SF], F32, kind="ExternalInput").ap()
    wsd = dt_("wsd", [SF, D], F32, kind="ExternalInput").ap()
    wg_all = dt_("wg_all", [EPC * D, FF], F32, kind="ExternalInput").ap()
    wu_all = dt_("wu_all", [EPC * D, FF], F32, kind="ExternalInput").ap()
    wd_all = dt_("wd_all", [EPC * FF, D], F32, kind="ExternalInput").ap()
    ones_bf = dt_("ones_bf", [128, 128], BF16, kind="ExternalInput").ap()
    ones_f = dt_("ones_f", [128, 128], F32, kind="ExternalInput").ap()
    lt_strict = dt_("lt_strict", [128, 128], F32, kind="ExternalInput").ap()
    ident_f = dt_("ident_f", [128, 128], F32, kind="ExternalInput").ap()
    iota64 = dt_("iota64", [128, E], F32, kind="ExternalInput").ap()
    detie = dt_("detie", [128, E], F32, kind="ExternalInput").ap()
    iota128 = dt_("iota128", [128, 1], F32, kind="ExternalInput").ap()
    rowbase = dt_("rowbase", [128, 1], F32, kind="ExternalInput").ap()
    slotb = dt_("slotb", [128, 1], F32, kind="ExternalInput").ap()
    presel = dt_("presel", [16, 2], F32, kind="ExternalInput").ap()
    ub8 = dt_("ub8", [128, NCB], F32, kind="ExternalInput").ap()
    epsc = dt_("epsc", [128, 1], F32, kind="ExternalInput").ap()
    zerc = dt_("zerc", [128, 1], F32, kind="ExternalInput").ap()
    gb8 = dt_("gb8", [128, NCB], F32, kind="ExternalInput").ap()

    ycm = dt_("ycm", [D, TPC], F32, kind="ExternalOutput").ap()

    x1_dram = dt_("x1_dram", [D, TPC], F32, kind="Internal")
    x2_dram = dt_("x2_dram", [D, TPC], F32, kind="Internal")
    x3cm_dram = dt_("x3cm_dram", [D, TPC], BF16, kind="Internal")
    shared_dram = dt_("shared_dram", [D, TPC], F32, kind="Internal")
    routed_dram = dt_("routed_dram", [TPC, D], F32, kind="Internal")
    routedcm_dram = dt_("routedcm_dram", [D, TPC], F32, kind="Internal")
    xb_dram = dt_("xb_dram", [EPC * CAP, D], BF16, kind="Internal")
    yb_dram = dt_("yb_dram", [2 * D, CAP], BF16, kind="Internal")
    ug_a2a_in = dt_("ug_a2a_in", [NCORES * 256, TPC], F32, kind="Internal")
    ug_a2a_out = dt_("ug_a2a_out", [NCORES * 256, TPC], F32, kind="Internal")
    hs_a2a_in = dt_("hs_a2a_in", [NCORES * 128, TPC], BF16, kind="Internal")
    hs_a2a_out = dt_("hs_a2a_out", [NCORES * 128, TPC], BF16, kind="Internal")
    x3row_dram = dt_("x3row_dram", [TPC, D], BF16, kind="Internal")
    x3rep = dt_("x3rep", [NCORES * TPC, D], BF16, kind="Internal", addr_space="Shared")
    slot_out = dt_("slot_out", [EPC * CAP, D], BF16, kind="Internal")
    slot_rep = dt_("slot_rep", [E * CAP, D], BF16, kind="Internal", addr_space="Shared")
    meta_in = dt_("meta_in", [E * CAP, 2], F32, kind="Internal")
    meta_out = dt_("meta_out", [E * CAP, 2], F32, kind="Internal")
    cnt_in = dt_("cnt_in", [2, E], F32, kind="Internal")
    cnt_out = dt_("cnt_out", [2 * NCORES, E], F32, kind="Internal", addr_space="Shared")

    tap_aps = {}

    def tap(name, shape, dtype):
        if name in taps:
            tap_aps[name] = dt_("tap_" + name, shape, dtype, kind="ExternalOutput").ap()
            return tap_aps[name]
        return None

    t_x1 = tap("x1", [D, TPC], F32)
    t_ug = tap("ug", [NCORES * 256, TPC], F32)
    t_hs = tap("hs", [NCORES * 128, TPC], BF16)
    t_x2 = tap("x2", [D, TPC], F32)
    t_x3 = tap("x3", [D, TPC], BF16)
    t_eid = tap("eid", [128, NTILE * TK], F32)
    t_wts = tap("wts", [128, NTILE * TK], F32)
    t_idx = tap("idx", [128, NTILE * TK], I32)
    t_meta = tap("meta", [E * CAP, 2], F32)
    t_routed = tap("routed", [TPC, D], F32)
    t_shared = tap("shared", [D, TPC], F32)

    with ExitStack() as OUTER:
        block = OUTER.enter_context(nc.Block())
        SEMQ = {}
        for nm in ("TE", "VE", "SE", "PO", "SY", "GD", "SC", "CC", "G0", "G1", "G2", "G3"):
            SEMQ[nm] = OUTER.enter_context(nc.semaphore("s" + nm))
        CNT = {nm: 0 for nm in SEMQ}
        WAITED = {e: {nm: 0 for nm in SEMQ} for e in
                  ("tensor", "vector", "scalar", "gpsimd", "sync")}
        QSEL = {"tensor": None, "vector": None, "scalar": None, "gpsimd": None,
                "sync": None}

        def _engsel(q):
            return {"tensor": block.tensor, "vector": block.vector,
                    "scalar": block.scalar, "gpsimd": block.gpsimd,
                    "sync": block.sync}[q]

        ENG_SEM = {"tensor": "TE", "vector": "VE", "scalar": "SE",
                   "gpsimd": "PO", "sync": "SY"}

        def step(queue, fn, deps=(), dma=0, sem=None, cc=False):
            mysem = sem or (("GD" if queue == "gpsimd" else
                             "SC" if queue == "scalar" else "SY")
                            if dma else ENG_SEM[queue])
            if cc:
                mysem = "CC"
            w = WAITED[queue]
            dep_list = [(s, v) for (s, v) in deps if v > w[s]]
            for s, v in dep_list:
                w[s] = max(w[s], v)
            if dma:
                CNT[mysem] += 16 * dma
            elif cc:
                CNT[mysem] += 1
            else:
                CNT[mysem] += 1
            after = CNT[mysem]
            handle = SEMQ[mysem]
            inc_each = 16 if dma else 1

            @_engsel(queue)
            def _(eng):
                for s, v in dep_list:
                    eng.wait_ge(SEMQ[s], v)
                if dma:
                    insts = fn(eng)
                    if not isinstance(insts, (list, tuple)):
                        insts = [insts]
                    assert len(insts) == dma, f"dma count {len(insts)} != {dma}"
                    for i_ in insts:
                        i_.then_inc(handle, 16)
                else:
                    last = fn(eng)
                    if last is None:
                        last = eng.nop()
                    last.then_inc(handle, 1)
            return (mysem, after)

        ALLQ = ("TE", "VE", "SE", "PO", "SY", "GD", "SC", "CC", "G0", "G1", "G2", "G3")

        def now():
            return [(s, CNT[s]) for s in ALLQ if CNT[s] > 0]

        def sstep(queue, fn, dma=0, cc=False, exclude=(), extra=()):
            deps = [t for t in now() if t[0] not in exclude] + list(extra)
            return step(queue, fn, deps=deps, dma=dma, cc=cc)

        def vseq(fns):
            for f in fns:
                sstep("vector", f)

        def sbuf(stk, name, shape, dtype):
            return stk.enter_context(nc.sbuf_tensor(name, shape, dtype))

        def psum(stk, name, shape, dtype=F32):
            return stk.enter_context(nc.psum_tensor(name, shape, dtype))

        c_onesbf = sbuf(OUTER, "c_onesbf", [128, 128], BF16)
        c_onesf = sbuf(OUTER, "c_onesf", [128, 128], F32)
        c_lt = sbuf(OUTER, "c_lt", [128, 128], F32)
        c_idf = sbuf(OUTER, "c_idf", [128, 128], F32)
        c_iota64 = sbuf(OUTER, "c_iota64", [128, E], F32)
        c_detie = sbuf(OUTER, "c_detie", [128, E], F32)
        c_iota128 = sbuf(OUTER, "c_iota128", [128, 1], F32)
        c_rowbase = sbuf(OUTER, "c_rowbase", [128, 1], F32)
        c_slotb = sbuf(OUTER, "c_slotb", [128, 1], F32)
        c_g8 = sbuf(OUTER, "c_g8", [128, NCB], F32)
        c_b8 = sbuf(OUTER, "c_b8", [128, NCB], F32)
        c_bo8 = sbuf(OUTER, "c_bo8", [128, NCB], F32)
        c_acol = sbuf(OUTER, "c_acol", [128, 1], F32)
        c_presel = sbuf(OUTER, "c_presel", [16, 2], F32)
        c_ub8 = sbuf(OUTER, "c_ub8", [128, NCB], F32)
        c_eps = sbuf(OUTER, "c_eps", [128, 1], F32)
        c_zero = sbuf(OUTER, "c_zero", [128, 1], F32)
        c_gb8 = sbuf(OUTER, "c_gb8", [128, NCB], F32)
        p_idxi = sbuf(OUTER, "p_idxi", [128, NTILE * TK], I32)
        p_wts = sbuf(OUTER, "p_wts", [128, NTILE * TK], F32)
        p_zero = sbuf(OUTER, "p_zero", [128, 512], F32)

        def _loads(sy):
            return [sy.dma_start(out=dst[:], in_=src)
                    for dst, src in [(c_onesbf, ones_bf), (c_onesf, ones_f),
                                     (c_lt, lt_strict), (c_idf, ident_f),
                                     (c_iota64, iota64), (c_detie, detie),
                                     (c_iota128, iota128), (c_rowbase, rowbase),
                                     (c_slotb, slotb), (c_g8, g8), (c_b8, b8),
                                     (c_bo8, bo8), (c_acol, acol), (c_presel, presel),
                                     (c_ub8, ub8), (c_gb8, gb8), (c_eps, epsc),
                                     (c_zero, zerc)]]
        sstep("sync", _loads, dma=18)
        breg = []

        def _breg(g):
            r = g.alloc_register(name="bnd")
            g.reg_mov(r, E * CAP - 1)
            breg.append(r)
            return g.nop()
        sstep("gpsimd", _breg)
        sstep("vector", lambda v: v.memset(p_zero[:], 0.0))
        sstep("sync", lambda sy: sy.dma_start(
            out=meta_in.ap().rearrange("(p r) c -> p (r c)", p=128), in_=p_zero[:, 0:512]),
            dma=1)



        def emit_ln_tok(P, xap, outap, dep_in, out_war=()):
            bf, sq = P["bf"], P["sq"]
            q_ = P["rot"][0]
            P["rot"][0] = 1 - q_
            mu, m2 = P["mu"][q_], P["m2"]
            var, rs, scr = P["var"], P["rs"][q_], P["scr"]
            ps1, ps2 = P["ps1"], P["ps2"]
            t_cast = step("scalar", lambda se: se.activation(out=bf[:], in_=xap, func=AF.Copy),
                          deps=list(dep_in) + P.get("war_bf", []))
            t_sq = step("vector", lambda ve: ve.tensor_tensor(out=sq[:], in0=bf[:],
                                                              in1=bf[:], op=ALU.mult),
                        deps=[t_cast])

            def _mm(te):
                te.ldweights(c_onesbf[:])
                for cb in range(NCB):
                    te.matmul(ps1[:], c_onesbf[:], bf[:, cb * TCH:(cb + 1) * TCH],
                              start=(cb == 0), stop=(cb == NCB - 1))
                for cb in range(NCB):
                    last = te.matmul(ps2[:], c_onesbf[:], sq[:, cb * TCH:(cb + 1) * TCH],
                                     start=(cb == 0), stop=(cb == NCB - 1))
                return last
            t_mm = step("tensor", _mm, deps=[t_sq] + P.get("war_ps", []))

            def _m1(ve):
                ve.tensor_scalar(out=mu[:], in0=ps1[:], scalar1=INV_D, scalar2=None,
                                 op0=ALU.mult)
                ve.tensor_scalar(out=m2[:], in0=ps2[:], scalar1=INV_D, scalar2=None,
                                 op0=ALU.mult)
                ve.tensor_tensor(out=scr[:], in0=mu[:], in1=mu[:], op=ALU.mult)
                return ve.tensor_tensor(out=var[:], in0=m2[:], in1=scr[:], op=ALU.subtract)
            t_m1 = step("vector", _m1, deps=[t_mm] + P["warq"].get(q_, []) +
                        P.get("war_v", []))
            t_sqrt = step("scalar", lambda se: se.activation(out=scr[:], in_=var[:],
                                                             func=AF.Sqrt, bias=c_eps[:]),
                          deps=[t_m1])
            t_rs = step("vector", lambda ve: ve.reciprocal(out=rs[:], in_=scr[:]),
                        deps=[t_sqrt])

            def _apv(ve):
                for cb in range(5):
                    xi = xap[:, cb * TCH:(cb + 1) * TCH]
                    oi = outap[:, cb * TCH:(cb + 1) * TCH]
                    ve.tensor_tensor(out=oi, in0=xi, in1=mu[:], op=ALU.subtract)
                    last = ve.tensor_tensor(out=oi, in0=oi, in1=rs[:], op=ALU.mult)
                return last

            def _app(po):
                for cb in range(5, NCB):
                    xi = xap[:, cb * TCH:(cb + 1) * TCH]
                    oi = outap[:, cb * TCH:(cb + 1) * TCH]
                    po.tensor_tensor(out=oi, in0=xi, in1=mu[:], op=ALU.subtract)
                    last = po.tensor_tensor(out=oi, in0=oi, in1=rs[:], op=ALU.mult)
                return last
            t_apv = step("vector", _apv, deps=[t_rs] + list(out_war))
            t_app = step("gpsimd", _app, deps=[t_rs] + list(out_war))
            P["war_bf"] = [t_mm]
            P["war_ps"] = [t_m1]
            P["war_v"] = [t_rs]
            P["warq"][q_] = [t_apv, t_app]
            return [t_apv, t_app]

        def emit_ln(P, xap, outap):
            bf, sq = P["bf"], P["sq"]
            mu, m2, var, rs, scr = P["mu"], P["m2"], P["var"], P["rs"], P["scr"]
            ps1, ps2 = P["ps1"], P["ps2"]

            def _cast(se):
                se.activation(out=bf[:], in_=xap, func=AF.Copy)
                return se.activation(out=sq[:], in_=bf[:], func=AF.Square)
            sstep("scalar", _cast)

            def _mm(te):
                te.ldweights(c_onesbf[:])
                for cb in range(NCB):
                    te.matmul(ps1[:], c_onesbf[:], bf[:, cb * TCH:(cb + 1) * TCH],
                              start=(cb == 0), stop=(cb == NCB - 1))
                for cb in range(NCB):
                    last = te.matmul(ps2[:], c_onesbf[:], sq[:, cb * TCH:(cb + 1) * TCH],
                                     start=(cb == 0), stop=(cb == NCB - 1))
                return last
            sstep("tensor", _mm)

            def _m1(ve):
                ve.tensor_scalar(out=mu[:], in0=ps1[:], scalar1=INV_D, scalar2=None, op0=ALU.mult)
                ve.tensor_scalar(out=m2[:], in0=ps2[:], scalar1=INV_D, scalar2=None, op0=ALU.mult)
                ve.tensor_tensor(out=scr[:], in0=mu[:], in1=mu[:], op=ALU.mult)
                return ve.tensor_tensor(out=var[:], in0=m2[:], in1=scr[:], op=ALU.subtract)
            sstep("vector", _m1)
            sstep("scalar", lambda se: se.activation(out=scr[:], in_=var[:], func=AF.Sqrt,
                                                        bias=c_eps[:]))

            def _m2(ve):
                last = ve.reciprocal(out=rs[:], in_=scr[:])
                for cb in range(NCB):
                    xi = xap[:, cb * TCH:(cb + 1) * TCH]
                    oi = outap[:, cb * TCH:(cb + 1) * TCH]
                    ve.tensor_tensor(out=oi, in0=xi, in1=mu[:], op=ALU.subtract)
                    last = ve.tensor_tensor(out=oi, in0=oi, in1=rs[:], op=ALU.mult)
                return last
            sstep("vector", _m2)

        def ln_tiles(stk, pfx):
            return dict(
                bf=sbuf(stk, pfx + "_bf", [128, NCB * TCH], BF16),
                sq=sbuf(stk, pfx + "_sq", [128, NCB * TCH], BF16),
                mu=[sbuf(stk, pfx + "_mu%d" % q_, [128, TCH], F32) for q_ in range(2)],
                m2=sbuf(stk, pfx + "_m2", [128, TCH], F32),
                var=sbuf(stk, pfx + "_var", [128, TCH], F32),
                rs=[sbuf(stk, pfx + "_rs%d" % q_, [128, TCH], F32) for q_ in range(2)],
                scr=sbuf(stk, pfx + "_scr", [128, TCH], F32),
                ps1=psum(stk, pfx + "_ps1", [128, TCH]),
                ps2=psum(stk, pfx + "_ps2", [128, TCH]),
                rot=[0],
                warq={},
            )

        xcm_v = xcm.rearrange("(cb p) n -> p cb n", p=128)
        x1_v = x1_dram.ap().rearrange("(cb p) n -> p cb n", p=128)
        x2_v = x2_dram.ap().rearrange("(cb p) n -> p cb n", p=128)
        x3_v = x3cm_dram.ap().rearrange("(cb p) n -> p cb n", p=128)

        with ExitStack() as S:
          if nphases >= 1:
            x0c = [sbuf(S, "p1_x0%d" % p_, [128, NCB * TCH], F32) for p_ in range(2)]
            x1c = [sbuf(S, "p1_x1%d" % p_, [128, NCB * TCH], F32) for p_ in range(2)]
            znc = sbuf(S, "p1_zn", [128, NCB * TCH], F32)
            P = ln_tiles(S, "p1")
            zhalf = sbuf(S, "p1_z", [128, NCB * 4 * TCH], BF16)
            winb = sbuf(S, "p1_win", [128, NCB * DD], BF16)
            evb = [sbuf(S, "p1_ev%d" % p_, [128, 4 * TCH], F32) for p_ in range(2)]
            mmps = psum(S, "p1_mm", [128, 4 * TCH])

            t_win = step("gpsimd", lambda g: g.dma_start(
                out=winb[:].rearrange("p (k m) -> p k m", k=NCB),
                in_=w_in.rearrange("(k p) m -> p k m", p=128)), deps=now(), dma=1)

            ln1_done = {}
            ln2_done = {}
            aff_tok = {}
            ld_tok = {}
            ev_tok = {}
            st_tok = {}
            wm_last = {}
            for i in range(NTCH):
                par = i % 2
                d = list(now()) if i == 0 else []
                if i == 1:
                    d = [ld_tok[0]]
                if i >= 2:
                    d = ln1_done[i - 2] + ln2_done[i - 2]
                ld_tok[i] = step("sync", lambda sy, i=i, par=par: sy.dma_start(
                    out=x0c[par][:].rearrange("p (cb n) -> p cb n", cb=NCB),
                    in_=xcm_v[:, :, i * TCH:(i + 1) * TCH]), deps=d, dma=1)
                ow = ln2_done[i - 2] if i >= 2 else []
                ln1_done[i] = emit_ln_tok(P, x0c[par][:], x1c[par][:],
                                          [ld_tok[i]], out_war=ow)
                st_x1 = step("sync", lambda sy, i=i, par=par: sy.dma_start(
                    out=x1_v[:, :, i * TCH:(i + 1) * TCH],
                    in_=x1c[par][:].rearrange("p (cb n) -> p cb n", cb=NCB)),
                    deps=ln1_done[i], dma=1)
                q_ = 1 - P["rot"][0]
                varT, rsT = P["var"], P["rs"][q_]
                v2 = P["mu"][q_]
                t_v1 = step("vector", lambda ve: ve.tensor_tensor(
                    out=v2[:], in0=varT[:], in1=rsT[:], op=ALU.mult), deps=ln1_done[i])
                t_v2 = step("vector", lambda ve: ve.tensor_tensor(
                    out=v2[:], in0=v2[:], in1=rsT[:], op=ALU.mult), deps=[t_v1])
                t_sq2 = step("scalar", lambda se: se.activation(
                    out=v2[:], in_=v2[:], func=AF.Sqrt, bias=c_eps[:]), deps=[t_v2])
                t_rs2 = step("vector", lambda ve: ve.reciprocal(out=v2[:], in_=v2[:]),
                             deps=[t_sq2])

                def _zn(ve, par=par):
                    for cb in range(NCB):
                        last = ve.tensor_tensor(out=znc[:, cb * TCH:(cb + 1) * TCH],
                                                in0=x1c[par][:, cb * TCH:(cb + 1) * TCH],
                                                in1=v2[:], op=ALU.mult)
                    return last
                ow = [aff_tok[i - 1]] if i >= 1 else []
                t_znc = step("vector", _zn, deps=[t_rs2] + ow)
                ln2_done[i] = [t_znc]
                P["warq"][q_] = [t_znc]

                def _aff(se, i=i):
                    for cb in range(NCB):
                        last = se.activation(
                            out=zhalf[:, cb * 4 * TCH + (i % 4) * TCH: cb * 4 * TCH + (i % 4 + 1) * TCH],
                            in_=znc[:, cb * TCH:(cb + 1) * TCH], func=AF.Identity,
                            scale=c_g8[:, cb:cb + 1], bias=c_b8[:, cb:cb + 1])
                    return last
                d = list(ln2_done[i])
                if i >= 4:
                    d.append(wm_last[i // 4 - 1])
                aff_tok[i] = step("scalar", _aff, deps=d)

                if i % 4 == 3:
                    half = i // 4
                    zdeps = [aff_tok[j] for j in range(half * 4, half * 4 + 4)]
                    for m in range(16):
                        def _wmm(te, m=m):
                            for k in range(NCB):
                                wt = winb[:, k * DD + m * 128: k * DD + (m + 1) * 128]
                                te.ldweights(wt)
                                for c in range(4):
                                    last = te.matmul(
                                        mmps[:, c * TCH:(c + 1) * TCH], wt,
                                        zhalf[:, k * 4 * TCH + c * TCH: k * 4 * TCH + (c + 1) * TCH],
                                        start=(k == 0), stop=(k == NCB - 1))
                            return last
                        d = zdeps + [t_win]
                        if m >= 1:
                            d.append(ev_tok[(half, m - 1)])
                        elif half == 1:
                            d.append(ev_tok[(0, 15)])
                        t_wm = step("tensor", _wmm, deps=d)
                        fn = AF.Identity if m < 8 else AF.Sigmoid
                        bcol = c_ub8[:, m:m + 1] if m < 8 else c_gb8[:, m - 8:m - 7]

                        def _ev(se, fn=fn, bcol=bcol, par=m % 2):
                            for c in range(4):
                                last = se.activation(out=evb[par][:, c * TCH:(c + 1) * TCH],
                                                     in_=mmps[:, c * TCH:(c + 1) * TCH],
                                                     func=fn, bias=bcol)
                            return last
                        d = [t_wm]
                        if m >= 2:
                            d.append(st_tok[(half, m - 2)])
                        ev_tok[(half, m)] = step("scalar", _ev, deps=d)
                        r0 = 256 * m if m < 8 else 256 * (m - 8) + 128
                        st_tok[(half, m)] = step(
                            "sync", lambda sy, r0=r0, half=half, par=m % 2: sy.dma_start(
                                out=ug_a2a_in.ap()[r0:r0 + 128, half * 2048:(half + 1) * 2048],
                                in_=evb[par][:]), deps=[ev_tok[(half, m)]], dma=1)
                    wm_last[half] = t_wm

            def _cc_ug(g):
                with allow_ici_all_to_all():
                    return g.collective_compute(
                        "AllToAll", ALU.bypass, replica_groups=RG,
                        ins=[ug_a2a_in.ap().opt()], outs=[ug_a2a_out.ap().opt()])
            sstep("gpsimd", _cc_ug, cc=True)

        with ExitStack() as S:
          if nphases >= 2:
            u = [sbuf(S, "p2_u%d" % p_, [128, 2 * LCH], F32) for p_ in range(2)]
            g_ = [sbuf(S, "p2_g%d" % p_, [128, 2 * LCH], F32) for p_ in range(2)]
            gp = [sbuf(S, "p2_gp%d" % p_, [128, 2 * LCH], F32) for p_ in range(2)]
            d1 = [sbuf(S, "p2_d1%d" % p_, [128, 2 * LCH], F32) for p_ in range(2)]
            G1 = [sbuf(S, "p2_G1%d" % p_, [128, 2 * LCH], F32) for p_ in range(2)]
            h = sbuf(S, "p2_h", [128, 2 * LCH], F32)
            m_ = sbuf(S, "p2_m", [128, 2 * LCH], F32)
            w_ = sbuf(S, "p2_w", [128, 2 * LCH], F32)
            s_ = sbuf(S, "p2_s", [128, 2 * LCH], F32)
            qq = sbuf(S, "p2_q", [128, 2 * LCH], F32)
            aa = sbuf(S, "p2_a", [128, 2 * LCH], F32)
            rr = sbuf(S, "p2_r", [128, 2 * LCH], F32)
            c1 = sbuf(S, "p2_c1", [128, 2 * LCH], F32)
            c2 = sbuf(S, "p2_c2", [128, 2 * LCH], F32)
            cc = sbuf(S, "p2_cc", [128, 2 * LCH], F32)
            car = [sbuf(S, "p2_car0", [128, 1], F32), sbuf(S, "p2_car1", [128, 1], F32)]

            p2_entry = now()
            t_car = step("vector", lambda ve: (ve.memset(car[0][:], 0.0),
                                               ve.memset(car[1][:], 0.0))[1],
                         deps=p2_entry)
            ld_tok = {}
            free_tok = {}

            def _mk_ld(ci, par):
                j = ci // 2
                lo = (ci % 2) * LCH

                def _ld(sy):
                    out = []
                    for b_i in range(B):
                        out.append(sy.dma_start(
                            out=u[par][:, b_i * LCH:(b_i + 1) * LCH],
                            in_=ug_a2a_out.ap()[256 * j:256 * j + 128,
                                                b_i * 2048 + lo: b_i * 2048 + lo + LCH]))
                        out.append(sy.dma_start(
                            out=g_[par][:, b_i * LCH:(b_i + 1) * LCH],
                            in_=ug_a2a_out.ap()[256 * j + 128:256 * j + 256,
                                                b_i * 2048 + lo: b_i * 2048 + lo + LCH]))
                    return out
                return _ld

            ld_tok[0] = step("sync", _mk_ld(0, 0), deps=p2_entry, dma=4)
            ld_tok[1] = step("sync", _mk_ld(1, 1), deps=p2_entry, dma=4)

            pre_tok = {}
            t_scan = t_car
            t_m_last = None
            for ci in range(NLCH):
                par = ci % 2
                def _pre1(ve, par=par):
                    return ve.tensor_scalar(out=gp[par][:], in0=g_[par][:], scalar1=-1.0,
                                            scalar2=1.0, op0=ALU.mult, op1=ALU.add)
                d = [ld_tok[ci]]
                if ci >= 2:
                    d.append(free_tok[ci - 2])
                t_p1 = step("vector", _pre1, deps=d)
                t_p2 = step("vector", lambda ve, par=par: ve.tensor_scalar(
                    out=d1[par][:], in0=gp[par][:], scalar1=c_acol[:], scalar2=None,
                    op0=ALU.mult), deps=[t_p1])
                t_p3 = step("gpsimd", lambda po, par=par: po.tensor_tensor(
                    out=G1[par][:], in0=g_[par][:], in1=d1[par][:], op=ALU.add),
                    deps=[t_p2])
                pre_tok[ci] = t_p3

                t_ms = step("vector", lambda ve: ve.memset(h[:], 0.0), deps=[t_scan])

                for si, kind in enumerate(("P", "N", "N")):
                    def _m(ve, par=par, first=(si == 0)):
                        for b_i in range(B):
                            c0 = b_i * LCH
                            ve.scalar_tensor_tensor(out=m_[:, c0:c0 + 1], in0=car[b_i][:],
                                                    scalar=c_acol[:],
                                                    in1=u[par][:, c0:c0 + 1],
                                                    op0=ALU.mult, op1=ALU.add)
                            last = ve.scalar_tensor_tensor(
                                out=m_[:, c0 + 1:c0 + LCH], in0=h[:, c0:c0 + LCH - 1],
                                scalar=c_acol[:], in1=u[par][:, c0 + 1:c0 + LCH],
                                op0=ALU.mult, op1=ALU.add)
                        return last
                    d = [ld_tok[ci], t_ms if si == 0 else t_scan]
                    t_m = step("vector", _m, deps=d)
                    t_m_last = t_m
                    t_w = step("scalar", lambda se: se.activation(
                        out=w_[:], in_=m_[:], func=AF.Tanh), deps=[t_m])
                    if kind == "P":
                        t_c1 = step("gpsimd", lambda po, par=par: po.tensor_tensor(
                            out=cc[:], in0=gp[par][:], in1=w_[:], op=ALU.mult),
                            deps=[t_w, pre_tok[ci]])

                        def _scanp(ve, par=par):
                            for b_i in range(B):
                                c0 = b_i * LCH
                                last = ve.tensor_tensor_scan(
                                    out=h[:, c0:c0 + LCH], data0=g_[par][:, c0:c0 + LCH],
                                    data1=cc[:, c0:c0 + LCH], initial=car[b_i][:],
                                    op0=ALU.mult, op1=ALU.add)
                            return last
                        t_scan = step("vector", _scanp, deps=[t_c1])
                    else:
                        t_s = step("vector", lambda ve: ve.tensor_tensor(
                            out=s_[:], in0=w_[:], in1=w_[:], op=ALU.mult), deps=[t_w])
                        t_q = step("gpsimd", lambda po, par=par: po.tensor_tensor(
                            out=qq[:], in0=d1[par][:], in1=s_[:], op=ALU.mult),
                            deps=[t_s, pre_tok[ci]])
                        t_r = step("vector", lambda ve, par=par: ve.tensor_tensor(
                            out=rr[:], in0=d1[par][:], in1=qq[:], op=ALU.subtract),
                            deps=[t_q])
                        t_c1 = step("gpsimd", lambda po, par=par: po.tensor_tensor(
                            out=c1[:], in0=gp[par][:], in1=w_[:], op=ALU.mult),
                            deps=[t_w])

                        def _c2(ve):
                            for b_i in range(B):
                                c0 = b_i * LCH
                                ve.tensor_tensor(out=c2[:, c0:c0 + 1], in0=rr[:, c0:c0 + 1],
                                                 in1=car[b_i][:], op=ALU.mult)
                                last = ve.tensor_tensor(out=c2[:, c0 + 1:c0 + LCH],
                                                        in0=rr[:, c0 + 1:c0 + LCH],
                                                        in1=h[:, c0:c0 + LCH - 1],
                                                        op=ALU.mult)
                            return last
                        t_c2 = step("vector", _c2, deps=[t_r])
                        t_a = step("gpsimd", lambda po, par=par: po.tensor_tensor(
                            out=aa[:], in0=G1[par][:], in1=qq[:], op=ALU.subtract),
                            deps=[t_q])
                        t_c = step("vector", lambda ve: ve.tensor_tensor(
                            out=cc[:], in0=c1[:], in1=c2[:], op=ALU.subtract),
                            deps=[t_c2, t_c1])

                        def _scann(ve):
                            for b_i in range(B):
                                c0 = b_i * LCH
                                last = ve.tensor_tensor_scan(
                                    out=h[:, c0:c0 + LCH], data0=aa[:, c0:c0 + LCH],
                                    data1=cc[:, c0:c0 + LCH], initial=car[b_i][:],
                                    op0=ALU.mult, op1=ALU.add)
                            return last
                        t_scan = step("vector", _scann, deps=[t_a, t_c])

                free_tok[ci] = t_m_last
                free_tok[ci] = t_scan

                def _carry(ve):
                    ve.tensor_copy(car[0][:], h[:, LCH - 1:LCH])
                    return ve.tensor_copy(car[1][:], h[:, 2 * LCH - 1:2 * LCH])
                t_scan = step("vector", _carry, deps=[t_scan])

                def _hsout(g, ci=ci):
                    j = ci // 2
                    lo = (ci % 2) * LCH
                    return [g.dma_start(
                        out=hs_a2a_in.ap()[128 * j:128 * (j + 1),
                                           b_i * 2048 + lo: b_i * 2048 + lo + LCH],
                        in_=h[:, b_i * LCH:(b_i + 1) * LCH]) for b_i in range(B)]
                t_hsdma = step("gpsimd", _hsout, deps=[t_scan], dma=2)
                t_scan = t_scan
                hs_free = t_hsdma
                if ci + 2 < NLCH:
                    ld_tok[ci + 2] = step("sync", _mk_ld(ci + 2, par),
                                          deps=[free_tok[ci]], dma=4)

                t_scan = (t_scan[0], t_scan[1])
                last_hs = t_hsdma
                t_scan_chain = t_scan

                def _noop():
                    pass
                
                if ci < NLCH - 1:
                    t_scan = ("VE", t_scan_chain[1])
                    t_scan = step("vector", lambda ve: ve.nop(), deps=[t_scan_chain, last_hs])

            def _cc_hs(g):
                with allow_ici_all_to_all():
                    return g.collective_compute(
                        "AllToAll", ALU.bypass, replica_groups=RG,
                        ins=[hs_a2a_in.ap().opt()], outs=[hs_a2a_out.ap().opt()])
            sstep("gpsimd", _cc_hs, cc=True)

        with ExitStack() as S:
          if nphases >= 3:
            x3bf = sbuf(S, "p3_x3bf", [128, NCB * TPC], BF16)
            trt_s = sbuf(S, "p3_tr0", [128, D], BF16)
            trt = [trt_s, trt_s]
            rwtb = sbuf(S, "p3_rwt", [128, NCB * E], BF16)
            lgt = sbuf(S, "p3_lg", [128, E], F32)
            mx = sbuf(S, "p3_mx", [128, TK], F32)
            mi_ = sbuf(S, "p3_mi", [128, TK], U32)
            nmx = sbuf(S, "p3_nmx", [128, 1], F32)
            exw = sbuf(S, "p3_ex", [128, TK], F32)
            s8 = sbuf(S, "p3_s8", [128, 1], F32)
            wv = sbuf(S, "p3_wv", [128, 1], F32)
            eid = sbuf(S, "p3_eid", [128, NTILE * TK], F32)
            mkt = sbuf(S, "p3_mk", [128, E], F32)
            tot = sbuf(S, "p3_tot", [128, NTILE * E], F32)
            scr64 = sbuf(S, "p3_scr64", [128, E], F32)
            cnt01 = [sbuf(S, "p3_cnt0", [1, E], F32), sbuf(S, "p3_cnt1", [1, E], F32)]
            SA = S.enter_context(ExitStack())
            ps_lg = psum(SA, "ps_lg", [128, E])
            ps_cnt2 = psum(SA, "ps_cnt2", [128, 2 * E])
            SC = S.enter_context(ExitStack())
            woutb = sbuf(SC, "p3_wout", [128, NCB * D], BF16)
            hsc_s = sbuf(SC, "p3_hs", [128, NCB * TCH], BF16)
            hsc = [hsc_s, hsc_s]
            x0c_s = sbuf(SC, "p3_x0", [128, NCB * TCH], F32)
            x0c = [x0c_s, x0c_s]
            x1c_s = sbuf(SC, "p3_x1", [128, NCB * TCH], F32)
            x1c = [x1c_s, x1c_s]
            arc_s = sbuf(SC, "p3_ar", [128, NCB * TCH], F32)
            arc = [arc_s, arc_s]
            r3c_s = sbuf(SC, "p3_r3", [128, NCB * TCH], F32)
            r3c = [r3c_s, r3c_s]
            P = ln_tiles(SC, "p3")
            mmps = psum(SC, "p3_mm", [128, 4 * TCH])

            trt_free = [[], []]
            x3row_toks = []
            t_wout = step("gpsimd", lambda g: g.dma_start(
                out=woutb[:].rearrange("p (k m) -> p k m", k=NCB),
                in_=w_out.rearrange("(k p) m -> p k m", p=128)), deps=now(), dma=1)
            t_rwt = step("gpsimd", lambda g: g.dma_start(
                out=rwtb[:].rearrange("p (k m) -> p k m", k=NCB),
                in_=rwt.rearrange("(k p) m -> p k m", p=128)), deps=[], dma=1)

            pa = {"vprev": None, "lgt_free": None}

            def emit_passA_tile(t32):
                def _lmm(te, t32=t32):
                    for k in range(NCB):
                        te.ldweights(x3bf[:, k * TPC + t32 * 128: k * TPC + (t32 + 1) * 128])
                        last = te.matmul(ps_lg[:],
                                         x3bf[:, k * TPC + t32 * 128: k * TPC + (t32 + 1) * 128],
                                         rwtb[:, k * E:(k + 1) * E],
                                         start=(k == 0), stop=(k == NCB - 1))
                    return last
                d = [x3c_tok[t32 // 4], t_rwt]
                if pa["lgt_free"] is not None:
                    d.append(pa["lgt_free"])
                t_lmm = step("tensor", _lmm, deps=d)
                d = [t_lmm]
                if pa["vprev"] is not None:
                    d.append(pa["vprev"])
                t_lgt = step("vector", lambda ve: ve.tensor_tensor(
                    out=lgt[:], in0=ps_lg[:], in1=c_detie[:], op=ALU.add), deps=d)
                pa["lgt_free"] = t_lgt

                def _mwi(ve):
                    ve.max_with_indices(out_max=mx[:], out_indices=mi_[:], in_=lgt[:])
                    return ve.nop()
                t_mwi = step("vector", _mwi, deps=[t_lgt])
                t_nmx = step("vector", lambda ve: ve.tensor_scalar(
                    out=nmx[:], in0=mx[:, 0:1], scalar1=-1.0, scalar2=None, op0=ALU.mult),
                    deps=[t_mwi])
                t_exp = step("scalar", lambda se: se.activation(
                    out=exw[:], in_=mx[:], func=AF.Exp, bias=nmx[:]), deps=[t_nmx])
                t_rsm = step("vector", lambda ve: ve.reduce_sum(
                    out=s8[:], in_=exw[:], axis=mybir.AxisListType.X), deps=[t_exp])
                t_rc = step("vector", lambda ve: ve.reciprocal(out=wv[:], in_=s8[:]),
                            deps=[t_rsm])
                t_wtsA = step("vector", lambda ve, t32=t32: ve.tensor_scalar(
                    out=p_wts[:, t32 * TK:(t32 + 1) * TK], in0=exw[:],
                    scalar1=wv[:], scalar2=None, op0=ALU.mult), deps=[t_rc])
                tprev = t_wtsA
                for k in range(TK):
                    t_mk = step("vector", lambda ve, t32=t32, k=k: ve.tensor_scalar(
                        out=mkt[:], in0=lgt[:], scalar1=mx[:, k:k + 1],
                        scalar2=None, op0=ALU.is_equal), deps=[tprev])
                    t_sc = step("vector", lambda ve: ve.tensor_tensor(
                        out=scr64[:], in0=mkt[:], in1=c_iota64[:], op=ALU.mult),
                        deps=[t_mk])
                    tprev = step("vector", lambda ve, t32=t32, k=k: ve.reduce_sum(
                        out=eid[:, t32 * TK + k: t32 * TK + k + 1],
                        in_=scr64[:], axis=mybir.AxisListType.X), deps=[t_sc])
                t_tot = step("vector", lambda ve, t32=t32: ve.tensor_scalar(
                    out=tot[:, t32 * E:(t32 + 1) * E], in0=lgt[:],
                    scalar1=mx[:, TK - 1:TK], scalar2=None, op0=ALU.is_ge), deps=[tprev])
                pa["vprev"] = t_tot

            def emit_cnt_seg(seg):
                def _cmm(te, seg=seg):
                    for t32 in range(seg * 16, seg * 16 + 16):
                        last = te.matmul(ps_cnt2[:, seg * E:(seg + 1) * E], c_onesf[:],
                                         tot[:, t32 * E:(t32 + 1) * E],
                                         start=(t32 % 16 == 0), stop=(t32 % 16 == 15))
                    return last
                return step("tensor", _cmm, deps=[pa["vprev"]])

            hs_v = hs_a2a_out.ap().rearrange("(k p) n -> p k n", p=128)
            hs_tok = {}
            x1_tok = {}
            x0_tok = {}
            wo_ev = {}
            wo_last = {}
            a1_tok = {}
            ln3_done = {}
            ln4_done = {}
            x2st_tok = {}
            x3c_tok = {}
            tok_cc_hs = [t for t in now() if t[0] == "CC"]
            for i in range(NTCH):
                par = i % 2
                d = list(tok_cc_hs) if i < 1 else [wo_last[i - 1]]
                hs_tok[i] = step("sync", lambda sy, i=i, par=par: sy.dma_start(
                    out=hsc[par][:].rearrange("p (k n) -> p k n", k=NCB),
                    in_=hs_v[:, :, i * TCH:(i + 1) * TCH]), deps=d, dma=1)
                d = [] if i < 1 else list(a1_tok[i - 1])
                x1_tok[i] = step("sync", lambda sy, i=i, par=par: sy.dma_start(
                    out=x1c[par][:].rearrange("p (cb n) -> p cb n", cb=NCB),
                    in_=x1_v[:, :, i * TCH:(i + 1) * TCH]), deps=d, dma=1)
                d = [] if i < 1 else [x2st_tok[i - 1]] + ln4_done[i - 1]
                x0_tok[i] = step("sync", lambda sy, i=i, par=par: sy.dma_start(
                    out=x0c[par][:].rearrange("p (cb n) -> p cb n", cb=NCB),
                    in_=xcm_v[:, :, i * TCH:(i + 1) * TCH]), deps=d, dma=1)
                for grp in range(2):
                    def _wo(te, grp=grp, par=par):
                        for mi in range(4):
                            m = grp * 4 + mi
                            for k in range(NCB):
                                wt = woutb[:, k * D + m * 128: k * D + (m + 1) * 128]
                                te.ldweights(wt)
                                last = te.matmul(
                                    mmps[:, mi * TCH:(mi + 1) * TCH], wt,
                                    hsc[par][:, k * TCH:(k + 1) * TCH],
                                    start=(k == 0), stop=(k == NCB - 1))
                        return last
                    d = [hs_tok[i], t_wout]
                    if grp == 1:
                        d.append(wo_ev[(i, 0)])
                    elif i >= 1:
                        d.append(wo_ev[(i - 1, 1)])
                    t_wo = step("tensor", _wo, deps=d)
                    wo_last[i] = t_wo

                    def _ev(se, grp=grp, par=par):
                        for mi in range(4):
                            m = grp * 4 + mi
                            last = se.activation(out=arc[par][:, m * TCH:(m + 1) * TCH],
                                                 in_=mmps[:, mi * TCH:(mi + 1) * TCH],
                                                 func=AF.Identity, bias=c_bo8[:, m:m + 1])
                        return last
                    d = [t_wo]
                    if i >= 1:
                        d += ln3_done[i - 1]
                    wo_ev[(i, grp)] = step("scalar", _ev, deps=d)

                def _addx1(ve, par=par):
                    for cb in range(NCB):
                        last = ve.tensor_tensor(out=arc[par][:, cb * TCH:(cb + 1) * TCH],
                                                in0=arc[par][:, cb * TCH:(cb + 1) * TCH],
                                                in1=x1c[par][:, cb * TCH:(cb + 1) * TCH],
                                                op=ALU.add)
                    return last
                a1 = step("vector", _addx1, deps=[wo_ev[(i, 0)], wo_ev[(i, 1)], x1_tok[i]])
                a1_tok[i] = [a1]
                ow = [] if i < 1 else [x3c_tok[i - 1]]
                ln3_done[i] = emit_ln_tok(P, arc[par][:], r3c[par][:],
                                          [a1], out_war=ow)

                def _addx0(ve, par=par):
                    for cb in range(NCB):
                        last = ve.tensor_tensor(out=x0c[par][:, cb * TCH:(cb + 1) * TCH],
                                                in0=x0c[par][:, cb * TCH:(cb + 1) * TCH],
                                                in1=r3c[par][:, cb * TCH:(cb + 1) * TCH],
                                                op=ALU.add)
                    return last
                a0 = step("vector", _addx0, deps=ln3_done[i] + [x0_tok[i]])
                x2st_tok[i] = step("sync", lambda sy, i=i, par=par: sy.dma_start(
                    out=x2_v[:, :, i * TCH:(i + 1) * TCH],
                    in_=x0c[par][:].rearrange("p (cb n) -> p cb n", cb=NCB)),
                    deps=[a0], dma=1)
                ln4_done[i] = emit_ln_tok(P, x0c[par][:], r3c[par][:],
                                          [a0], out_war=ln3_done[i])

                def _x3cast(se, i=i, par=par):
                    for cb in range(NCB):
                        last = se.activation(
                            out=x3bf[:, cb * TPC + i * TCH: cb * TPC + (i + 1) * TCH],
                            in_=r3c[par][:, cb * TCH:(cb + 1) * TCH], func=AF.Copy)
                    return last
                x3c_tok[i] = step("scalar", _x3cast, deps=ln4_done[i])

                def _x3st(sy, i=i):
                    return sy.dma_start(
                        out=x3_v[:, :, i * TCH:(i + 1) * TCH],
                        in_=x3bf[:].rearrange("p (cb n) -> p cb n", cb=NCB)[:, :, i * TCH:(i + 1) * TCH])
                t_x3st = step("sync", _x3st, deps=[x3c_tok[i]], dma=1)
                for blk in range(i * 4, (i + 1) * 4):
                    tr_t = step("sync", lambda sy, blk=blk: sy.dma_start(
                        out=trt[0][:], in_=x3cm_dram.ap()[:, blk * 128:(blk + 1) * 128],
                        transpose=True), deps=[t_x3st] + trt_free[0], dma=1)
                    st_t = step("sync", lambda sy, blk=blk: sy.dma_start(
                        out=x3row_dram.ap()[blk * 128:(blk + 1) * 128, :], in_=trt[0][:]),
                        deps=[tr_t], dma=1)
                    trt_free[0] = [st_t]
                    x3row_toks.append(st_t)

                if i >= 2:
                    for t32 in range(4 * (i - 2), 4 * (i - 1)):
                        emit_passA_tile(t32)
                    if 4 * (i - 1) == 16:
                        emit_cnt_seg(0)

            for t32 in range(4 * (NTCH - 2), NTILE):
                emit_passA_tile(t32)
            emit_cnt_seg(1)

            SC.close()

            def _cntev(se):
                se.activation(out=cnt01[0][:], in_=ps_cnt2[0:1, 0:E], func=AF.Copy)
                return se.activation(out=cnt01[1][:], in_=ps_cnt2[0:1, E:2 * E], func=AF.Copy)
            sstep("scalar", _cntev, exclude=("CC",))
            passA_done = now()
            SA.close()

            def _cntst(sy):
                return [sy.dma_start(out=cnt_in.ap()[0:1, :], in_=cnt01[0][:]),
                        sy.dma_start(out=cnt_in.ap()[1:2, :], in_=cnt01[1][:])]
            sstep("sync", _cntst, dma=2)
            tok_cnt_ag = sstep("gpsimd", lambda g: g.collective_compute(
                "AllGather", ALU.bypass, replica_groups=RG,
                ins=[cnt_in.ap().opt()], outs=[cnt_out.ap().opt()]), cc=True)
            tok_x3rep = step("gpsimd", lambda g: g.collective_compute(
                "AllGather", ALU.bypass, replica_groups=RG,
                ins=[x3row_dram.ap().opt()], outs=[x3rep.ap().opt()]),
                deps=x3row_toks + [tok_cnt_ag], cc=True)

            _sstep_real = sstep

            def sstep(queue, fn, dma=0, cc=False, exclude=(), extra=()):
                return _sstep_real(queue, fn, dma=dma, cc=cc,
                                   exclude=tuple(set(exclude) | {"CC"}),
                                   extra=list(extra) + [tok_cnt_ag])
            cnta = sbuf(S, "p3_cnta", [16, E], F32)
            pre01 = [sbuf(S, "p3_pre0", [1, E], F32), sbuf(S, "p3_pre1", [1, E], F32)]
            pfull = sbuf(S, "p3_pfull", [128, E], F32)
            sr = sbuf(S, "p3_sr", [128, E], F32)
            psk = sbuf(S, "p3_psk", [128, E], F32)
            pos = sbuf(S, "p3_pos", [128, TK], F32)
            slotf = sbuf(S, "p3_slotf", [128, TK], F32)
            vmk = sbuf(S, "p3_vm", [128, TK], F32)
            idxf = sbuf(S, "p3_idxf", [128, TK], F32)
            rowid = sbuf(S, "p3_rowid", [128, 1], F32)
            payA = sbuf(S, "p3_payA", [128, NTILE * 2 * TK], F32)
            e1s = sbuf(S, "p3_e1s", [128, E], F32)
            ps_tp = psum(S, "ps_tp", [128, E])
            ps_run = [psum(S, "ps_run0", [128, E]), psum(S, "ps_run1", [128, E])]
            ps_pre = psum(S, "ps_pre", [1, E])

            t_cnta = sstep("sync", lambda sy: sy.dma_start(out=cnta[:], in_=cnt_out.ap()),
                           dma=1, exclude=("CC",), extra=[tok_cnt_ag])
            pre_ev = {}
            t_pev = None
            for b_i in range(B):
                def _pmm(te, b_i=b_i):
                    return te.matmul(ps_pre[:], c_presel[:, b_i:b_i + 1], cnta[:],
                                     start=True, stop=True)
                d = [t_cnta] + ([t_pev] if t_pev else [])
                t_pmm = step("tensor", _pmm, deps=d)
                t_pev = step("scalar", lambda se, b_i=b_i: se.activation(
                    out=pre01[b_i][:], in_=ps_pre[:], func=AF.Copy), deps=[t_pmm])
                pre_ev[b_i] = t_pev

            vtok = [sstep("vector", lambda ve: ve.nop(), exclude=("CC",),
                          extra=[tok_cnt_ag])]
            e1_tok = {}
            pf_tok = {}
            acc_tok = {}
            base_done = {}
            sc_glist = []
            for t32 in range(NTILE):
                seg = t32 // 16
                if t32 % 16 == 0:
                    def _base(te, seg=seg):
                        return te.matmul(ps_run[seg][:], c_onesf[0:1, :], pre01[seg][:],
                                         start=True, stop=False)
                    d = [pre_ev[seg]]
                    base_done[seg] = step("tensor", _base, deps=d)

                def _tp(te, t32=t32):
                    return te.matmul(ps_tp[:], c_lt[:], tot[:, t32 * E:(t32 + 1) * E],
                                     start=True, stop=True)
                d = [base_done[seg]]
                if t32 >= 1:
                    d.append(e1_tok[t32 - 1])
                t_tp = step("tensor", _tp, deps=d)
                d = [t_tp]
                if t32 >= 1:
                    d.append(pf_tok[t32 - 1])
                e1_tok[t32] = step("scalar", lambda se: se.activation(
                    out=e1s[:], in_=ps_tp[:], func=AF.Copy), deps=d)

                def _pf(ve, seg=seg):
                    ve.tensor_tensor(out=pfull[:], in0=e1s[:], in1=ps_run[seg][:],
                                     op=ALU.add)
                    return ve.memset(sr[:], 0.0)
                d = [e1_tok[t32], vtok[-1]]
                if t32 % 16 != 0:
                    d.append(acc_tok[t32 - 1])
                pf_tok[t32] = step("vector", _pf, deps=d)
                vtok.append(pf_tok[t32])

                def _acc(te, t32=t32, seg=seg):
                    return te.matmul(ps_run[seg][:], c_onesf[:], tot[:, t32 * E:(t32 + 1) * E],
                                     start=False, stop=(t32 % 16 == 15))
                acc_tok[t32] = step("tensor", _acc, deps=[pf_tok[t32]])

                for k in range(TK):
                    def _s1(ve, t32=t32, k=k):
                        if k > 0:
                            ve.tensor_tensor(out=sr[:], in0=sr[:], in1=mkt[:], op=ALU.add)
                        return ve.tensor_scalar(out=mkt[:], in0=c_iota64[:],
                                                scalar1=eid[:, t32 * TK + k: t32 * TK + k + 1],
                                                scalar2=None, op0=ALU.is_equal)
                    t1 = step("vector", _s1, deps=[vtok[-1]])

                    def _s1b(ve):
                        ve.tensor_tensor(out=psk[:], in0=pfull[:], in1=sr[:], op=ALU.add)
                        return ve.tensor_tensor(out=scr64[:], in0=mkt[:], in1=psk[:],
                                                op=ALU.mult)
                    t1b = step("vector", _s1b, deps=[t1])
                    t2 = step("vector", lambda ve, k=k: ve.reduce_sum(
                        out=pos[:, k:k + 1], in_=scr64[:], axis=mybir.AxisListType.X),
                        deps=[t1b])
                    vtok.append(t2)

                t3a = step("vector", lambda ve, t32=t32: ve.scalar_tensor_tensor(
                    out=slotf[:], in0=eid[:, t32 * TK:(t32 + 1) * TK],
                    scalar=float(CAP), in1=pos[:], op0=ALU.mult, op1=ALU.add),
                    deps=[vtok[-1]])
                t3b = step("vector", lambda ve: ve.tensor_scalar(
                    out=vmk[:], in0=pos[:], scalar1=float(CAP), scalar2=None,
                    op0=ALU.is_ge), deps=[t3a])
                t3 = step("vector", lambda ve: ve.scalar_tensor_tensor(
                    out=idxf[:], in0=vmk[:], scalar=BIGIDX, in1=slotf[:],
                    op0=ALU.mult, op1=ALU.add), deps=[t3b])
                t4 = step("vector", lambda ve, t32=t32: ve.tensor_copy(
                    p_idxi[:, t32 * TK:(t32 + 1) * TK], idxf[:]), deps=[t3])

                t5a = step("vector", lambda ve, t32=t32: ve.tensor_scalar(
                    out=rowid[:], in0=c_iota128[:], scalar1=c_rowbase[:],
                    scalar2=float(t32 * 128 + 1), op0=ALU.add, op1=ALU.add), deps=[t4])
                pview = payA[:].rearrange("p (t k c) -> p t k c", t=NTILE, k=TK)

                def _s3(ve, t32=t32):
                    ve.tensor_scalar(
                        out=pview[:, t32, :, 0], in0=p_wts[:, t32 * TK:(t32 + 1) * TK],
                        scalar1=0.0, scalar2=rowid[:], op0=ALU.mult, op1=ALU.add)
                    return ve.tensor_copy(pview[:, t32, :, 1],
                                          p_wts[:, t32 * TK:(t32 + 1) * TK])
                t5 = step("vector", _s3, deps=[t5a])
                vtok.append(t5)

                def _sc8(g, t32=t32):
                    return [g.indirect_dma_start(
                        out=meta_in.ap(),
                        out_offset=IndirectOffsetOnAxis(
                            ap=p_idxi[:, t32 * TK + k: t32 * TK + k + 1], axis=0),
                        in_=payA[:, (t32 * TK + k) * 2:(t32 * TK + k) * 2 + 2],
                        in_offset=None,
                        bounds_check=breg[0], oob_is_err=False) for k in range(TK)]
                d = [t5]
                if len(sc_glist) >= 1:
                    d.append(sc_glist[-1])
                sc_glist.append(step("gpsimd", _sc8, deps=d, dma=TK))

            sstep = _sstep_real
            tok_meta = sstep("gpsimd", lambda g: g.collective_compute(
                "AllReduce", ALU.add, replica_groups=RG,
                ins=[meta_in.ap().opt()], outs=[meta_out.ap().opt()]), cc=True)

            if nphases >= 4:
                wgub = sbuf(S, "p4_wgu", [128, NCB * 2 * SF], BF16)
                wsdb = sbuf(S, "p4_wsd", [128, 6 * D], BF16)
                hb = sbuf(S, "p4_h", [128, 6 * TPC], BF16)
                shc = sbuf(S, "p4_sh", [128, 2 * TCH], F32)
                hgs4 = sbuf(S, "p4_hgs", [128, TCH], BF16)
                ps_g = psum(S, "p4_psg", [128, TCH])
                ps_u = psum(S, "p4_psu", [128, TCH])
                ps_d = psum(S, "p4_psd", [128, 2 * TCH])

                t_wgu = step("gpsimd", lambda g: g.dma_start(
                    out=wgub[:].rearrange("p (k m) -> p k m", k=NCB),
                    in_=wsgu.rearrange("(k p) m -> p k m", p=128)), deps=[], dma=1)
                t_wsd = step("gpsimd", lambda g: g.dma_start(
                    out=wsdb[:].rearrange("p (k m) -> p k m", k=6),
                    in_=wsd.rearrange("(k p) m -> p k m", p=128)), deps=[], dma=1)

                t_se = None
                t_ve = None
                t_te = None
                for m in range(6):
                    for c in range(NTCH):
                        def _gu(te, m=m, c=c):
                            for k in range(NCB):
                                wt = wgub[:, k * 2 * SF + m * 128: k * 2 * SF + (m + 1) * 128]
                                te.ldweights(wt)
                                te.matmul(ps_g[:], wt,
                                          x3bf[:, k * TPC + c * TCH: k * TPC + (c + 1) * TCH],
                                          start=(k == 0), stop=(k == NCB - 1))
                            for k in range(NCB):
                                wt = wgub[:, k * 2 * SF + (m + 6) * 128: k * 2 * SF + (m + 7) * 128]
                                te.ldweights(wt)
                                last = te.matmul(
                                    ps_u[:], wt,
                                    x3bf[:, k * TPC + c * TCH: k * TPC + (c + 1) * TCH],
                                    start=(k == 0), stop=(k == NCB - 1))
                            return last
                        d = [t_wgu] + passA_done
                        if t_se is not None:
                            d += [t_se, t_ve]
                        t_te = step("tensor", _gu, deps=d)
                        t_se = step("scalar", lambda se: se.activation(
                            out=hgs4[:], in_=ps_g[:], func=AF.Silu), deps=[t_te])
                        t_ve = step("vector", lambda ve, m=m, c=c: ve.tensor_tensor(
                            out=hb[:, m * TPC + c * TCH: m * TPC + (c + 1) * TCH],
                            in0=hgs4[:], in1=ps_u[:], op=ALU.mult), deps=[t_te, t_se])

                sh_v = shared_dram.ap().rearrange("(cb p) n -> p cb n", p=128)
                t_hdone = t_ve
                t_dev = None
                t_dst = None
                p4_sh_toks = []
                for m in range(NCB):
                    for quad in range(4):
                        def _dn(te, m=m, quad=quad):
                            for k in range(6):
                                wt = wsdb[:, k * D + m * 128: k * D + (m + 1) * 128]
                                te.ldweights(wt)
                                for c in range(2):
                                    cg = quad * 2 + c
                                    last = te.matmul(
                                        ps_d[:, c * TCH:(c + 1) * TCH], wt,
                                        hb[:, k * TPC + cg * TCH: k * TPC + (cg + 1) * TCH],
                                        start=(k == 0), stop=(k == 5))
                            return last
                        d = [t_wsd, t_hdone]
                        if t_dev is not None:
                            d.append(t_dev)
                        t_te = step("tensor", _dn, deps=d)

                        def _dev(se):
                            for c in range(2):
                                last = se.activation(out=shc[:, c * TCH:(c + 1) * TCH],
                                                     in_=ps_d[:, c * TCH:(c + 1) * TCH],
                                                     func=AF.Copy)
                            return last
                        d = [t_te]
                        if t_dst is not None:
                            d.append(t_dst)
                        t_dev = step("scalar", _dev, deps=d)
                        t_dst = step("sync", lambda sy, m=m, quad=quad: sy.dma_start(
                            out=sh_v[:, m, quad * 1024:(quad + 1) * 1024], in_=shc[:]),
                            deps=[t_dev], dma=1)
                        p4_sh_toks.append(t_dst)
                p4_done = now()

        with ExitStack() as S:
          if nphases >= 5:
            meta_g = [sbuf(S, "p5_meta%d" % i_, [128, 2], F32) for i_ in range(32)]
            gidx = [sbuf(S, "p5_gidx%d" % p_, [128, 4], I32) for p_ in range(2)]
            scr1 = sbuf(S, "p5_scr1", [128, 1], F32)
            scr2 = sbuf(S, "p5_scr2", [128, 1], F32)
            xbs = [sbuf(S, "p5_xbs%d" % i_, [128, D], BF16) for i_ in range(8)]
            xbt = [sbuf(S, "p5_xbt%d" % p_, [128, NCB * CAP], BF16) for p_ in range(2)]
            wgb = [sbuf(S, "p5_wg%d" % p_, [128, NCB * FF], BF16) for p_ in range(2)]
            wub = [sbuf(S, "p5_wu%d" % p_, [128, NCB * FF], BF16) for p_ in range(2)]
            wdb = [sbuf(S, "p5_wd%d" % p_, [128, 6 * D], BF16) for p_ in range(2)]
            hgs = sbuf(S, "p5_hgs", [128, CAP], BF16)
            hbe = sbuf(S, "p5_h", [128, 6 * CAP], BF16)
            ycmb = [sbuf(S, "p5_ycm%d" % p_, [128, NCB * CAP], BF16) for p_ in range(2)]
            yt = [sbuf(S, "p5_yt%d" % p_, [128, D], BF16) for p_ in range(2)]
            ys = [sbuf(S, "p5_ys%d" % p_, [128, D], BF16) for p_ in range(2)]
            ps_a = psum(S, "p5_psa", [128, CAP])
            ps_b = psum(S, "p5_psb", [128, CAP])
            ps_c = psum(S, "p5_psc", [128, CAP])

            p5_entry = now()
            xbst_tok = {}
            meta_tok = {}
            vprev = None
            gx_prev = {}
            for e in range(EPC):
                tprev = None
                for g_i in range(4):
                    if e == 0 and g_i == 0:
                        d = list(p5_entry)
                    else:
                        d = [tprev or vprev]
                    if g_i == 0 and e >= 2:
                        d.append(gx_prev[e - 2])
                    t1 = step("vector", lambda ve, e=e, g_i=g_i: ve.tensor_scalar(
                        out=scr1[:], in0=c_iota128[:], scalar1=c_slotb[:],
                        scalar2=float(e * CAP + g_i * 128), op0=ALU.add, op1=ALU.add),
                        deps=d)
                    tprev = step("vector", lambda ve, e=e, g_i=g_i: ve.tensor_copy(
                        gidx[e % 2][:, g_i:g_i + 1], scr1[:]), deps=[t1])
                d = [tprev, tok_meta]
                if e >= 1:
                    d.append(gx_prev[e - 1])

                def _gmeta(g, e=e):
                    return [g.indirect_dma_start(
                        out=meta_g[e * 4 + g_i][:], out_offset=None,
                        in_=meta_out.ap(),
                        in_offset=IndirectOffsetOnAxis(ap=gidx[e % 2][:, g_i:g_i + 1], axis=0),
                        bounds_check=breg[0], oob_is_err=False) for g_i in range(4)]
                t_gmeta = step("gpsimd", _gmeta, deps=d, dma=4)

                st_toks5 = []
                tprev = t_gmeta
                for g_i in range(4):
                    t1 = step("vector", lambda ve, e=e, g_i=g_i: ve.tensor_scalar(
                        out=scr1[:], in0=meta_g[e * 4 + g_i][:, 0:1], scalar1=0.0,
                        scalar2=None, op0=ALU.is_equal), deps=[tprev])
                    t2 = step("vector", lambda ve, e=e, g_i=g_i: ve.scalar_tensor_tensor(
                        out=scr2[:], in0=scr1[:], scalar=BIGIDX,
                        in1=meta_g[e * 4 + g_i][:, 0:1], op0=ALU.mult, op1=ALU.add),
                        deps=[t1])
                    t3 = step("vector", lambda ve: ve.tensor_scalar(
                        out=scr2[:], in0=scr2[:], scalar1=1.0, scalar2=None,
                        op0=ALU.subtract), deps=[t2])
                    tprev = step("vector", lambda ve, e=e, g_i=g_i: ve.tensor_copy(
                        gidx[e % 2][:, g_i:g_i + 1], scr2[:]), deps=[t3])
                    bi = (e * 4 + g_i) % 8
                    d = [tprev, tok_x3rep]
                    if (e * 4 + g_i) >= 4:
                        d.append(xbst_tok[e * 4 + g_i - 4])
                    t_gx = step("gpsimd", lambda g, e=e, g_i=g_i, bi=bi: g.indirect_dma_start(
                        out=xbs[bi][:], out_offset=None,
                        in_=x3rep.ap(),
                        in_offset=IndirectOffsetOnAxis(ap=gidx[e % 2][:, g_i:g_i + 1], axis=0),
                        bounds_check=breg[0], oob_is_err=False), deps=d, dma=1)
                    gx_prev[e] = t_gx
                    t_st = step("sync", lambda sy, e=e, g_i=g_i, bi=bi: sy.dma_start(
                        out=xb_dram.ap()[e * CAP + g_i * 128: e * CAP + (g_i + 1) * 128, :],
                        in_=xbs[bi][:]), deps=[t_gx], dma=1)
                    xbst_tok[e * 4 + g_i] = t_st
                    st_toks5.append(t_st)
                meta_tok[e] = st_toks5
                vprev = tprev

            def _mk_wld(e, par):
                def _wld(g):
                    return [
                        g.dma_start(out=wgb[par][:].rearrange("p (k m) -> p k m", k=NCB),
                                    in_=wg_all[e * D:(e + 1) * D, :]
                                    .rearrange("(k p) m -> p k m", p=128)),
                        g.dma_start(out=wub[par][:].rearrange("p (k m) -> p k m", k=NCB),
                                    in_=wu_all[e * D:(e + 1) * D, :]
                                    .rearrange("(k p) m -> p k m", p=128)),
                        g.dma_start(out=wdb[par][:].rearrange("p (k m) -> p k m", k=6),
                                    in_=wd_all[e * FF:(e + 1) * FF, :]
                                    .rearrange("(k p) m -> p k m", p=128))]
                return _wld
            tok_wld = {}
            tok_free_w = {}
            tok_xbt = {}
            tok_ys = {}
            tok_ybst = {}
            slot_toks = []
            t_se = None
            t_ve = None
            t_ev = None
            tok_ytr_all = {}
            for e in (0, 1):
                tok_wld[e] = step("gpsimd", _mk_wld(e, e % 2), deps=p5_entry, dma=3)
            for e in range(EPC):
                par = e % 2
                d = list(meta_tok[e])
                if e >= 2:
                    d.append(tok_free_w[e - 2])

                def _xbt(sy, e=e, par=par):
                    return [sy.dma_start(out=xbt[par][:, cb * CAP:(cb + 1) * CAP],
                                         in_=xb_dram.ap()[e * CAP:(e + 1) * CAP,
                                                          cb * 128:(cb + 1) * 128],
                                         transpose=True) for cb in range(NCB)]
                tok_xbt[e] = step("sync", _xbt, deps=d, dma=NCB)

                for m in range(6):
                    def _ge(te, m=m, par=par):
                        for k in range(NCB):
                            wt = wgb[par][:, k * FF + m * 128: k * FF + (m + 1) * 128]
                            te.ldweights(wt)
                            te.matmul(ps_a[:], wt, xbt[par][:, k * CAP:(k + 1) * CAP],
                                      start=(k == 0), stop=(k == NCB - 1))
                        for k in range(NCB):
                            wt = wub[par][:, k * FF + m * 128: k * FF + (m + 1) * 128]
                            te.ldweights(wt)
                            last = te.matmul(ps_b[:], wt, xbt[par][:, k * CAP:(k + 1) * CAP],
                                             start=(k == 0), stop=(k == NCB - 1))
                        return last
                    d = [tok_xbt[e], tok_wld[e]]
                    if t_se is not None:
                        d += [t_se, t_ve]
                    t_te = step("tensor", _ge, deps=d)
                    t_se = step("scalar", lambda se: se.activation(
                        out=hgs[:], in_=ps_a[:], func=AF.Silu), deps=[t_te])
                    t_ve = step("vector", lambda ve, m=m: ve.tensor_tensor(
                        out=hbe[:, m * CAP:(m + 1) * CAP], in0=hgs[:], in1=ps_b[:],
                        op=ALU.mult), deps=[t_te, t_se])

                t_hdone = t_ve
                first_ev = True
                for m in range(NCB):
                    def _de(te, m=m, par=par):
                        for k in range(6):
                            wt = wdb[par][:, k * D + m * 128: k * D + (m + 1) * 128]
                            te.ldweights(wt)
                            last = te.matmul(ps_c[:], wt, hbe[:, k * CAP:(k + 1) * CAP],
                                             start=(k == 0), stop=(k == 5))
                        return last
                    d = [t_hdone]
                    if t_ev is not None:
                        d.append(t_ev)
                    t_te = step("tensor", _de, deps=d)
                    dev = [t_te]
                    if first_ev and e >= 2:
                        dev.append(tok_ybst[e - 2])
                        first_ev = False
                    t_ev = step("scalar", lambda se, m=m, par=par: se.activation(
                        out=ycmb[par][:, m * CAP:(m + 1) * CAP], in_=ps_c[:], func=AF.Copy),
                        deps=dev)
                tok_free_w[e] = t_te
                if e + 2 < EPC:
                    tok_wld[e + 2] = step("gpsimd", _mk_wld(e + 2, par), deps=[t_te], dma=3)

                d = [t_ev]
                if e >= 2 and e - 2 in tok_ybst:
                    d.append(tok_ytr_all[e - 2])
                t_ybst = step("sync", lambda sy, par=par: sy.dma_start(
                    out=yb_dram.ap().rearrange("(q m p) c -> q p m c", q=2, p=128)[par],
                    in_=ycmb[par][:].rearrange("p (m c) -> p m c", m=NCB)), deps=d, dma=1)
                tok_ybst[e] = t_ybst
                t_last = None
                for g_i in range(4):
                    yp = g_i % 2
                    d = [t_ybst]
                    if g_i >= 2:
                        d.append(tok_ysl[g_i - 2])
                    elif e >= 1:
                        d.append(tok_ys[e - 1])
                    t_tr = step("sync", lambda sy, g_i=g_i, par=par, yp=yp: sy.dma_start(
                        out=yt[yp][:], in_=yb_dram.ap()[par * D:(par + 1) * D,
                                                        g_i * 128:(g_i + 1) * 128],
                        transpose=True), deps=d, dma=1)
                    t_w = step("vector", lambda ve, e=e, g_i=g_i, yp=yp: ve.tensor_scalar(
                        out=ys[yp][:], in0=yt[yp][:], scalar1=meta_g[e * 4 + g_i][:, 1:2],
                        scalar2=None, op0=ALU.mult), deps=[t_tr])
                    t_sl = step("sync", lambda sy, e=e, g_i=g_i, yp=yp: sy.dma_start(
                        out=slot_out.ap()[e * CAP + g_i * 128: e * CAP + (g_i + 1) * 128, :],
                        in_=ys[yp][:]), deps=[t_w], dma=1)
                    if g_i == 0:
                        tok_ysl = {}
                    tok_ysl[g_i] = t_sl
                    t_last = t_sl
                    slot_toks.append(t_sl)
                tok_ytr_all[e] = t_last
                tok_ys[e] = t_last

            tok_slot_ag = step("gpsimd", lambda g: g.collective_compute(
                "AllGather", ALU.bypass, replica_groups=RG,
                ins=[slot_out.ap().opt()], outs=[slot_rep.ap().opt()]),
                deps=slot_toks[-1:] + [slot_toks[-5]], cc=True)

        with ExitStack() as S:
          if nphases >= 6:
            gbuf = [sbuf(S, "p6_gb%d" % p_, [128, TK * D], BF16) for p_ in range(2)]
            p6r = [sbuf(S, "p6_r%d" % p_, [128, D], F32) for p_ in range(2)]
            sA_s = sbuf(S, "p6_sA", [128, D], F32)
            sB_s = sbuf(S, "p6_sB", [128, D], F32)
            sA = [sA_s, sA_s]
            sB = [sB_s, sB_s]
            p6rc = [sbuf(S, "p6_rc%d" % p_, [128, D], F32) for p_ in range(2)]
            ps_t = [psum(S, "p6_pst%d" % p_, [128, 4 * 128]) for p_ in range(2)]
            rview = routed_dram.ap().rearrange("(a p) c -> a p c", p=128)
            rcm_v = routedcm_dram.ap().rearrange("(cb p) n -> p cb n", p=128)
            fc = [sbuf(S, "p6_f%d" % p_, [128, NCB * TCH], F32) for p_ in range(2)]
            sc_s = sbuf(S, "p6_s", [128, NCB * TCH], F32)
            sc_ = [sc_s, sc_s]
            x2c_s = sbuf(S, "p6_x2", [128, NCB * TCH], F32)
            x2c = [x2c_s, x2c_s]
            r5c = sbuf(S, "p6_r5", [128, NCB * TCH], F32)
            P = ln_tiles(S, "p6")
            y_v = ycm.rearrange("(cb p) n -> p cb n", p=128)
            sh_v = shared_dram.ap().rearrange("(cb p) n -> p cb n", p=128)
            f_tok = {}
            ya_tok = {}
            yst_tok = {}
            ln5_tok = {}

            def emit_final_chunk(i):
                par = i % 2
                d = [st_toks[4 * i + jj] for jj in range(4)]
                if i >= 2:
                    d += ln5_tok[i - 2]
                tf_ = step("sync", lambda sy, i=i, par=par: sy.dma_start(
                    out=fc[par][:].rearrange("p (cb n) -> p cb n", cb=NCB),
                    in_=rcm_v[:, :, i * TCH:(i + 1) * TCH]), deps=d, dma=1)
                d = [p4_sh_toks[-1]]
                if i >= 1:
                    d.append(f_tok[i - 1])
                tss_ = step("sync", lambda sy, i=i, par=par: sy.dma_start(
                    out=sc_[par][:].rearrange("p (cb n) -> p cb n", cb=NCB),
                    in_=sh_v[:, :, i * TCH:(i + 1) * TCH]), deps=d, dma=1)
                d = [] if i < 1 else [yst_tok[i - 1]]
                tx2_ = step("sync", lambda sy, i=i, par=par: sy.dma_start(
                    out=x2c[par][:].rearrange("p (cb n) -> p cb n", cb=NCB),
                    in_=x2_v[:, :, i * TCH:(i + 1) * TCH]), deps=d, dma=1)

                def _fadd(ve, par=par):
                    for cb in range(NCB):
                        last = ve.tensor_tensor(out=fc[par][:, cb * TCH:(cb + 1) * TCH],
                                                in0=fc[par][:, cb * TCH:(cb + 1) * TCH],
                                                in1=sc_[par][:, cb * TCH:(cb + 1) * TCH],
                                                op=ALU.add)
                    return last
                t_fa = step("vector", _fadd, deps=[tf_, tss_])
                f_tok[i] = t_fa
                ow = [ya_tok[i - 1]] if i >= 1 else []
                ln5 = emit_ln_tok(P, fc[par][:], r5c[:], [t_fa], out_war=ow)
                ln5_tok[i] = ln5

                def _yadd(ve, par=par):
                    for cb in range(NCB):
                        last = ve.tensor_tensor(out=x2c[par][:, cb * TCH:(cb + 1) * TCH],
                                                in0=x2c[par][:, cb * TCH:(cb + 1) * TCH],
                                                in1=r5c[:, cb * TCH:(cb + 1) * TCH],
                                                op=ALU.add)
                    return last
                ya = step("vector", _yadd, deps=ln5 + [tx2_])
                ya_tok[i] = ya
                yst_tok[i] = step("sync", lambda sy, i=i, par=par: sy.dma_start(
                    out=y_v[:, :, i * TCH:(i + 1) * TCH],
                    in_=x2c[par][:].rearrange("p (cb n) -> p cb n", cb=NCB)),
                    deps=[ya], dma=1)

            p6_entry = now()
            tr_tok = {}
            ev_tok = {}
            st_toks = {}
            sum_tok = {}
            glist = []
            for i in range(NTILE):
                par = i % 2
                if i < 2:
                    t_gz = step("vector", lambda ve, par=par: ve.memset(
                        gbuf[par][:], 0.0), deps=[])
                else:
                    d = [sum_tok[i - 2][0], sum_tok[i - 2][1]]
                    t_gz = step("scalar", lambda se, par=par: se.activation(
                        out=gbuf[par][:], in_=gbuf[par][:], func=AF.Identity,
                        scale=c_zero[:], bias=c_zero[:]), deps=d)
                gt = []
                for k in range(TK):
                    d = [t_gz, tok_slot_ag]
                    if len(glist) >= 8:
                        d.append(glist[-8])
                    t_g = step("gpsimd", lambda g, i=i, k=k, par=par: g.indirect_dma_start(
                        out=gbuf[par][:, k * D:(k + 1) * D],
                        out_offset=None,
                        in_=slot_rep.ap(),
                        in_offset=IndirectOffsetOnAxis(
                            ap=p_idxi[:, i * TK + k: i * TK + k + 1], axis=0),
                        bounds_check=breg[0], oob_is_err=False), deps=d, dma=1)
                    gt.append(t_g)
                    glist.append(t_g)

                def _sa(ve, par=par):
                    ve.tensor_tensor(out=sA[par][:], in0=gbuf[par][:, 0:D],
                                     in1=gbuf[par][:, D:2 * D], op=ALU.add)
                    ve.tensor_tensor(out=sA[par][:], in0=sA[par][:],
                                     in1=gbuf[par][:, 2 * D:3 * D], op=ALU.add)
                    return ve.tensor_tensor(out=sA[par][:], in0=sA[par][:],
                                            in1=gbuf[par][:, 3 * D:4 * D], op=ALU.add)
                d = gt[0:4] + ([sum_tok[i - 1][2]] if i >= 1 else [])
                t_sa = step("vector", _sa, deps=d)

                def _sb(ve, par=par):
                    ve.tensor_tensor(out=sB[par][:], in0=gbuf[par][:, 4 * D:5 * D],
                                     in1=gbuf[par][:, 5 * D:6 * D], op=ALU.add)
                    ve.tensor_tensor(out=sB[par][:], in0=sB[par][:],
                                     in1=gbuf[par][:, 6 * D:7 * D], op=ALU.add)
                    return ve.tensor_tensor(out=sB[par][:], in0=sB[par][:],
                                            in1=gbuf[par][:, 7 * D:8 * D], op=ALU.add)
                d = gt[4:8] + ([sum_tok[i - 1][2]] if i >= 1 else [])
                t_sb = step("vector", _sb, deps=d)
                d = [t_sa, t_sb]
                if i >= 2:
                    d.append(tr_tok[i - 2])
                t_sum = step("vector", lambda ve, par=par: ve.tensor_tensor(
                    out=p6r[par][:], in0=sA[par][:], in1=sB[par][:], op=ALU.add), deps=d)
                sum_tok[i] = (t_sa, t_sb, t_sum)
                if t_routed is not None:
                    step("sync", lambda sy, i=i, par=par: sy.dma_start(
                        out=rview[i], in_=p6r[par][:]), deps=[t_sum], dma=1)

                def _tr4(te, i=i, par=par, grp=0):
                    for cb in range(4 * grp, 4 * grp + 4):
                        last = te.transpose(
                            ps_t[par][:, (cb % 4) * 128:((cb % 4) + 1) * 128],
                            p6r[par][:, cb * 128:(cb + 1) * 128], c_idf[:])
                    return last

                def _tev4(se, i=i, par=par, grp=0):
                    for cb in range(4 * grp, 4 * grp + 4):
                        last = se.activation(
                            out=p6rc[par][:, cb * 128:(cb + 1) * 128],
                            in_=ps_t[par][:, (cb % 4) * 128:((cb % 4) + 1) * 128],
                            func=AF.Copy)
                    return last
                tprev_ev = ev_tok.get(i - 2)
                for grp in range(2):
                    d = [t_sum]
                    if tprev_ev is not None:
                        d.append(tprev_ev)
                    t_t = step("tensor", lambda te, i=i, par=par, grp=grp: _tr4(te, i, par, grp),
                               deps=d)
                    dev = [t_t]
                    if grp == 0 and i >= 2:
                        dev.append(st_toks[i - 2])
                    tprev_ev = step("scalar", lambda se, i=i, par=par, grp=grp: _tev4(se, i, par, grp),
                                    deps=dev)
                tr_tok[i] = t_t
                ev_tok[i] = tprev_ev
                st_toks[i] = step("sync", lambda sy, i=i, par=par: sy.dma_start(
                    out=rcm_v[:, :, i * 128:(i + 1) * 128],
                    in_=p6rc[par][:].rearrange("p (cb n) -> p cb n", cb=NCB)),
                    deps=[tprev_ev], dma=1)
                if i % 4 == 3:
                    emit_final_chunk(i // 4)

        def _taps(sy):
            out = []
            for t_ap, srcap in [(t_x1, x1_dram.ap()), (t_ug, ug_a2a_out.ap()),
                                (t_hs, hs_a2a_out.ap()), (t_x2, x2_dram.ap()),
                                (t_x3, x3cm_dram.ap()), (t_meta, meta_out.ap()),
                                (t_routed, routed_dram.ap()), (t_shared, shared_dram.ap())]:
                if t_ap is not None:
                    out.append(sy.dma_start(out=t_ap, in_=srcap))
            if t_wts is not None:
                out.append(sy.dma_start(out=t_wts, in_=p_wts[:]))
            if t_idx is not None:
                out.append(sy.dma_start(out=t_idx, in_=p_idxi[:]))
            return out
        ntap = sum(x is not None for x in
                   (t_x1, t_ug, t_hs, t_x2, t_x3, t_meta, t_routed, t_shared, t_wts, t_idx))
        if ntap:
            sstep("sync", _taps, dma=ntap)

        fin_deps = now()

        @block.sync
        def _(sy):
            for s, v in fin_deps:
                sy.wait_ge(SEMQ[s], v)

    return nc, tap_aps


def _mk_inputs(L0_attn_x, L0_attn_gamma, L0_attn_beta, L0_attn_A, L0_attn_W_in,
               L0_attn_b_in, L0_attn_W_out, L0_attn_b_out, L0_ffn_router_w,
               L0_ffn_w_gate, L0_ffn_w_up, L0_ffn_w_down, L0_ffn_ws_gate_up,
               L0_ffn_ws_down):
    f32 = np.float32
    bf16 = ml_dtypes.bfloat16
    x = np.asarray(L0_attn_x, f32)
    gamma = np.asarray(L0_attn_gamma, f32)
    beta = np.asarray(L0_attn_beta, f32)
    A = np.asarray(L0_attn_A, f32)
    W_in = np.ascontiguousarray(np.asarray(L0_attn_W_in, f32))
    b_in = np.asarray(L0_attn_b_in, f32)
    W_out = np.ascontiguousarray(np.asarray(L0_attn_W_out, f32))
    b_out = np.asarray(L0_attn_b_out, f32)
    router_w = np.asarray(L0_ffn_router_w, f32)
    w_gate = np.asarray(L0_ffn_w_gate, f32)
    w_up = np.asarray(L0_ffn_w_up, f32)
    w_down = np.asarray(L0_ffn_w_down, f32)
    ws_gate_up = np.ascontiguousarray(np.asarray(L0_ffn_ws_gate_up, f32))
    ws_down = np.ascontiguousarray(np.asarray(L0_ffn_ws_down, f32))

    ones_bf = np.ones((128, 128), bf16)
    ones_f = np.ones((128, 128), f32)
    lt = (np.arange(128)[:, None] < np.arange(128)[None, :]).astype(f32)
    ident = np.eye(128, dtype=f32)
    iota64_ = np.tile(np.arange(E, dtype=f32), (128, 1))
    detie_ = (-1e-6 * np.arange(E, dtype=f32)).astype(f32)
    detie_ = np.tile(detie_, (128, 1))
    iota128_ = np.arange(128, dtype=f32).reshape(128, 1)

    in_maps = []
    for j in range(NCORES):
        xs = x[:, j * LPC:(j + 1) * LPC, :].reshape(B * LPC, D)
        xcm_j = np.ascontiguousarray(xs.T)
        presel_j = np.zeros((16, 2), f32)
        for bp in range(B):
            for jp in range(NCORES):
                for b_i in range(B):
                    if bp * NCORES + jp < b_i * NCORES + j:
                        presel_j[2 * jp + bp, b_i] = 1.0
        d = {
            "xcm": xcm_j,
            "g8": np.ascontiguousarray(gamma.reshape(NCB, 128).T),
            "b8": np.ascontiguousarray(beta.reshape(NCB, 128).T),
            "bo8": np.ascontiguousarray(b_out.reshape(NCB, 128).T),
            "acol": np.ascontiguousarray(A[j * 128:(j + 1) * 128].reshape(128, 1)),
            "w_in": W_in,
            "w_out": W_out,
            "rwt": np.ascontiguousarray(router_w.T),
            "wsgu": ws_gate_up,
            "wsd": ws_down,
            "wg_all": np.ascontiguousarray(w_gate[j * EPC:(j + 1) * EPC].reshape(EPC * D, FF)),
            "wu_all": np.ascontiguousarray(w_up[j * EPC:(j + 1) * EPC].reshape(EPC * D, FF)),
            "wd_all": np.ascontiguousarray(w_down[j * EPC:(j + 1) * EPC].reshape(EPC * FF, D)),
            "ones_bf": ones_bf,
            "ones_f": ones_f,
            "lt_strict": lt,
            "ident_f": ident,
            "iota64": iota64_,
            "detie": detie_,
            "iota128": iota128_,
            "rowbase": np.full((128, 1), j * TPC, f32),
            "slotb": np.full((128, 1), j * EPC * CAP, f32),
            "presel": presel_j,
            "ub8": np.ascontiguousarray(b_in[:D].reshape(NCB, 128).T),
            "epsc": np.full((128, 1), LN_EPS, f32),
            "zerc": np.zeros((128, 1), f32),
            "gb8": np.ascontiguousarray(b_in[D:].reshape(NCB, 128).T),
        }
        in_maps.append(d)
    return in_maps


def kernel(**inputs):
    in_maps = _mk_inputs(**inputs)
    nc, _ = build_nc()
    res = run_bass_kernel_spmd(nc, in_maps, core_ids=list(range(NCORES)))
    y = np.empty((B, L, D), np.float32)
    for j in range(NCORES):
        blk = res.results[j]["ycm"]
        y[:, j * LPC:(j + 1) * LPC, :] = blk.T.reshape(B, LPC, D)
    return y


# version 23 (final)
# speedup vs baseline: 1.2434x; 1.0546x over previous
import os
import numpy as np
import ml_dtypes
from contextlib import ExitStack
import concourse.bass as bass
import concourse.mybir as mybir
from concourse.bass_utils import run_bass_kernel_spmd
from concourse.bass_core_rust import IndirectOffsetOnAxis
from concourse.aot_env import allow_ici_all_to_all

F32 = mybir.dt.float32
BF16 = mybir.dt.bfloat16
I32 = mybir.dt.int32
U32 = mybir.dt.uint32
AF = mybir.ActivationFunctionType
ALU = mybir.AluOpType

NCORES = 8
B = 2
L = 16384
D = 1024
DD = 2048
TPC = 4096
LPC = 2048
NCB = 8
E = 64
TK = 8
FF = 768
CAP = 512
SF = 768
EPC = 8
TCH = 512
NTCH = TPC // TCH
LCH = 1024
NLCH = L // LCH
NTILE = TPC // 128
LN_EPS = 1e-5
BIGIDX = 1.0e9
RG = [list(range(NCORES))]
INV_D = 1.0 / D
SWEEPS = ("P", "N", "N", "N", "N")


def build_nc(taps=(), nphases=6):
    nc = bass.Bass("TRN2", target_bir_lowering=False)
    dt_ = nc.dram_tensor

    xcm = dt_("xcm", [D, TPC], F32, kind="ExternalInput").ap()
    g8 = dt_("g8", [128, NCB], F32, kind="ExternalInput").ap()
    b8 = dt_("b8", [128, NCB], F32, kind="ExternalInput").ap()
    bo8 = dt_("bo8", [128, NCB], F32, kind="ExternalInput").ap()
    acol = dt_("acol", [128, 1], F32, kind="ExternalInput").ap()
    w_in = dt_("w_in", [D, DD], F32, kind="ExternalInput").ap()
    w_out = dt_("w_out", [D, D], F32, kind="ExternalInput").ap()
    rwt = dt_("rwt", [D, E], F32, kind="ExternalInput").ap()
    wsgu = dt_("wsgu", [D, 2 * SF], F32, kind="ExternalInput").ap()
    wsd = dt_("wsd", [SF, D], F32, kind="ExternalInput").ap()
    wg_all = dt_("wg_all", [EPC * D, FF], F32, kind="ExternalInput").ap()
    wu_all = dt_("wu_all", [EPC * D, FF], F32, kind="ExternalInput").ap()
    wd_all = dt_("wd_all", [EPC * FF, D], F32, kind="ExternalInput").ap()
    ones_bf = dt_("ones_bf", [128, 128], BF16, kind="ExternalInput").ap()
    ones_f = dt_("ones_f", [128, 128], F32, kind="ExternalInput").ap()
    lt_strict = dt_("lt_strict", [128, 128], F32, kind="ExternalInput").ap()
    ident_f = dt_("ident_f", [128, 128], F32, kind="ExternalInput").ap()
    iota64 = dt_("iota64", [128, E], F32, kind="ExternalInput").ap()
    detie = dt_("detie", [128, E], F32, kind="ExternalInput").ap()
    iota128 = dt_("iota128", [128, 1], F32, kind="ExternalInput").ap()
    rowbase = dt_("rowbase", [128, 1], F32, kind="ExternalInput").ap()
    slotb = dt_("slotb", [128, 1], F32, kind="ExternalInput").ap()
    presel = dt_("presel", [16, 2], F32, kind="ExternalInput").ap()
    ub8 = dt_("ub8", [128, NCB], F32, kind="ExternalInput").ap()
    epsc = dt_("epsc", [128, 1], F32, kind="ExternalInput").ap()
    zerc = dt_("zerc", [128, 1], F32, kind="ExternalInput").ap()
    gb8 = dt_("gb8", [128, NCB], F32, kind="ExternalInput").ap()

    ycm = dt_("ycm", [D, TPC], F32, kind="ExternalOutput").ap()

    x1_dram = dt_("x1_dram", [D, TPC], F32, kind="Internal")
    x2_dram = dt_("x2_dram", [D, TPC], F32, kind="Internal")
    x3cm_dram = dt_("x3cm_dram", [D, TPC], BF16, kind="Internal")
    shared_dram = dt_("shared_dram", [D, TPC], F32, kind="Internal")
    routed_dram = dt_("routed_dram", [TPC, D], F32, kind="Internal")
    routedcm_dram = dt_("routedcm_dram", [D, TPC], F32, kind="Internal")
    xb_dram = dt_("xb_dram", [EPC * CAP, D], BF16, kind="Internal")
    yb_dram = dt_("yb_dram", [2 * D, CAP], BF16, kind="Internal")
    ug_a2a_in = dt_("ug_a2a_in", [NCORES * 256, TPC], F32, kind="Internal")
    ug_a2a_out = dt_("ug_a2a_out", [NCORES * 256, TPC], F32, kind="Internal")
    hs_a2a_in = dt_("hs_a2a_in", [NCORES * 128, TPC], BF16, kind="Internal")
    hs_a2a_out = dt_("hs_a2a_out", [NCORES * 128, TPC], BF16, kind="Internal")
    x3row_dram = dt_("x3row_dram", [TPC, D], BF16, kind="Internal")
    x3rep = dt_("x3rep", [NCORES * TPC, D], BF16, kind="Internal", addr_space="Shared")
    slot_out = dt_("slot_out", [EPC * CAP, D], BF16, kind="Internal")
    slot_rep = dt_("slot_rep", [E * CAP, D], BF16, kind="Internal", addr_space="Shared")
    meta_in = dt_("meta_in", [E * CAP, 2], F32, kind="Internal")
    meta_out = dt_("meta_out", [E * CAP, 2], F32, kind="Internal")
    cnt_in = dt_("cnt_in", [2, E], F32, kind="Internal")
    cnt_out = dt_("cnt_out", [2 * NCORES, E], F32, kind="Internal", addr_space="Shared")

    tap_aps = {}

    def tap(name, shape, dtype):
        if name in taps:
            tap_aps[name] = dt_("tap_" + name, shape, dtype, kind="ExternalOutput").ap()
            return tap_aps[name]
        return None

    t_x1 = tap("x1", [D, TPC], F32)
    t_ug = tap("ug", [NCORES * 256, TPC], F32)
    t_hs = tap("hs", [NCORES * 128, TPC], BF16)
    t_x2 = tap("x2", [D, TPC], F32)
    t_x3 = tap("x3", [D, TPC], BF16)
    t_eid = tap("eid", [128, NTILE * TK], F32)
    t_wts = tap("wts", [128, NTILE * TK], F32)
    t_idx = tap("idx", [128, NTILE * TK], I32)
    t_meta = tap("meta", [E * CAP, 2], F32)
    t_routed = tap("routed", [TPC, D], F32)
    t_shared = tap("shared", [D, TPC], F32)

    with ExitStack() as OUTER:
        block = OUTER.enter_context(nc.Block())
        SEMQ = {}
        for nm in ("TE", "VE", "SE", "PO", "SY", "GD", "SC", "CC", "G0", "G1", "G2", "G3"):
            SEMQ[nm] = OUTER.enter_context(nc.semaphore("s" + nm))
        CNT = {nm: 0 for nm in SEMQ}
        WAITED = {e: {nm: 0 for nm in SEMQ} for e in
                  ("tensor", "vector", "scalar", "gpsimd", "sync")}
        QSEL = {"tensor": None, "vector": None, "scalar": None, "gpsimd": None,
                "sync": None}

        def _engsel(q):
            return {"tensor": block.tensor, "vector": block.vector,
                    "scalar": block.scalar, "gpsimd": block.gpsimd,
                    "sync": block.sync}[q]

        ENG_SEM = {"tensor": "TE", "vector": "VE", "scalar": "SE",
                   "gpsimd": "PO", "sync": "SY"}

        def step(queue, fn, deps=(), dma=0, sem=None, cc=False):
            mysem = sem or (("GD" if queue == "gpsimd" else
                             "SC" if queue == "scalar" else "SY")
                            if dma else ENG_SEM[queue])
            if cc:
                mysem = "CC"
            w = WAITED[queue]
            dep_list = [(s, v) for (s, v) in deps if v > w[s]]
            for s, v in dep_list:
                w[s] = max(w[s], v)
            if dma:
                CNT[mysem] += 16 * dma
            elif cc:
                CNT[mysem] += 1
            else:
                CNT[mysem] += 1
            after = CNT[mysem]
            handle = SEMQ[mysem]
            inc_each = 16 if dma else 1

            @_engsel(queue)
            def _(eng):
                for s, v in dep_list:
                    eng.wait_ge(SEMQ[s], v)
                if dma:
                    insts = fn(eng)
                    if not isinstance(insts, (list, tuple)):
                        insts = [insts]
                    assert len(insts) == dma, f"dma count {len(insts)} != {dma}"
                    for i_ in insts:
                        i_.then_inc(handle, 16)
                else:
                    last = fn(eng)
                    if last is None:
                        last = eng.nop()
                    last.then_inc(handle, 1)
            return (mysem, after)

        ALLQ = ("TE", "VE", "SE", "PO", "SY", "GD", "SC", "CC", "G0", "G1", "G2", "G3")

        def now():
            return [(s, CNT[s]) for s in ALLQ if CNT[s] > 0]

        def sstep(queue, fn, dma=0, cc=False, exclude=(), extra=()):
            deps = [t for t in now() if t[0] not in exclude] + list(extra)
            return step(queue, fn, deps=deps, dma=dma, cc=cc)

        def vseq(fns):
            for f in fns:
                sstep("vector", f)

        def sbuf(stk, name, shape, dtype):
            return stk.enter_context(nc.sbuf_tensor(name, shape, dtype))

        def psum(stk, name, shape, dtype=F32):
            return stk.enter_context(nc.psum_tensor(name, shape, dtype))

        c_onesbf = sbuf(OUTER, "c_onesbf", [128, 128], BF16)
        c_onesf = sbuf(OUTER, "c_onesf", [128, 128], F32)
        c_lt = sbuf(OUTER, "c_lt", [128, 128], F32)
        c_idf = sbuf(OUTER, "c_idf", [128, 128], F32)
        c_iota64 = sbuf(OUTER, "c_iota64", [128, E], F32)
        c_detie = sbuf(OUTER, "c_detie", [128, E], F32)
        c_iota128 = sbuf(OUTER, "c_iota128", [128, 1], F32)
        c_rowbase = sbuf(OUTER, "c_rowbase", [128, 1], F32)
        c_slotb = sbuf(OUTER, "c_slotb", [128, 1], F32)
        c_g8 = sbuf(OUTER, "c_g8", [128, NCB], F32)
        c_b8 = sbuf(OUTER, "c_b8", [128, NCB], F32)
        c_bo8 = sbuf(OUTER, "c_bo8", [128, NCB], F32)
        c_acol = sbuf(OUTER, "c_acol", [128, 1], F32)
        c_presel = sbuf(OUTER, "c_presel", [16, 2], F32)
        c_ub8 = sbuf(OUTER, "c_ub8", [128, NCB], F32)
        c_eps = sbuf(OUTER, "c_eps", [128, 1], F32)
        c_zero = sbuf(OUTER, "c_zero", [128, 1], F32)
        c_gb8 = sbuf(OUTER, "c_gb8", [128, NCB], F32)
        p_idxi = sbuf(OUTER, "p_idxi", [128, NTILE * TK], I32)
        p_wts = sbuf(OUTER, "p_wts", [128, NTILE * TK], F32)
        p_zero = sbuf(OUTER, "p_zero", [128, 512], F32)

        def _loads(sy):
            return [sy.dma_start(out=dst[:], in_=src)
                    for dst, src in [(c_onesbf, ones_bf), (c_onesf, ones_f),
                                     (c_lt, lt_strict), (c_idf, ident_f),
                                     (c_iota64, iota64), (c_detie, detie),
                                     (c_iota128, iota128), (c_rowbase, rowbase),
                                     (c_slotb, slotb), (c_g8, g8), (c_b8, b8),
                                     (c_bo8, bo8), (c_acol, acol), (c_presel, presel),
                                     (c_ub8, ub8), (c_gb8, gb8), (c_eps, epsc),
                                     (c_zero, zerc)]]
        sstep("sync", _loads, dma=18)
        breg = []

        def _breg(g):
            r = g.alloc_register(name="bnd")
            g.reg_mov(r, E * CAP - 1)
            breg.append(r)
            return g.nop()
        sstep("gpsimd", _breg)
        sstep("vector", lambda v: v.memset(p_zero[:], 0.0))
        sstep("sync", lambda sy: sy.dma_start(
            out=meta_in.ap().rearrange("(p r) c -> p (r c)", p=128), in_=p_zero[:, 0:512]),
            dma=1)



        def emit_ln_tok(P, xap, outap, dep_in, out_war=()):
            bf, sq = P["bf"], P["sq"]
            q_ = P["rot"][0]
            P["rot"][0] = 1 - q_
            mu, m2 = P["mu"][q_], P["m2"]
            var, rs, scr = P["var"], P["rs"][q_], P["scr"]
            ps1, ps2 = P["ps1"], P["ps2"]
            t_cast = step("scalar", lambda se: se.activation(out=bf[:], in_=xap, func=AF.Copy),
                          deps=list(dep_in) + P.get("war_bf", []))
            t_sq = step("vector", lambda ve: ve.tensor_tensor(out=sq[:], in0=bf[:],
                                                              in1=bf[:], op=ALU.mult),
                        deps=[t_cast])

            def _mm(te):
                te.ldweights(c_onesbf[:])
                for cb in range(NCB):
                    te.matmul(ps1[:], c_onesbf[:], bf[:, cb * TCH:(cb + 1) * TCH],
                              start=(cb == 0), stop=(cb == NCB - 1))
                for cb in range(NCB):
                    last = te.matmul(ps2[:], c_onesbf[:], sq[:, cb * TCH:(cb + 1) * TCH],
                                     start=(cb == 0), stop=(cb == NCB - 1))
                return last
            t_mm = step("tensor", _mm, deps=[t_sq] + P.get("war_ps", []))

            def _m1(ve):
                ve.tensor_scalar(out=mu[:], in0=ps1[:], scalar1=INV_D, scalar2=None,
                                 op0=ALU.mult)
                ve.tensor_scalar(out=m2[:], in0=ps2[:], scalar1=INV_D, scalar2=None,
                                 op0=ALU.mult)
                ve.tensor_tensor(out=scr[:], in0=mu[:], in1=mu[:], op=ALU.mult)
                return ve.tensor_tensor(out=var[:], in0=m2[:], in1=scr[:], op=ALU.subtract)
            t_m1 = step("vector", _m1, deps=[t_mm] + P["warq"].get(q_, []) +
                        P.get("war_v", []))
            t_sqrt = step("scalar", lambda se: se.activation(out=scr[:], in_=var[:],
                                                             func=AF.Sqrt, bias=c_eps[:]),
                          deps=[t_m1])
            t_rs = step("vector", lambda ve: ve.reciprocal(out=rs[:], in_=scr[:]),
                        deps=[t_sqrt])

            def _apv(ve):
                for cb in range(5):
                    xi = xap[:, cb * TCH:(cb + 1) * TCH]
                    oi = outap[:, cb * TCH:(cb + 1) * TCH]
                    ve.tensor_tensor(out=oi, in0=xi, in1=mu[:], op=ALU.subtract)
                    last = ve.tensor_tensor(out=oi, in0=oi, in1=rs[:], op=ALU.mult)
                return last

            def _app(po):
                for cb in range(5, NCB):
                    xi = xap[:, cb * TCH:(cb + 1) * TCH]
                    oi = outap[:, cb * TCH:(cb + 1) * TCH]
                    po.tensor_tensor(out=oi, in0=xi, in1=mu[:], op=ALU.subtract)
                    last = po.tensor_tensor(out=oi, in0=oi, in1=rs[:], op=ALU.mult)
                return last
            t_apv = step("vector", _apv, deps=[t_rs] + list(out_war))
            t_app = step("gpsimd", _app, deps=[t_rs] + list(out_war))
            P["war_bf"] = [t_mm]
            P["war_ps"] = [t_m1]
            P["war_v"] = [t_rs]
            P["warq"][q_] = [t_apv, t_app]
            return [t_apv, t_app]

        def emit_ln(P, xap, outap):
            bf, sq = P["bf"], P["sq"]
            mu, m2, var, rs, scr = P["mu"], P["m2"], P["var"], P["rs"], P["scr"]
            ps1, ps2 = P["ps1"], P["ps2"]

            def _cast(se):
                se.activation(out=bf[:], in_=xap, func=AF.Copy)
                return se.activation(out=sq[:], in_=bf[:], func=AF.Square)
            sstep("scalar", _cast)

            def _mm(te):
                te.ldweights(c_onesbf[:])
                for cb in range(NCB):
                    te.matmul(ps1[:], c_onesbf[:], bf[:, cb * TCH:(cb + 1) * TCH],
                              start=(cb == 0), stop=(cb == NCB - 1))
                for cb in range(NCB):
                    last = te.matmul(ps2[:], c_onesbf[:], sq[:, cb * TCH:(cb + 1) * TCH],
                                     start=(cb == 0), stop=(cb == NCB - 1))
                return last
            sstep("tensor", _mm)

            def _m1(ve):
                ve.tensor_scalar(out=mu[:], in0=ps1[:], scalar1=INV_D, scalar2=None, op0=ALU.mult)
                ve.tensor_scalar(out=m2[:], in0=ps2[:], scalar1=INV_D, scalar2=None, op0=ALU.mult)
                ve.tensor_tensor(out=scr[:], in0=mu[:], in1=mu[:], op=ALU.mult)
                return ve.tensor_tensor(out=var[:], in0=m2[:], in1=scr[:], op=ALU.subtract)
            sstep("vector", _m1)
            sstep("scalar", lambda se: se.activation(out=scr[:], in_=var[:], func=AF.Sqrt,
                                                        bias=c_eps[:]))

            def _m2(ve):
                last = ve.reciprocal(out=rs[:], in_=scr[:])
                for cb in range(NCB):
                    xi = xap[:, cb * TCH:(cb + 1) * TCH]
                    oi = outap[:, cb * TCH:(cb + 1) * TCH]
                    ve.tensor_tensor(out=oi, in0=xi, in1=mu[:], op=ALU.subtract)
                    last = ve.tensor_tensor(out=oi, in0=oi, in1=rs[:], op=ALU.mult)
                return last
            sstep("vector", _m2)

        def ln_tiles(stk, pfx):
            return dict(
                bf=sbuf(stk, pfx + "_bf", [128, NCB * TCH], BF16),
                sq=sbuf(stk, pfx + "_sq", [128, NCB * TCH], BF16),
                mu=[sbuf(stk, pfx + "_mu%d" % q_, [128, TCH], F32) for q_ in range(2)],
                m2=sbuf(stk, pfx + "_m2", [128, TCH], F32),
                var=sbuf(stk, pfx + "_var", [128, TCH], F32),
                rs=[sbuf(stk, pfx + "_rs%d" % q_, [128, TCH], F32) for q_ in range(2)],
                scr=sbuf(stk, pfx + "_scr", [128, TCH], F32),
                ps1=psum(stk, pfx + "_ps1", [128, TCH]),
                ps2=psum(stk, pfx + "_ps2", [128, TCH]),
                rot=[0],
                warq={},
            )

        xcm_v = xcm.rearrange("(cb p) n -> p cb n", p=128)
        x1_v = x1_dram.ap().rearrange("(cb p) n -> p cb n", p=128)
        x2_v = x2_dram.ap().rearrange("(cb p) n -> p cb n", p=128)
        x3_v = x3cm_dram.ap().rearrange("(cb p) n -> p cb n", p=128)

        with ExitStack() as S:
          if nphases >= 1:
            x0c = [sbuf(S, "p1_x0%d" % p_, [128, NCB * TCH], F32) for p_ in range(2)]
            x1c = [sbuf(S, "p1_x1%d" % p_, [128, NCB * TCH], F32) for p_ in range(2)]
            znc = sbuf(S, "p1_zn", [128, NCB * TCH], F32)
            P = ln_tiles(S, "p1")
            zhalf = sbuf(S, "p1_z", [128, NCB * 4 * TCH], BF16)
            winb = sbuf(S, "p1_win", [128, NCB * DD], BF16)
            evb = [sbuf(S, "p1_ev%d" % p_, [128, 4 * TCH], F32) for p_ in range(2)]
            mmps = psum(S, "p1_mm", [128, 4 * TCH])

            t_win = step("gpsimd", lambda g: g.dma_start(
                out=winb[:].rearrange("p (k m) -> p k m", k=NCB),
                in_=w_in.rearrange("(k p) m -> p k m", p=128)), deps=now(), dma=1)

            ln1_done = {}
            ln2_done = {}
            aff_tok = {}
            ld_tok = {}
            ev_tok = {}
            st_tok = {}
            wm_last = {}
            for i in range(NTCH):
                par = i % 2
                d = list(now()) if i == 0 else []
                if i == 1:
                    d = [ld_tok[0]]
                if i >= 2:
                    d = ln1_done[i - 2] + ln2_done[i - 2]
                ld_tok[i] = step("sync", lambda sy, i=i, par=par: sy.dma_start(
                    out=x0c[par][:].rearrange("p (cb n) -> p cb n", cb=NCB),
                    in_=xcm_v[:, :, i * TCH:(i + 1) * TCH]), deps=d, dma=1)
                ow = ln2_done[i - 2] if i >= 2 else []
                ln1_done[i] = emit_ln_tok(P, x0c[par][:], x1c[par][:],
                                          [ld_tok[i]], out_war=ow)
                st_x1 = step("sync", lambda sy, i=i, par=par: sy.dma_start(
                    out=x1_v[:, :, i * TCH:(i + 1) * TCH],
                    in_=x1c[par][:].rearrange("p (cb n) -> p cb n", cb=NCB)),
                    deps=ln1_done[i], dma=1)
                q_ = 1 - P["rot"][0]
                varT, rsT = P["var"], P["rs"][q_]
                v2 = P["mu"][q_]
                t_v1 = step("vector", lambda ve: ve.tensor_tensor(
                    out=v2[:], in0=varT[:], in1=rsT[:], op=ALU.mult), deps=ln1_done[i])
                t_v2 = step("vector", lambda ve: ve.tensor_tensor(
                    out=v2[:], in0=v2[:], in1=rsT[:], op=ALU.mult), deps=[t_v1])
                t_sq2 = step("scalar", lambda se: se.activation(
                    out=v2[:], in_=v2[:], func=AF.Sqrt, bias=c_eps[:]), deps=[t_v2])
                t_rs2 = step("vector", lambda ve: ve.reciprocal(out=v2[:], in_=v2[:]),
                             deps=[t_sq2])

                def _zn(ve, par=par):
                    for cb in range(NCB):
                        last = ve.tensor_tensor(out=znc[:, cb * TCH:(cb + 1) * TCH],
                                                in0=x1c[par][:, cb * TCH:(cb + 1) * TCH],
                                                in1=v2[:], op=ALU.mult)
                    return last
                ow = [aff_tok[i - 1]] if i >= 1 else []
                t_znc = step("vector", _zn, deps=[t_rs2] + ow)
                ln2_done[i] = [t_znc]
                P["warq"][q_] = [t_znc]

                def _aff(se, i=i):
                    for cb in range(NCB):
                        last = se.activation(
                            out=zhalf[:, cb * 4 * TCH + (i % 4) * TCH: cb * 4 * TCH + (i % 4 + 1) * TCH],
                            in_=znc[:, cb * TCH:(cb + 1) * TCH], func=AF.Identity,
                            scale=c_g8[:, cb:cb + 1], bias=c_b8[:, cb:cb + 1])
                    return last
                d = list(ln2_done[i])
                if i >= 4:
                    d.append(wm_last[i // 4 - 1])
                aff_tok[i] = step("scalar", _aff, deps=d)

                if i % 4 == 3:
                    half = i // 4
                    zdeps = [aff_tok[j] for j in range(half * 4, half * 4 + 4)]
                    for m in range(16):
                        def _wmm(te, m=m):
                            for k in range(NCB):
                                wt = winb[:, k * DD + m * 128: k * DD + (m + 1) * 128]
                                te.ldweights(wt)
                                for c in range(4):
                                    last = te.matmul(
                                        mmps[:, c * TCH:(c + 1) * TCH], wt,
                                        zhalf[:, k * 4 * TCH + c * TCH: k * 4 * TCH + (c + 1) * TCH],
                                        start=(k == 0), stop=(k == NCB - 1))
                            return last
                        d = zdeps + [t_win]
                        if m >= 1:
                            d.append(ev_tok[(half, m - 1)])
                        elif half == 1:
                            d.append(ev_tok[(0, 15)])
                        t_wm = step("tensor", _wmm, deps=d)
                        fn = AF.Identity if m < 8 else AF.Sigmoid
                        bcol = c_ub8[:, m:m + 1] if m < 8 else c_gb8[:, m - 8:m - 7]

                        def _ev(se, fn=fn, bcol=bcol, par=m % 2):
                            for c in range(4):
                                last = se.activation(out=evb[par][:, c * TCH:(c + 1) * TCH],
                                                     in_=mmps[:, c * TCH:(c + 1) * TCH],
                                                     func=fn, bias=bcol)
                            return last
                        d = [t_wm]
                        if m >= 2:
                            d.append(st_tok[(half, m - 2)])
                        ev_tok[(half, m)] = step("scalar", _ev, deps=d)
                        r0 = 256 * m if m < 8 else 256 * (m - 8) + 128
                        st_tok[(half, m)] = step(
                            "sync", lambda sy, r0=r0, half=half, par=m % 2: sy.dma_start(
                                out=ug_a2a_in.ap()[r0:r0 + 128, half * 2048:(half + 1) * 2048],
                                in_=evb[par][:]), deps=[ev_tok[(half, m)]], dma=1)
                    wm_last[half] = t_wm

            def _cc_ug(g):
                with allow_ici_all_to_all():
                    return g.collective_compute(
                        "AllToAll", ALU.bypass, replica_groups=RG,
                        ins=[ug_a2a_in.ap().opt()], outs=[ug_a2a_out.ap().opt()])
            sstep("gpsimd", _cc_ug, cc=True)

        with ExitStack() as S:
          if nphases >= 2:
            u = [sbuf(S, "p2_u%d" % p_, [128, 2 * LCH], F32) for p_ in range(2)]
            g_ = [sbuf(S, "p2_g%d" % p_, [128, 2 * LCH], F32) for p_ in range(2)]
            gp = [sbuf(S, "p2_gp%d" % p_, [128, 2 * LCH], F32) for p_ in range(2)]
            d1 = [sbuf(S, "p2_d1%d" % p_, [128, 2 * LCH], F32) for p_ in range(2)]
            G1 = [sbuf(S, "p2_G1%d" % p_, [128, 2 * LCH], F32) for p_ in range(2)]
            h = sbuf(S, "p2_h", [128, 2 * LCH], F32)
            m_ = sbuf(S, "p2_m", [128, 2 * LCH], F32)
            w_ = sbuf(S, "p2_w", [128, 2 * LCH], F32)
            s_ = sbuf(S, "p2_s", [128, 2 * LCH], F32)
            qq = sbuf(S, "p2_q", [128, 2 * LCH], F32)
            aa = sbuf(S, "p2_a", [128, 2 * LCH], F32)
            rr = sbuf(S, "p2_r", [128, 2 * LCH], F32)
            c1 = sbuf(S, "p2_c1", [128, 2 * LCH], F32)
            c2 = sbuf(S, "p2_c2", [128, 2 * LCH], F32)
            cc = sbuf(S, "p2_cc", [128, 2 * LCH], F32)
            car = [sbuf(S, "p2_car0", [128, 1], F32), sbuf(S, "p2_car1", [128, 1], F32)]

            p2_entry = now()
            t_car = step("vector", lambda ve: (ve.memset(car[0][:], 0.0),
                                               ve.memset(car[1][:], 0.0))[1],
                         deps=p2_entry)
            ld_tok = {}
            free_tok = {}

            def _mk_ld(ci, par):
                j = ci // 2
                lo = (ci % 2) * LCH

                def _ld(sy):
                    out = []
                    for b_i in range(B):
                        out.append(sy.dma_start(
                            out=u[par][:, b_i * LCH:(b_i + 1) * LCH],
                            in_=ug_a2a_out.ap()[256 * j:256 * j + 128,
                                                b_i * 2048 + lo: b_i * 2048 + lo + LCH]))
                        out.append(sy.dma_start(
                            out=g_[par][:, b_i * LCH:(b_i + 1) * LCH],
                            in_=ug_a2a_out.ap()[256 * j + 128:256 * j + 256,
                                                b_i * 2048 + lo: b_i * 2048 + lo + LCH]))
                    return out
                return _ld

            ld_tok[0] = step("sync", _mk_ld(0, 0), deps=p2_entry, dma=4)
            ld_tok[1] = step("sync", _mk_ld(1, 1), deps=p2_entry, dma=4)

            pre_tok = {}
            t_scan = t_car
            t_m_last = None
            for ci in range(NLCH):
                par = ci % 2
                def _pre1(ve, par=par):
                    return ve.tensor_scalar(out=gp[par][:], in0=g_[par][:], scalar1=-1.0,
                                            scalar2=1.0, op0=ALU.mult, op1=ALU.add)
                d = [ld_tok[ci]]
                if ci >= 2:
                    d.append(free_tok[ci - 2])
                t_p1 = step("vector", _pre1, deps=d)
                t_p2 = step("vector", lambda ve, par=par: ve.tensor_scalar(
                    out=d1[par][:], in0=gp[par][:], scalar1=c_acol[:], scalar2=None,
                    op0=ALU.mult), deps=[t_p1])
                t_p3 = step("gpsimd", lambda po, par=par: po.tensor_tensor(
                    out=G1[par][:], in0=g_[par][:], in1=d1[par][:], op=ALU.add),
                    deps=[t_p2])
                pre_tok[ci] = t_p3

                t_ms = step("vector", lambda ve: ve.memset(h[:], 0.0), deps=[t_scan])

                for si, kind in enumerate(("P", "N", "N")):
                    def _m(ve, par=par, first=(si == 0)):
                        for b_i in range(B):
                            c0 = b_i * LCH
                            ve.scalar_tensor_tensor(out=m_[:, c0:c0 + 1], in0=car[b_i][:],
                                                    scalar=c_acol[:],
                                                    in1=u[par][:, c0:c0 + 1],
                                                    op0=ALU.mult, op1=ALU.add)
                            last = ve.scalar_tensor_tensor(
                                out=m_[:, c0 + 1:c0 + LCH], in0=h[:, c0:c0 + LCH - 1],
                                scalar=c_acol[:], in1=u[par][:, c0 + 1:c0 + LCH],
                                op0=ALU.mult, op1=ALU.add)
                        return last
                    d = [ld_tok[ci], t_ms if si == 0 else t_scan]
                    t_m = step("vector", _m, deps=d)
                    t_m_last = t_m
                    t_w = step("scalar", lambda se: se.activation(
                        out=w_[:], in_=m_[:], func=AF.Tanh), deps=[t_m])
                    if kind == "P":
                        t_c1 = step("gpsimd", lambda po, par=par: po.tensor_tensor(
                            out=cc[:], in0=gp[par][:], in1=w_[:], op=ALU.mult),
                            deps=[t_w, pre_tok[ci]])

                        def _scanp(ve, par=par):
                            for b_i in range(B):
                                c0 = b_i * LCH
                                last = ve.tensor_tensor_scan(
                                    out=h[:, c0:c0 + LCH], data0=g_[par][:, c0:c0 + LCH],
                                    data1=cc[:, c0:c0 + LCH], initial=car[b_i][:],
                                    op0=ALU.mult, op1=ALU.add)
                            return last
                        t_scan = step("vector", _scanp, deps=[t_c1])
                    else:
                        t_s = step("vector", lambda ve: ve.tensor_tensor(
                            out=s_[:], in0=w_[:], in1=w_[:], op=ALU.mult), deps=[t_w])
                        t_q = step("gpsimd", lambda po, par=par: po.tensor_tensor(
                            out=qq[:], in0=d1[par][:], in1=s_[:], op=ALU.mult),
                            deps=[t_s, pre_tok[ci]])
                        t_r = step("vector", lambda ve, par=par: ve.tensor_tensor(
                            out=rr[:], in0=d1[par][:], in1=qq[:], op=ALU.subtract),
                            deps=[t_q])
                        t_c1 = step("gpsimd", lambda po, par=par: po.tensor_tensor(
                            out=c1[:], in0=gp[par][:], in1=w_[:], op=ALU.mult),
                            deps=[t_w])

                        def _c2(ve):
                            for b_i in range(B):
                                c0 = b_i * LCH
                                ve.tensor_tensor(out=c2[:, c0:c0 + 1], in0=rr[:, c0:c0 + 1],
                                                 in1=car[b_i][:], op=ALU.mult)
                                last = ve.tensor_tensor(out=c2[:, c0 + 1:c0 + LCH],
                                                        in0=rr[:, c0 + 1:c0 + LCH],
                                                        in1=h[:, c0:c0 + LCH - 1],
                                                        op=ALU.mult)
                            return last
                        t_c2 = step("vector", _c2, deps=[t_r])
                        t_a = step("gpsimd", lambda po, par=par: po.tensor_tensor(
                            out=aa[:], in0=G1[par][:], in1=qq[:], op=ALU.subtract),
                            deps=[t_q])
                        t_c = step("vector", lambda ve: ve.tensor_tensor(
                            out=cc[:], in0=c1[:], in1=c2[:], op=ALU.subtract),
                            deps=[t_c2, t_c1])

                        def _scann(ve):
                            for b_i in range(B):
                                c0 = b_i * LCH
                                last = ve.tensor_tensor_scan(
                                    out=h[:, c0:c0 + LCH], data0=aa[:, c0:c0 + LCH],
                                    data1=cc[:, c0:c0 + LCH], initial=car[b_i][:],
                                    op0=ALU.mult, op1=ALU.add)
                            return last
                        t_scan = step("vector", _scann, deps=[t_a, t_c])

                free_tok[ci] = t_m_last
                free_tok[ci] = t_scan

                def _carry(ve):
                    ve.tensor_copy(car[0][:], h[:, LCH - 1:LCH])
                    return ve.tensor_copy(car[1][:], h[:, 2 * LCH - 1:2 * LCH])
                t_scan = step("vector", _carry, deps=[t_scan])

                def _hsout(g, ci=ci):
                    j = ci // 2
                    lo = (ci % 2) * LCH
                    return [g.dma_start(
                        out=hs_a2a_in.ap()[128 * j:128 * (j + 1),
                                           b_i * 2048 + lo: b_i * 2048 + lo + LCH],
                        in_=h[:, b_i * LCH:(b_i + 1) * LCH]) for b_i in range(B)]
                t_hsdma = step("gpsimd", _hsout, deps=[t_scan], dma=2)
                t_scan = t_scan
                hs_free = t_hsdma
                if ci + 2 < NLCH:
                    ld_tok[ci + 2] = step("sync", _mk_ld(ci + 2, par),
                                          deps=[free_tok[ci]], dma=4)

                t_scan = (t_scan[0], t_scan[1])
                last_hs = t_hsdma
                t_scan_chain = t_scan

                def _noop():
                    pass
                
                if ci < NLCH - 1:
                    t_scan = ("VE", t_scan_chain[1])
                    t_scan = step("vector", lambda ve: ve.nop(), deps=[t_scan_chain, last_hs])

            def _cc_hs(g):
                with allow_ici_all_to_all():
                    return g.collective_compute(
                        "AllToAll", ALU.bypass, replica_groups=RG,
                        ins=[hs_a2a_in.ap().opt()], outs=[hs_a2a_out.ap().opt()])
            sstep("gpsimd", _cc_hs, cc=True)

        with ExitStack() as S:
          if nphases >= 3:
            x3bf = sbuf(S, "p3_x3bf", [128, NCB * TPC], BF16)
            trt_s = sbuf(S, "p3_tr0", [128, D], BF16)
            trt = [trt_s, trt_s]
            rwtb = sbuf(S, "p3_rwt", [128, NCB * E], BF16)
            lgt = sbuf(S, "p3_lg", [128, E], F32)
            mx = sbuf(S, "p3_mx", [128, TK], F32)
            mi_ = sbuf(S, "p3_mi", [128, TK], U32)
            nmx = sbuf(S, "p3_nmx", [128, 1], F32)
            exw = sbuf(S, "p3_ex", [128, TK], F32)
            s8 = sbuf(S, "p3_s8", [128, 1], F32)
            wv = sbuf(S, "p3_wv", [128, 1], F32)
            eid = sbuf(S, "p3_eid", [128, NTILE * TK], F32)
            mkt = sbuf(S, "p3_mk", [128, E], F32)
            tot = sbuf(S, "p3_tot", [128, NTILE * E], F32)
            scr64 = sbuf(S, "p3_scr64", [128, E], F32)
            cnt01 = [sbuf(S, "p3_cnt0", [1, E], F32), sbuf(S, "p3_cnt1", [1, E], F32)]
            SA = S.enter_context(ExitStack())
            ps_lg = psum(SA, "ps_lg", [128, E])
            ps_cnt2 = psum(SA, "ps_cnt2", [128, 2 * E])
            SC = S.enter_context(ExitStack())
            woutb = sbuf(SC, "p3_wout", [128, NCB * D], BF16)
            hsc_s = sbuf(SC, "p3_hs", [128, NCB * TCH], BF16)
            hsc = [hsc_s, hsc_s]
            x0c_s = sbuf(SC, "p3_x0", [128, NCB * TCH], F32)
            x0c = [x0c_s, x0c_s]
            x1c_s = sbuf(SC, "p3_x1", [128, NCB * TCH], F32)
            x1c = [x1c_s, x1c_s]
            arc_s = sbuf(SC, "p3_ar", [128, NCB * TCH], F32)
            arc = [arc_s, arc_s]
            r3c_s = sbuf(SC, "p3_r3", [128, NCB * TCH], F32)
            r3c = [r3c_s, r3c_s]
            P = ln_tiles(SC, "p3")
            mmps = psum(SC, "p3_mm", [128, 4 * TCH])

            trt_free = [[], []]
            x3row_toks = []
            t_wout = step("gpsimd", lambda g: g.dma_start(
                out=woutb[:].rearrange("p (k m) -> p k m", k=NCB),
                in_=w_out.rearrange("(k p) m -> p k m", p=128)), deps=now(), dma=1)
            t_rwt = step("gpsimd", lambda g: g.dma_start(
                out=rwtb[:].rearrange("p (k m) -> p k m", k=NCB),
                in_=rwt.rearrange("(k p) m -> p k m", p=128)), deps=[], dma=1)

            pa = {"vprev": None, "lgt_free": None}

            def emit_passA_tile(t32):
                def _lmm(te, t32=t32):
                    for k in range(NCB):
                        te.ldweights(x3bf[:, k * TPC + t32 * 128: k * TPC + (t32 + 1) * 128])
                        last = te.matmul(ps_lg[:],
                                         x3bf[:, k * TPC + t32 * 128: k * TPC + (t32 + 1) * 128],
                                         rwtb[:, k * E:(k + 1) * E],
                                         start=(k == 0), stop=(k == NCB - 1))
                    return last
                d = [x3c_tok[t32 // 4], t_rwt]
                if pa["lgt_free"] is not None:
                    d.append(pa["lgt_free"])
                t_lmm = step("tensor", _lmm, deps=d)
                d = [t_lmm]
                if pa["vprev"] is not None:
                    d.append(pa["vprev"])
                t_lgt = step("vector", lambda ve: ve.tensor_tensor(
                    out=lgt[:], in0=ps_lg[:], in1=c_detie[:], op=ALU.add), deps=d)
                pa["lgt_free"] = t_lgt

                def _mwi(ve):
                    ve.max_with_indices(out_max=mx[:], out_indices=mi_[:], in_=lgt[:])
                    return ve.nop()
                t_mwi = step("vector", _mwi, deps=[t_lgt])
                t_nmx = step("vector", lambda ve: ve.tensor_scalar(
                    out=nmx[:], in0=mx[:, 0:1], scalar1=-1.0, scalar2=None, op0=ALU.mult),
                    deps=[t_mwi])
                t_exp = step("scalar", lambda se: se.activation(
                    out=exw[:], in_=mx[:], func=AF.Exp, bias=nmx[:]), deps=[t_nmx])
                t_rsm = step("vector", lambda ve: ve.reduce_sum(
                    out=s8[:], in_=exw[:], axis=mybir.AxisListType.X), deps=[t_exp])
                t_rc = step("vector", lambda ve: ve.reciprocal(out=wv[:], in_=s8[:]),
                            deps=[t_rsm])
                t_wtsA = step("vector", lambda ve, t32=t32: ve.tensor_scalar(
                    out=p_wts[:, t32 * TK:(t32 + 1) * TK], in0=exw[:],
                    scalar1=wv[:], scalar2=None, op0=ALU.mult), deps=[t_rc])
                tprev = t_wtsA
                for k in range(TK):
                    t_mk = step("vector", lambda ve, t32=t32, k=k: ve.tensor_scalar(
                        out=mkt[:], in0=lgt[:], scalar1=mx[:, k:k + 1],
                        scalar2=None, op0=ALU.is_equal), deps=[tprev])
                    t_sc = step("vector", lambda ve: ve.tensor_tensor(
                        out=scr64[:], in0=mkt[:], in1=c_iota64[:], op=ALU.mult),
                        deps=[t_mk])
                    tprev = step("vector", lambda ve, t32=t32, k=k: ve.reduce_sum(
                        out=eid[:, t32 * TK + k: t32 * TK + k + 1],
                        in_=scr64[:], axis=mybir.AxisListType.X), deps=[t_sc])
                t_tot = step("vector", lambda ve, t32=t32: ve.tensor_scalar(
                    out=tot[:, t32 * E:(t32 + 1) * E], in0=lgt[:],
                    scalar1=mx[:, TK - 1:TK], scalar2=None, op0=ALU.is_ge), deps=[tprev])
                pa["vprev"] = t_tot

            def emit_cnt_seg(seg):
                def _cmm(te, seg=seg):
                    for t32 in range(seg * 16, seg * 16 + 16):
                        last = te.matmul(ps_cnt2[:, seg * E:(seg + 1) * E], c_onesf[:],
                                         tot[:, t32 * E:(t32 + 1) * E],
                                         start=(t32 % 16 == 0), stop=(t32 % 16 == 15))
                    return last
                return step("tensor", _cmm, deps=[pa["vprev"]])

            hs_v = hs_a2a_out.ap().rearrange("(k p) n -> p k n", p=128)
            hs_tok = {}
            x1_tok = {}
            x0_tok = {}
            wo_ev = {}
            wo_last = {}
            a1_tok = {}
            ln3_done = {}
            ln4_done = {}
            x2st_tok = {}
            x3c_tok = {}
            tok_cc_hs = [t for t in now() if t[0] == "CC"]
            for i in range(NTCH):
                par = i % 2
                d = list(tok_cc_hs) if i < 1 else [wo_last[i - 1]]
                hs_tok[i] = step("sync", lambda sy, i=i, par=par: sy.dma_start(
                    out=hsc[par][:].rearrange("p (k n) -> p k n", k=NCB),
                    in_=hs_v[:, :, i * TCH:(i + 1) * TCH]), deps=d, dma=1)
                d = [] if i < 1 else list(a1_tok[i - 1])
                x1_tok[i] = step("sync", lambda sy, i=i, par=par: sy.dma_start(
                    out=x1c[par][:].rearrange("p (cb n) -> p cb n", cb=NCB),
                    in_=x1_v[:, :, i * TCH:(i + 1) * TCH]), deps=d, dma=1)
                d = [] if i < 1 else [x2st_tok[i - 1]] + ln4_done[i - 1]
                x0_tok[i] = step("sync", lambda sy, i=i, par=par: sy.dma_start(
                    out=x0c[par][:].rearrange("p (cb n) -> p cb n", cb=NCB),
                    in_=xcm_v[:, :, i * TCH:(i + 1) * TCH]), deps=d, dma=1)
                for grp in range(2):
                    def _wo(te, grp=grp, par=par):
                        for mi in range(4):
                            m = grp * 4 + mi
                            for k in range(NCB):
                                wt = woutb[:, k * D + m * 128: k * D + (m + 1) * 128]
                                te.ldweights(wt)
                                last = te.matmul(
                                    mmps[:, mi * TCH:(mi + 1) * TCH], wt,
                                    hsc[par][:, k * TCH:(k + 1) * TCH],
                                    start=(k == 0), stop=(k == NCB - 1))
                        return last
                    d = [hs_tok[i], t_wout]
                    if grp == 1:
                        d.append(wo_ev[(i, 0)])
                    elif i >= 1:
                        d.append(wo_ev[(i - 1, 1)])
                    t_wo = step("tensor", _wo, deps=d)
                    wo_last[i] = t_wo

                    def _ev(se, grp=grp, par=par):
                        for mi in range(4):
                            m = grp * 4 + mi
                            last = se.activation(out=arc[par][:, m * TCH:(m + 1) * TCH],
                                                 in_=mmps[:, mi * TCH:(mi + 1) * TCH],
                                                 func=AF.Identity, bias=c_bo8[:, m:m + 1])
                        return last
                    d = [t_wo]
                    if i >= 1:
                        d += ln3_done[i - 1]
                    wo_ev[(i, grp)] = step("scalar", _ev, deps=d)

                def _addx1(ve, par=par):
                    for cb in range(NCB):
                        last = ve.tensor_tensor(out=arc[par][:, cb * TCH:(cb + 1) * TCH],
                                                in0=arc[par][:, cb * TCH:(cb + 1) * TCH],
                                                in1=x1c[par][:, cb * TCH:(cb + 1) * TCH],
                                                op=ALU.add)
                    return last
                a1 = step("vector", _addx1, deps=[wo_ev[(i, 0)], wo_ev[(i, 1)], x1_tok[i]])
                a1_tok[i] = [a1]
                ow = [] if i < 1 else [x3c_tok[i - 1]]
                ln3_done[i] = emit_ln_tok(P, arc[par][:], r3c[par][:],
                                          [a1], out_war=ow)

                def _addx0(ve, par=par):
                    for cb in range(NCB):
                        last = ve.tensor_tensor(out=x0c[par][:, cb * TCH:(cb + 1) * TCH],
                                                in0=x0c[par][:, cb * TCH:(cb + 1) * TCH],
                                                in1=r3c[par][:, cb * TCH:(cb + 1) * TCH],
                                                op=ALU.add)
                    return last
                a0 = step("vector", _addx0, deps=ln3_done[i] + [x0_tok[i]])
                x2st_tok[i] = step("sync", lambda sy, i=i, par=par: sy.dma_start(
                    out=x2_v[:, :, i * TCH:(i + 1) * TCH],
                    in_=x0c[par][:].rearrange("p (cb n) -> p cb n", cb=NCB)),
                    deps=[a0], dma=1)
                ln4_done[i] = emit_ln_tok(P, x0c[par][:], r3c[par][:],
                                          [a0], out_war=ln3_done[i])

                def _x3cast(se, i=i, par=par):
                    for cb in range(NCB):
                        last = se.activation(
                            out=x3bf[:, cb * TPC + i * TCH: cb * TPC + (i + 1) * TCH],
                            in_=r3c[par][:, cb * TCH:(cb + 1) * TCH], func=AF.Copy)
                    return last
                x3c_tok[i] = step("scalar", _x3cast, deps=ln4_done[i])

                def _x3st(sy, i=i):
                    return sy.dma_start(
                        out=x3_v[:, :, i * TCH:(i + 1) * TCH],
                        in_=x3bf[:].rearrange("p (cb n) -> p cb n", cb=NCB)[:, :, i * TCH:(i + 1) * TCH])
                t_x3st = step("sync", _x3st, deps=[x3c_tok[i]], dma=1)
                for blk in range(i * 4, (i + 1) * 4):
                    tr_t = step("sync", lambda sy, blk=blk: sy.dma_start(
                        out=trt[0][:], in_=x3cm_dram.ap()[:, blk * 128:(blk + 1) * 128],
                        transpose=True), deps=[t_x3st] + trt_free[0], dma=1)
                    st_t = step("sync", lambda sy, blk=blk: sy.dma_start(
                        out=x3row_dram.ap()[blk * 128:(blk + 1) * 128, :], in_=trt[0][:]),
                        deps=[tr_t], dma=1)
                    trt_free[0] = [st_t]
                    x3row_toks.append(st_t)

                if i >= 2:
                    for t32 in range(4 * (i - 2), 4 * (i - 1)):
                        emit_passA_tile(t32)
                    if 4 * (i - 1) == 16:
                        emit_cnt_seg(0)

            for t32 in range(4 * (NTCH - 2), NTILE):
                emit_passA_tile(t32)
            emit_cnt_seg(1)

            SC.close()

            def _cntev(se):
                se.activation(out=cnt01[0][:], in_=ps_cnt2[0:1, 0:E], func=AF.Copy)
                return se.activation(out=cnt01[1][:], in_=ps_cnt2[0:1, E:2 * E], func=AF.Copy)
            sstep("scalar", _cntev, exclude=("CC",))
            passA_done = now()
            SA.close()

            def _cntst(sy):
                return [sy.dma_start(out=cnt_in.ap()[0:1, :], in_=cnt01[0][:]),
                        sy.dma_start(out=cnt_in.ap()[1:2, :], in_=cnt01[1][:])]
            sstep("sync", _cntst, dma=2)
            tok_cnt_ag = sstep("gpsimd", lambda g: g.collective_compute(
                "AllGather", ALU.bypass, replica_groups=RG,
                ins=[cnt_in.ap().opt()], outs=[cnt_out.ap().opt()]), cc=True)
            tok_x3rep = step("gpsimd", lambda g: g.collective_compute(
                "AllGather", ALU.bypass, replica_groups=RG,
                ins=[x3row_dram.ap().opt()], outs=[x3rep.ap().opt()]),
                deps=x3row_toks + [tok_cnt_ag], cc=True)

            pb_entry = now()

            if nphases >= 4:
                wgub = sbuf(S, "p4_wgu", [128, NCB * 2 * SF], BF16)
                wsdb = sbuf(S, "p4_wsd", [128, 6 * D], BF16)
                hb = sbuf(S, "p4_h", [128, 6 * TPC], BF16)
                shc = sbuf(S, "p4_sh", [128, 2 * TCH], F32)
                hgs4 = sbuf(S, "p4_hgs", [128, TCH], BF16)
                ps_g = psum(S, "p4_psg", [128, TCH])
                ps_u = psum(S, "p4_psu", [128, TCH])
                ps_d = psum(S, "p4_psd", [128, 2 * TCH])

                t_wgu = step("gpsimd", lambda g: g.dma_start(
                    out=wgub[:].rearrange("p (k m) -> p k m", k=NCB),
                    in_=wsgu.rearrange("(k p) m -> p k m", p=128)), deps=[], dma=1)
                t_wsd = step("gpsimd", lambda g: g.dma_start(
                    out=wsdb[:].rearrange("p (k m) -> p k m", k=6),
                    in_=wsd.rearrange("(k p) m -> p k m", p=128)), deps=[], dma=1)

                t_se = None
                t_ve = None
                t_te = None
                for m in range(6):
                    for c in range(NTCH):
                        def _gu(te, m=m, c=c):
                            for k in range(NCB):
                                wt = wgub[:, k * 2 * SF + m * 128: k * 2 * SF + (m + 1) * 128]
                                te.ldweights(wt)
                                te.matmul(ps_g[:], wt,
                                          x3bf[:, k * TPC + c * TCH: k * TPC + (c + 1) * TCH],
                                          start=(k == 0), stop=(k == NCB - 1))
                            for k in range(NCB):
                                wt = wgub[:, k * 2 * SF + (m + 6) * 128: k * 2 * SF + (m + 7) * 128]
                                te.ldweights(wt)
                                last = te.matmul(
                                    ps_u[:], wt,
                                    x3bf[:, k * TPC + c * TCH: k * TPC + (c + 1) * TCH],
                                    start=(k == 0), stop=(k == NCB - 1))
                            return last
                        d = [t_wgu] + passA_done
                        if t_se is not None:
                            d += [t_se, t_ve]
                        t_te = step("tensor", _gu, deps=d)
                        t_se = step("scalar", lambda se: se.activation(
                            out=hgs4[:], in_=ps_g[:], func=AF.Silu), deps=[t_te])
                        t_ve = step("vector", lambda ve, m=m, c=c: ve.tensor_tensor(
                            out=hb[:, m * TPC + c * TCH: m * TPC + (c + 1) * TCH],
                            in0=hgs4[:], in1=ps_u[:], op=ALU.mult), deps=[t_te, t_se])

                sh_v = shared_dram.ap().rearrange("(cb p) n -> p cb n", p=128)
                t_hdone = t_ve
                t_dev = None
                t_dst = None
                p4_sh_toks = []
                for m in range(NCB):
                    for quad in range(4):
                        def _dn(te, m=m, quad=quad):
                            for k in range(6):
                                wt = wsdb[:, k * D + m * 128: k * D + (m + 1) * 128]
                                te.ldweights(wt)
                                for c in range(2):
                                    cg = quad * 2 + c
                                    last = te.matmul(
                                        ps_d[:, c * TCH:(c + 1) * TCH], wt,
                                        hb[:, k * TPC + cg * TCH: k * TPC + (cg + 1) * TCH],
                                        start=(k == 0), stop=(k == 5))
                            return last
                        d = [t_wsd, t_hdone]
                        if t_dev is not None:
                            d.append(t_dev)
                        t_te = step("tensor", _dn, deps=d)

                        def _dev(se):
                            for c in range(2):
                                last = se.activation(out=shc[:, c * TCH:(c + 1) * TCH],
                                                     in_=ps_d[:, c * TCH:(c + 1) * TCH],
                                                     func=AF.Copy)
                            return last
                        d = [t_te]
                        if t_dst is not None:
                            d.append(t_dst)
                        t_dev = step("scalar", _dev, deps=d)
                        t_dst = step("sync", lambda sy, m=m, quad=quad: sy.dma_start(
                            out=sh_v[:, m, quad * 1024:(quad + 1) * 1024], in_=shc[:]),
                            deps=[t_dev], dma=1)
                        p4_sh_toks.append(t_dst)
                p4_done = now()

            _sstep_real = sstep

            def sstep(queue, fn, dma=0, cc=False, exclude=(), extra=()):
                return _sstep_real(queue, fn, dma=dma, cc=cc,
                                   exclude=tuple(set(exclude) | {"CC"}),
                                   extra=list(extra) + [tok_cnt_ag])
            cnta = sbuf(S, "p3_cnta", [16, E], F32)
            pre01 = [sbuf(S, "p3_pre0", [1, E], F32), sbuf(S, "p3_pre1", [1, E], F32)]
            pfull = sbuf(S, "p3_pfull", [128, E], F32)
            sr = sbuf(S, "p3_sr", [128, E], F32)
            psk = sbuf(S, "p3_psk", [128, E], F32)
            pos = sbuf(S, "p3_pos", [128, TK], F32)
            slotf = sbuf(S, "p3_slotf", [128, TK], F32)
            vmk = sbuf(S, "p3_vm", [128, TK], F32)
            idxf = sbuf(S, "p3_idxf", [128, TK], F32)
            rowid = sbuf(S, "p3_rowid", [128, 1], F32)
            payA = sbuf(S, "p3_payA", [128, NTILE * 2 * TK], F32)
            e1s = sbuf(S, "p3_e1s", [128, E], F32)
            ps_tp = psum(S, "ps_tp", [128, E])
            ps_run = [psum(S, "ps_run0", [128, E]), psum(S, "ps_run1", [128, E])]
            ps_pre = psum(S, "ps_pre", [1, E])

            t_cnta = step("sync", lambda sy: sy.dma_start(out=cnta[:], in_=cnt_out.ap()),
                          deps=list(pb_entry) + [tok_cnt_ag], dma=1)
            pre_ev = {}
            t_pev = None
            for b_i in range(B):
                def _pmm(te, b_i=b_i):
                    return te.matmul(ps_pre[:], c_presel[:, b_i:b_i + 1], cnta[:],
                                     start=True, stop=True)
                d = [t_cnta] + ([t_pev] if t_pev else [])
                t_pmm = step("tensor", _pmm, deps=d)
                t_pev = step("scalar", lambda se, b_i=b_i: se.activation(
                    out=pre01[b_i][:], in_=ps_pre[:], func=AF.Copy), deps=[t_pmm])
                pre_ev[b_i] = t_pev

            vtok = [step("vector", lambda ve: ve.nop(),
                         deps=list(pb_entry) + [tok_cnt_ag])]
            e1_tok = {}
            pf_tok = {}
            acc_tok = {}
            base_done = {}
            sc_glist = []
            for t32 in range(NTILE):
                seg = t32 // 16
                if t32 % 16 == 0:
                    def _base(te, seg=seg):
                        return te.matmul(ps_run[seg][:], c_onesf[0:1, :], pre01[seg][:],
                                         start=True, stop=False)
                    d = [pre_ev[seg]]
                    base_done[seg] = step("tensor", _base, deps=d)

                def _tp(te, t32=t32):
                    return te.matmul(ps_tp[:], c_lt[:], tot[:, t32 * E:(t32 + 1) * E],
                                     start=True, stop=True)
                d = [base_done[seg]]
                if t32 >= 1:
                    d.append(e1_tok[t32 - 1])
                t_tp = step("tensor", _tp, deps=d)
                d = [t_tp]
                if t32 >= 1:
                    d.append(pf_tok[t32 - 1])
                e1_tok[t32] = step("scalar", lambda se: se.activation(
                    out=e1s[:], in_=ps_tp[:], func=AF.Copy), deps=d)

                def _pf(ve, seg=seg):
                    ve.tensor_tensor(out=pfull[:], in0=e1s[:], in1=ps_run[seg][:],
                                     op=ALU.add)
                    return ve.memset(sr[:], 0.0)
                d = [e1_tok[t32], vtok[-1]]
                if t32 % 16 != 0:
                    d.append(acc_tok[t32 - 1])
                pf_tok[t32] = step("vector", _pf, deps=d)
                vtok.append(pf_tok[t32])

                def _acc(te, t32=t32, seg=seg):
                    return te.matmul(ps_run[seg][:], c_onesf[:], tot[:, t32 * E:(t32 + 1) * E],
                                     start=False, stop=(t32 % 16 == 15))
                acc_tok[t32] = step("tensor", _acc, deps=[pf_tok[t32]])

                for k in range(TK):
                    def _s1(ve, t32=t32, k=k):
                        if k > 0:
                            ve.tensor_tensor(out=sr[:], in0=sr[:], in1=mkt[:], op=ALU.add)
                        return ve.tensor_scalar(out=mkt[:], in0=c_iota64[:],
                                                scalar1=eid[:, t32 * TK + k: t32 * TK + k + 1],
                                                scalar2=None, op0=ALU.is_equal)
                    t1 = step("vector", _s1, deps=[vtok[-1]])

                    def _s1b(ve):
                        ve.tensor_tensor(out=psk[:], in0=pfull[:], in1=sr[:], op=ALU.add)
                        return ve.tensor_tensor(out=scr64[:], in0=mkt[:], in1=psk[:],
                                                op=ALU.mult)
                    t1b = step("vector", _s1b, deps=[t1])
                    t2 = step("vector", lambda ve, k=k: ve.reduce_sum(
                        out=pos[:, k:k + 1], in_=scr64[:], axis=mybir.AxisListType.X),
                        deps=[t1b])
                    vtok.append(t2)

                t3a = step("vector", lambda ve, t32=t32: ve.scalar_tensor_tensor(
                    out=slotf[:], in0=eid[:, t32 * TK:(t32 + 1) * TK],
                    scalar=1.0, in1=pos[:], op0=ALU.mult, op1=ALU.add),
                    deps=[vtok[-1]])
                t3b = step("vector", lambda ve: ve.tensor_scalar(
                    out=vmk[:], in0=pos[:], scalar1=float(CAP), scalar2=None,
                    op0=ALU.is_ge), deps=[t3a])
                t3 = step("vector", lambda ve: ve.scalar_tensor_tensor(
                    out=idxf[:], in0=vmk[:], scalar=BIGIDX, in1=slotf[:],
                    op0=ALU.mult, op1=ALU.add), deps=[t3b])
                t4 = step("vector", lambda ve, t32=t32: ve.tensor_copy(
                    p_idxi[:, t32 * TK:(t32 + 1) * TK], idxf[:]), deps=[t3])

                t5a = step("vector", lambda ve, t32=t32: ve.tensor_scalar(
                    out=rowid[:], in0=c_iota128[:], scalar1=c_rowbase[:],
                    scalar2=float(t32 * 128 + 1), op0=ALU.add, op1=ALU.add), deps=[t4])
                pview = payA[:].rearrange("p (t k c) -> p t k c", t=NTILE, k=TK)

                def _s3(ve, t32=t32):
                    ve.tensor_scalar(
                        out=pview[:, t32, :, 0], in0=p_wts[:, t32 * TK:(t32 + 1) * TK],
                        scalar1=0.0, scalar2=rowid[:], op0=ALU.mult, op1=ALU.add)
                    return ve.tensor_copy(pview[:, t32, :, 1],
                                          p_wts[:, t32 * TK:(t32 + 1) * TK])
                t5 = step("vector", _s3, deps=[t5a])
                vtok.append(t5)

                def _sc8(g, t32=t32):
                    return [g.indirect_dma_start(
                        out=meta_in.ap(),
                        out_offset=IndirectOffsetOnAxis(
                            ap=p_idxi[:, t32 * TK + k: t32 * TK + k + 1], axis=0),
                        in_=payA[:, (t32 * TK + k) * 2:(t32 * TK + k) * 2 + 2],
                        in_offset=None,
                        bounds_check=breg[0], oob_is_err=False) for k in range(TK)]
                d = [t5]
                if len(sc_glist) >= 4:
                    d.append(sc_glist[-4])
                sc_glist.append(step("gpsimd", _sc8, deps=d, dma=TK))

            sstep = _sstep_real
            tok_meta = step("gpsimd", lambda g: g.collective_compute(
                "AllReduce", ALU.add, replica_groups=RG,
                ins=[meta_in.ap().opt()], outs=[meta_out.ap().opt()]),
                deps=sc_glist[-4:], cc=True)


        with ExitStack() as S:
          if nphases >= 5:
            meta_g = [sbuf(S, "p5_meta%d" % i_, [128, 2], F32) for i_ in range(32)]
            gidx = [sbuf(S, "p5_gidx%d" % p_, [128, 4], I32) for p_ in range(2)]
            scr1 = sbuf(S, "p5_scr1", [128, 1], F32)
            scr2 = sbuf(S, "p5_scr2", [128, 1], F32)
            xbs = [sbuf(S, "p5_xbs%d" % i_, [128, D], BF16) for i_ in range(8)]
            xbt = [sbuf(S, "p5_xbt%d" % p_, [128, NCB * CAP], BF16) for p_ in range(2)]
            wgb = [sbuf(S, "p5_wg%d" % p_, [128, NCB * FF], BF16) for p_ in range(2)]
            wub = [sbuf(S, "p5_wu%d" % p_, [128, NCB * FF], BF16) for p_ in range(2)]
            wdb = [sbuf(S, "p5_wd%d" % p_, [128, 6 * D], BF16) for p_ in range(2)]
            hgs = [sbuf(S, "p5_hgs%d" % p_, [128, CAP], BF16) for p_ in range(2)]
            hbe = sbuf(S, "p5_h", [128, 6 * CAP], BF16)
            ycmb = [sbuf(S, "p5_ycm%d" % p_, [128, NCB * CAP], BF16) for p_ in range(2)]
            yt = [sbuf(S, "p5_yt%d" % p_, [128, D], BF16) for p_ in range(2)]
            ys = [sbuf(S, "p5_ys%d" % p_, [128, D], BF16) for p_ in range(2)]
            ps_a = [psum(S, "p5_psa%d" % p_, [128, CAP]) for p_ in range(2)]
            ps_b = [psum(S, "p5_psb%d" % p_, [128, CAP]) for p_ in range(2)]
            ps_c = [psum(S, "p5_psc%d" % p_, [128, CAP]) for p_ in range(2)]

            p5_entry = now()
            xbst_tok = {}
            meta_tok = {}
            vprev = None
            gx_prev = {}
            for e in range(EPC):
                tprev = None
                for g_i in range(4):
                    if e == 0 and g_i == 0:
                        d = list(p5_entry)
                    else:
                        d = [tprev or vprev]
                    if g_i == 0 and e >= 2:
                        d.append(gx_prev[e - 2])
                    t1 = step("vector", lambda ve, e=e, g_i=g_i: ve.tensor_scalar(
                        out=scr1[:], in0=c_iota128[:], scalar1=c_slotb[:],
                        scalar2=float(e * NCORES * CAP + g_i * 128),
                        op0=ALU.add, op1=ALU.add), deps=d)
                    tprev = step("vector", lambda ve, e=e, g_i=g_i: ve.tensor_copy(
                        gidx[e % 2][:, g_i:g_i + 1], scr1[:]), deps=[t1])
                d = [tprev, tok_meta]
                if e >= 1:
                    d.append(gx_prev[e - 1])

                def _gmeta(g, e=e):
                    return [g.indirect_dma_start(
                        out=meta_g[e * 4 + g_i][:], out_offset=None,
                        in_=meta_out.ap(),
                        in_offset=IndirectOffsetOnAxis(ap=gidx[e % 2][:, g_i:g_i + 1], axis=0),
                        bounds_check=breg[0], oob_is_err=False) for g_i in range(4)]
                t_gmeta = step("gpsimd", _gmeta, deps=d, dma=4)

                st_toks5 = []
                tprev = t_gmeta
                for g_i in range(4):
                    t1 = step("vector", lambda ve, e=e, g_i=g_i: ve.tensor_scalar(
                        out=scr1[:], in0=meta_g[e * 4 + g_i][:, 0:1], scalar1=0.0,
                        scalar2=None, op0=ALU.is_equal), deps=[tprev])
                    t2 = step("vector", lambda ve, e=e, g_i=g_i: ve.scalar_tensor_tensor(
                        out=scr2[:], in0=scr1[:], scalar=BIGIDX,
                        in1=meta_g[e * 4 + g_i][:, 0:1], op0=ALU.mult, op1=ALU.add),
                        deps=[t1])
                    t3 = step("vector", lambda ve: ve.tensor_scalar(
                        out=scr2[:], in0=scr2[:], scalar1=1.0, scalar2=None,
                        op0=ALU.subtract), deps=[t2])
                    tprev = step("vector", lambda ve, e=e, g_i=g_i: ve.tensor_copy(
                        gidx[e % 2][:, g_i:g_i + 1], scr2[:]), deps=[t3])
                    bi = (e * 4 + g_i) % 8
                    d = [tprev, tok_x3rep]
                    if (e * 4 + g_i) >= 4:
                        d.append(xbst_tok[e * 4 + g_i - 4])
                    t_gx = step("gpsimd", lambda g, e=e, g_i=g_i, bi=bi: g.indirect_dma_start(
                        out=xbs[bi][:], out_offset=None,
                        in_=x3rep.ap(),
                        in_offset=IndirectOffsetOnAxis(ap=gidx[e % 2][:, g_i:g_i + 1], axis=0),
                        bounds_check=breg[0], oob_is_err=False), deps=d, dma=1)
                    gx_prev[e] = t_gx
                    t_st = step("sync", lambda sy, e=e, g_i=g_i, bi=bi: sy.dma_start(
                        out=xb_dram.ap()[e * CAP + g_i * 128: e * CAP + (g_i + 1) * 128, :],
                        in_=xbs[bi][:]), deps=[t_gx], dma=1)
                    xbst_tok[e * 4 + g_i] = t_st
                    st_toks5.append(t_st)
                meta_tok[e] = st_toks5
                vprev = tprev

            def _mk_wld(e, par):
                def _wld(g):
                    return [
                        g.dma_start(out=wgb[par][:].rearrange("p (k m) -> p k m", k=NCB),
                                    in_=wg_all[e * D:(e + 1) * D, :]
                                    .rearrange("(k p) m -> p k m", p=128)),
                        g.dma_start(out=wub[par][:].rearrange("p (k m) -> p k m", k=NCB),
                                    in_=wu_all[e * D:(e + 1) * D, :]
                                    .rearrange("(k p) m -> p k m", p=128)),
                        g.dma_start(out=wdb[par][:].rearrange("p (k m) -> p k m", k=6),
                                    in_=wd_all[e * FF:(e + 1) * FF, :]
                                    .rearrange("(k p) m -> p k m", p=128))]
                return _wld
            tok_wld = {}
            tok_free_w = {}
            tok_xbt = {}
            tok_ys = {}
            tok_ybst = {}
            slot_toks = []
            sv_hist = {0: None, 1: None}
            ev_hist = {0: None, 1: None}
            tok_ytr_all = {}
            ag_toks = []
            for e in (0, 1):
                tok_wld[e] = step("gpsimd", _mk_wld(e, e % 2), deps=p5_entry, dma=3)
            for e in range(EPC):
                par = e % 2
                d = list(meta_tok[e])
                if e >= 2:
                    d.append(tok_free_w[e - 2])

                def _xbt(sy, e=e, par=par):
                    return [sy.dma_start(out=xbt[par][:, cb * CAP:(cb + 1) * CAP],
                                         in_=xb_dram.ap()[e * CAP:(e + 1) * CAP,
                                                          cb * 128:(cb + 1) * 128],
                                         transpose=True) for cb in range(NCB)]
                tok_xbt[e] = step("sync", _xbt, deps=d, dma=NCB)

                for m in range(6):
                    mp = m % 2

                    def _ge(te, m=m, par=par, mp=mp):
                        for k in range(NCB):
                            wt = wgb[par][:, k * FF + m * 128: k * FF + (m + 1) * 128]
                            te.ldweights(wt)
                            te.matmul(ps_a[mp][:], wt, xbt[par][:, k * CAP:(k + 1) * CAP],
                                      start=(k == 0), stop=(k == NCB - 1))
                        for k in range(NCB):
                            wt = wub[par][:, k * FF + m * 128: k * FF + (m + 1) * 128]
                            te.ldweights(wt)
                            last = te.matmul(ps_b[mp][:], wt, xbt[par][:, k * CAP:(k + 1) * CAP],
                                             start=(k == 0), stop=(k == NCB - 1))
                        return last
                    d = [tok_xbt[e], tok_wld[e]]
                    if sv_hist[mp] is not None:
                        d += sv_hist[mp]
                    t_te = step("tensor", _ge, deps=d)
                    t_se = step("scalar", lambda se, mp=mp: se.activation(
                        out=hgs[mp][:], in_=ps_a[mp][:], func=AF.Silu), deps=[t_te])
                    t_ve = step("vector", lambda ve, m=m, mp=mp: ve.tensor_tensor(
                        out=hbe[:, m * CAP:(m + 1) * CAP], in0=hgs[mp][:], in1=ps_b[mp][:],
                        op=ALU.mult), deps=[t_te, t_se])
                    sv_hist[mp] = [t_se, t_ve]

                t_hdone = t_ve
                for m in range(NCB):
                    mp = m % 2

                    def _de(te, m=m, par=par, mp=mp):
                        for k in range(6):
                            wt = wdb[par][:, k * D + m * 128: k * D + (m + 1) * 128]
                            te.ldweights(wt)
                            last = te.matmul(ps_c[mp][:], wt, hbe[:, k * CAP:(k + 1) * CAP],
                                             start=(k == 0), stop=(k == 5))
                        return last
                    d = [t_hdone]
                    if ev_hist[mp] is not None:
                        d.append(ev_hist[mp])
                    t_te = step("tensor", _de, deps=d)
                    dev = [t_te]
                    if m < 2 and e >= 2:
                        dev.append(tok_ybst[e - 2])
                    t_ev = step("scalar", lambda se, m=m, par=par, mp=mp: se.activation(
                        out=ycmb[par][:, m * CAP:(m + 1) * CAP], in_=ps_c[mp][:], func=AF.Copy),
                        deps=dev)
                    ev_hist[mp] = t_ev
                tok_free_w[e] = t_te
                if e + 2 < EPC:
                    tok_wld[e + 2] = step("gpsimd", _mk_wld(e + 2, par), deps=[t_te], dma=3)

                d = [ev_hist[0], ev_hist[1]]
                if e >= 2 and e - 2 in tok_ybst:
                    d.append(tok_ytr_all[e - 2])
                t_ybst = step("sync", lambda sy, par=par: sy.dma_start(
                    out=yb_dram.ap().rearrange("(q m p) c -> q p m c", q=2, p=128)[par],
                    in_=ycmb[par][:].rearrange("p (m c) -> p m c", m=NCB)), deps=d, dma=1)
                tok_ybst[e] = t_ybst
                t_last = None
                for g_i in range(4):
                    yp = g_i % 2
                    d = [t_ybst]
                    if g_i >= 2:
                        d.append(tok_ysl[g_i - 2])
                    elif e >= 1:
                        d.append(tok_ys[e - 1])
                    t_tr = step("sync", lambda sy, g_i=g_i, par=par, yp=yp: sy.dma_start(
                        out=yt[yp][:], in_=yb_dram.ap()[par * D:(par + 1) * D,
                                                        g_i * 128:(g_i + 1) * 128],
                        transpose=True), deps=d, dma=1)
                    t_w = step("vector", lambda ve, e=e, g_i=g_i, yp=yp: ve.tensor_scalar(
                        out=ys[yp][:], in0=yt[yp][:], scalar1=meta_g[e * 4 + g_i][:, 1:2],
                        scalar2=None, op0=ALU.mult), deps=[t_tr])
                    t_sl = step("sync", lambda sy, e=e, g_i=g_i, yp=yp: sy.dma_start(
                        out=slot_out.ap()[e * CAP + g_i * 128: e * CAP + (g_i + 1) * 128, :],
                        in_=ys[yp][:]), deps=[t_w], dma=1)
                    if g_i == 0:
                        tok_ysl = {}
                    tok_ysl[g_i] = t_sl
                    t_last = t_sl
                    slot_toks.append(t_sl)
                tok_ytr_all[e] = t_last
                tok_ys[e] = t_last
                ag_toks.append(step("gpsimd", lambda g, e=e: g.collective_compute(
                    "AllGather", ALU.bypass, replica_groups=RG,
                    ins=[slot_out.ap()[e * CAP:(e + 1) * CAP, :].opt()],
                    outs=[slot_rep.ap()[e * NCORES * CAP:(e + 1) * NCORES * CAP, :].opt()]),
                    deps=slot_toks[-4:], cc=True))

            tok_slot_ag = ag_toks[-1]

        with ExitStack() as S:
          if nphases >= 6:
            gbuf = [sbuf(S, "p6_gb%d" % p_, [128, TK * D], BF16) for p_ in range(3)]
            p6r = [sbuf(S, "p6_r%d" % p_, [128, D], F32) for p_ in range(2)]
            sA_s = sbuf(S, "p6_sA", [128, D], F32)
            sB_s = sbuf(S, "p6_sB", [128, D], F32)
            sA = [sA_s, sA_s]
            sB = [sB_s, sB_s]
            p6rc = [sbuf(S, "p6_rc%d" % p_, [128, D], F32) for p_ in range(2)]
            ps_t = [psum(S, "p6_pst%d" % p_, [128, 4 * 128]) for p_ in range(2)]
            rview = routed_dram.ap().rearrange("(a p) c -> a p c", p=128)
            rcm_v = routedcm_dram.ap().rearrange("(cb p) n -> p cb n", p=128)
            fc = [sbuf(S, "p6_f%d" % p_, [128, NCB * TCH], F32) for p_ in range(2)]
            sc_s = sbuf(S, "p6_s", [128, NCB * TCH], F32)
            sc_ = [sc_s, sc_s]
            x2c_s = sbuf(S, "p6_x2", [128, NCB * TCH], F32)
            x2c = [x2c_s, x2c_s]
            r5c = sbuf(S, "p6_r5", [128, NCB * TCH], F32)
            P = ln_tiles(S, "p6")
            y_v = ycm.rearrange("(cb p) n -> p cb n", p=128)
            sh_v = shared_dram.ap().rearrange("(cb p) n -> p cb n", p=128)
            f_tok = {}
            ya_tok = {}
            yst_tok = {}
            ln5_tok = {}

            def emit_final_chunk(i):
                par = i % 2
                d = [st_toks[4 * i + jj] for jj in range(4)]
                if i >= 2:
                    d += ln5_tok[i - 2]
                tf_ = step("sync", lambda sy, i=i, par=par: sy.dma_start(
                    out=fc[par][:].rearrange("p (cb n) -> p cb n", cb=NCB),
                    in_=rcm_v[:, :, i * TCH:(i + 1) * TCH]), deps=d, dma=1)
                d = [p4_sh_toks[-1]]
                if i >= 1:
                    d.append(f_tok[i - 1])
                tss_ = step("sync", lambda sy, i=i, par=par: sy.dma_start(
                    out=sc_[par][:].rearrange("p (cb n) -> p cb n", cb=NCB),
                    in_=sh_v[:, :, i * TCH:(i + 1) * TCH]), deps=d, dma=1)
                d = [] if i < 1 else [yst_tok[i - 1]]
                tx2_ = step("sync", lambda sy, i=i, par=par: sy.dma_start(
                    out=x2c[par][:].rearrange("p (cb n) -> p cb n", cb=NCB),
                    in_=x2_v[:, :, i * TCH:(i + 1) * TCH]), deps=d, dma=1)

                def _fadd(ve, par=par):
                    for cb in range(NCB):
                        last = ve.tensor_tensor(out=fc[par][:, cb * TCH:(cb + 1) * TCH],
                                                in0=fc[par][:, cb * TCH:(cb + 1) * TCH],
                                                in1=sc_[par][:, cb * TCH:(cb + 1) * TCH],
                                                op=ALU.add)
                    return last
                t_fa = step("vector", _fadd, deps=[tf_, tss_])
                f_tok[i] = t_fa
                ow = [ya_tok[i - 1]] if i >= 1 else []
                ln5 = emit_ln_tok(P, fc[par][:], r5c[:], [t_fa], out_war=ow)
                ln5_tok[i] = ln5

                def _yadd(ve, par=par):
                    for cb in range(NCB):
                        last = ve.tensor_tensor(out=x2c[par][:, cb * TCH:(cb + 1) * TCH],
                                                in0=x2c[par][:, cb * TCH:(cb + 1) * TCH],
                                                in1=r5c[:, cb * TCH:(cb + 1) * TCH],
                                                op=ALU.add)
                    return last
                ya = step("vector", _yadd, deps=ln5 + [tx2_])
                ya_tok[i] = ya
                yst_tok[i] = step("sync", lambda sy, i=i, par=par: sy.dma_start(
                    out=y_v[:, :, i * TCH:(i + 1) * TCH],
                    in_=x2c[par][:].rearrange("p (cb n) -> p cb n", cb=NCB)),
                    deps=[ya], dma=1)

            p6_entry = now()
            tr_tok = {}
            ev_tok = {}
            st_toks = {}
            sum_tok = {}
            glist = []
            for i in range(NTILE):
                par = i % 2
                parg = i % 3
                if i < 3:
                    t_gz = step("vector", lambda ve, parg=parg: ve.memset(
                        gbuf[parg][:], 0.0), deps=p6_entry)
                else:
                    d = [sum_tok[i - 3][0], sum_tok[i - 3][1]]
                    t_gz = step("scalar", lambda se, parg=parg: se.activation(
                        out=gbuf[parg][:], in_=gbuf[parg][:], func=AF.Identity,
                        scale=c_zero[:], bias=c_zero[:]), deps=d)
                gt = []
                for k in range(TK):
                    d = [t_gz, tok_slot_ag]
                    if len(glist) >= 18:
                        d.append(glist[-18])
                    t_g = step("gpsimd", lambda g, i=i, k=k, parg=parg: g.indirect_dma_start(
                        out=gbuf[parg][:, k * D:(k + 1) * D],
                        out_offset=None,
                        in_=slot_rep.ap(),
                        in_offset=IndirectOffsetOnAxis(
                            ap=p_idxi[:, i * TK + k: i * TK + k + 1], axis=0),
                        bounds_check=breg[0], oob_is_err=False), deps=d, dma=1)
                    gt.append(t_g)
                    glist.append(t_g)

                def _sa(ve, parg=parg):
                    ve.tensor_tensor(out=sA[0][:], in0=gbuf[parg][:, 0:D],
                                     in1=gbuf[parg][:, D:2 * D], op=ALU.add)
                    ve.tensor_tensor(out=sA[0][:], in0=sA[0][:],
                                     in1=gbuf[parg][:, 2 * D:3 * D], op=ALU.add)
                    return ve.tensor_tensor(out=sA[0][:], in0=sA[0][:],
                                            in1=gbuf[parg][:, 3 * D:4 * D], op=ALU.add)
                d = gt[0:4] + ([sum_tok[i - 1][2]] if i >= 1 else [])
                t_sa = step("vector", _sa, deps=d)

                def _sb(ve, parg=parg):
                    ve.tensor_tensor(out=sB[0][:], in0=gbuf[parg][:, 4 * D:5 * D],
                                     in1=gbuf[parg][:, 5 * D:6 * D], op=ALU.add)
                    ve.tensor_tensor(out=sB[0][:], in0=sB[0][:],
                                     in1=gbuf[parg][:, 6 * D:7 * D], op=ALU.add)
                    return ve.tensor_tensor(out=sB[0][:], in0=sB[0][:],
                                            in1=gbuf[parg][:, 7 * D:8 * D], op=ALU.add)
                d = gt[4:8] + ([sum_tok[i - 1][2]] if i >= 1 else [])
                t_sb = step("vector", _sb, deps=d)
                d = [t_sa, t_sb]
                if i >= 2:
                    d.append(tr_tok[i - 2])
                t_sum = step("vector", lambda ve, par=par: ve.tensor_tensor(
                    out=p6r[par][:], in0=sA[par][:], in1=sB[par][:], op=ALU.add), deps=d)
                sum_tok[i] = (t_sa, t_sb, t_sum)
                if t_routed is not None:
                    step("sync", lambda sy, i=i, par=par: sy.dma_start(
                        out=rview[i], in_=p6r[par][:]), deps=[t_sum], dma=1)

                def _tr4(te, i=i, par=par, grp=0):
                    for cb in range(4 * grp, 4 * grp + 4):
                        last = te.transpose(
                            ps_t[par][:, (cb % 4) * 128:((cb % 4) + 1) * 128],
                            p6r[par][:, cb * 128:(cb + 1) * 128], c_idf[:])
                    return last

                def _tev4(se, i=i, par=par, grp=0):
                    for cb in range(4 * grp, 4 * grp + 4):
                        last = se.activation(
                            out=p6rc[par][:, cb * 128:(cb + 1) * 128],
                            in_=ps_t[par][:, (cb % 4) * 128:((cb % 4) + 1) * 128],
                            func=AF.Copy)
                    return last
                tprev_ev = ev_tok.get(i - 2)
                for grp in range(2):
                    d = [t_sum]
                    if tprev_ev is not None:
                        d.append(tprev_ev)
                    t_t = step("tensor", lambda te, i=i, par=par, grp=grp: _tr4(te, i, par, grp),
                               deps=d)
                    dev = [t_t]
                    if grp == 0 and i >= 2:
                        dev.append(st_toks[i - 2])
                    tprev_ev = step("scalar", lambda se, i=i, par=par, grp=grp: _tev4(se, i, par, grp),
                                    deps=dev)
                tr_tok[i] = t_t
                ev_tok[i] = tprev_ev
                st_toks[i] = step("sync", lambda sy, i=i, par=par: sy.dma_start(
                    out=rcm_v[:, :, i * 128:(i + 1) * 128],
                    in_=p6rc[par][:].rearrange("p (cb n) -> p cb n", cb=NCB)),
                    deps=[tprev_ev], dma=1)
                if i % 4 == 3:
                    emit_final_chunk(i // 4)

        def _taps(sy):
            out = []
            for t_ap, srcap in [(t_x1, x1_dram.ap()), (t_ug, ug_a2a_out.ap()),
                                (t_hs, hs_a2a_out.ap()), (t_x2, x2_dram.ap()),
                                (t_x3, x3cm_dram.ap()), (t_meta, meta_out.ap()),
                                (t_routed, routed_dram.ap()), (t_shared, shared_dram.ap())]:
                if t_ap is not None:
                    out.append(sy.dma_start(out=t_ap, in_=srcap))
            if t_wts is not None:
                out.append(sy.dma_start(out=t_wts, in_=p_wts[:]))
            if t_idx is not None:
                out.append(sy.dma_start(out=t_idx, in_=p_idxi[:]))
            return out
        ntap = sum(x is not None for x in
                   (t_x1, t_ug, t_hs, t_x2, t_x3, t_meta, t_routed, t_shared, t_wts, t_idx))
        if ntap:
            sstep("sync", _taps, dma=ntap)

        fin_deps = now()

        @block.sync
        def _(sy):
            for s, v in fin_deps:
                sy.wait_ge(SEMQ[s], v)

    return nc, tap_aps


def _mk_inputs(L0_attn_x, L0_attn_gamma, L0_attn_beta, L0_attn_A, L0_attn_W_in,
               L0_attn_b_in, L0_attn_W_out, L0_attn_b_out, L0_ffn_router_w,
               L0_ffn_w_gate, L0_ffn_w_up, L0_ffn_w_down, L0_ffn_ws_gate_up,
               L0_ffn_ws_down):
    f32 = np.float32
    bf16 = ml_dtypes.bfloat16
    x = np.asarray(L0_attn_x, f32)
    gamma = np.asarray(L0_attn_gamma, f32)
    beta = np.asarray(L0_attn_beta, f32)
    A = np.asarray(L0_attn_A, f32)
    W_in = np.ascontiguousarray(np.asarray(L0_attn_W_in, f32))
    b_in = np.asarray(L0_attn_b_in, f32)
    W_out = np.ascontiguousarray(np.asarray(L0_attn_W_out, f32))
    b_out = np.asarray(L0_attn_b_out, f32)
    router_w = np.asarray(L0_ffn_router_w, f32)
    w_gate = np.asarray(L0_ffn_w_gate, f32)
    w_up = np.asarray(L0_ffn_w_up, f32)
    w_down = np.asarray(L0_ffn_w_down, f32)
    ws_gate_up = np.ascontiguousarray(np.asarray(L0_ffn_ws_gate_up, f32))
    ws_down = np.ascontiguousarray(np.asarray(L0_ffn_ws_down, f32))

    ones_bf = np.ones((128, 128), bf16)
    ones_f = np.ones((128, 128), f32)
    lt = (np.arange(128)[:, None] < np.arange(128)[None, :]).astype(f32)
    ident = np.eye(128, dtype=f32)
    eids = np.arange(E)
    base64 = ((eids % EPC) * NCORES * CAP + (eids // EPC) * CAP).astype(f32)
    iota64_ = np.tile(base64, (128, 1))
    detie_ = (-1e-6 * np.arange(E, dtype=f32)).astype(f32)
    detie_ = np.tile(detie_, (128, 1))
    iota128_ = np.arange(128, dtype=f32).reshape(128, 1)

    in_maps = []
    for j in range(NCORES):
        xs = x[:, j * LPC:(j + 1) * LPC, :].reshape(B * LPC, D)
        xcm_j = np.ascontiguousarray(xs.T)
        presel_j = np.zeros((16, 2), f32)
        for bp in range(B):
            for jp in range(NCORES):
                for b_i in range(B):
                    if bp * NCORES + jp < b_i * NCORES + j:
                        presel_j[2 * jp + bp, b_i] = 1.0
        d = {
            "xcm": xcm_j,
            "g8": np.ascontiguousarray(gamma.reshape(NCB, 128).T),
            "b8": np.ascontiguousarray(beta.reshape(NCB, 128).T),
            "bo8": np.ascontiguousarray(b_out.reshape(NCB, 128).T),
            "acol": np.ascontiguousarray(A[j * 128:(j + 1) * 128].reshape(128, 1)),
            "w_in": W_in,
            "w_out": W_out,
            "rwt": np.ascontiguousarray(router_w.T),
            "wsgu": ws_gate_up,
            "wsd": ws_down,
            "wg_all": np.ascontiguousarray(w_gate[j * EPC:(j + 1) * EPC].reshape(EPC * D, FF)),
            "wu_all": np.ascontiguousarray(w_up[j * EPC:(j + 1) * EPC].reshape(EPC * D, FF)),
            "wd_all": np.ascontiguousarray(w_down[j * EPC:(j + 1) * EPC].reshape(EPC * FF, D)),
            "ones_bf": ones_bf,
            "ones_f": ones_f,
            "lt_strict": lt,
            "ident_f": ident,
            "iota64": iota64_,
            "detie": detie_,
            "iota128": iota128_,
            "rowbase": np.full((128, 1), j * TPC, f32),
            "slotb": np.full((128, 1), j * CAP, f32),
            "presel": presel_j,
            "ub8": np.ascontiguousarray(b_in[:D].reshape(NCB, 128).T),
            "epsc": np.full((128, 1), LN_EPS, f32),
            "zerc": np.zeros((128, 1), f32),
            "gb8": np.ascontiguousarray(b_in[D:].reshape(NCB, 128).T),
        }
        in_maps.append(d)
    return in_maps


def kernel(**inputs):
    in_maps = _mk_inputs(**inputs)
    nc, _ = build_nc()
    res = run_bass_kernel_spmd(nc, in_maps, core_ids=list(range(NCORES)))
    y = np.empty((B, L, D), np.float32)
    for j in range(NCORES):
        blk = res.results[j]["ycm"]
        y[:, j * LPC:(j + 1) * LPC, :] = blk.T.reshape(B, LPC, D)
    return y
